# version 37
# baseline (speedup 1.0000x reference)
import numpy as np
import ml_dtypes
from contextlib import ExitStack

import concourse.bass as bass
import concourse.mybir as mybir
from concourse.bass_utils import run_bass_kernel_spmd

F32 = mybir.dt.float32
BF16 = mybir.dt.bfloat16
I16 = mybir.dt.int16
I32 = mybir.dt.int32
AF = mybir.ActivationFunctionType
ALU = mybir.AluOpType

C = 2048
D = 256
H = 8192
RMS_EPS = 1.1920928955078125e-07
LN_EPS = 1e-5
QW = 2048


class Ctr:
    def __init__(self, es, nc, name):
        self.h = es.enter_context(nc.semaphore(name))
        self.n = 0


def build_program(R=4096):
    NB = R // 128
    NYB = NB - 2
    NCH = R // 512
    nc = bass.Bass("TRN2", target_bir_lowering=False)

    z = nc.dram_tensor("z", [R, C], F32, kind="ExternalInput").ap()
    bT = nc.dram_tensor("bT", [D, D], F32, kind="ExternalInput").ap()
    cT = nc.dram_tensor("cT", [D, D], F32, kind="ExternalInput").ap()
    aT = nc.dram_tensor("aT", [D, D], F32, kind="ExternalInput").ap()
    rl = nc.dram_tensor("rl", [1], F32, kind="ExternalInput").ap()
    wguT = nc.dram_tensor("wguT", [C, 2 * H], F32, kind="ExternalInput").ap()
    wpT = nc.dram_tensor("wpT", [H, C], F32, kind="ExternalInput").ap()
    idf = nc.dram_tensor("idf", [128, 128], F32, kind="ExternalInput").ap()
    idb = nc.dram_tensor("idb", [128, 128], BF16, kind="ExternalInput").ap()
    ind64 = nc.dram_tensor("ind64", [128, 2], BF16, kind="ExternalInput").ap()
    ind2 = nc.dram_tensor("ind2", [2, 128], BF16, kind="ExternalInput").ap()
    ones1 = nc.dram_tensor("ones1", [128, 1], BF16, kind="ExternalInput").ap()
    onerow = nc.dram_tensor("onerow", [1, 128], F32, kind="ExternalInput").ap()
    out = nc.dram_tensor("out", [R, C], F32, kind="ExternalOutput").ap()
    sqg = nc.dram_tensor("sqg", [2 * H // 256, 128, 16, 256], BF16, kind="Internal").ap()
    sqp = nc.dram_tensor("sqp", [H, C], BF16, kind="Internal").ap()
    x1s = nc.dram_tensor("x1s", [R, C], BF16, kind="Internal").ap()
    thrh = nc.dram_tensor("thrh", [2, 4 * 1024], BF16, kind="Internal").ap()

    keep = ExitStack()
    def sbk(name, p, f, dt=F32):
        return keep.enter_context(nc.sbuf_tensor(name, [p, f], dt))

    pp = keep.enter_context(nc.psum_tensor("pp", [128, 4096], F32))
    ppb = pp.ap().bitcast(BF16)
    def bank(i):
        return pp[:, i * 512:(i + 1) * 512]

    idfs = sbk("idfs", 128, 128)
    idbs = sbk("idbs", 128, 128, BF16)
    ind64s = sbk("ind64s", 128, 2, BF16)
    ind2s = sbk("ind2s", 2, 128, BF16)
    ones1s = sbk("ones1s", 128, 1, BF16)
    onerow_s = sbk("onerow_s", 1, 128)
    rbc = sbk("rbc", 128, 1)
    zsrc = sbk("zsrc", 1, 8, BF16)
    sgnc = sbk("sgnc", 128, 1, I32)
    absc = sbk("absc", 128, 1, I32)
    qeps = sbk("qeps", 1, 1)

    es = ExitStack()
    def sb(name, p, f, dt=F32):
        return es.enter_context(nc.sbuf_tensor(name, [p, f], dt))

    att = sb("att", 128, 16 * C, BF16)
    xln = sb("xln", 128, 2 * C, BF16)
    xlnT = sb("xlnT", 128, 2 * 16 * 128, BF16)
    uu = sb("uu", 128, 2 * C, BF16)
    ww = sb("ww", 128, 2 * C, BF16)
    NZ = 2
    zring = sb("zring", 128, NZ * C, BF16)
    lnb = sb("lnb", 128, 2 * C, BF16)
    lnT = sb("lnT", 128, 2 * C, BF16)
    x1b = sb("x1b", 128, 2 * C, BF16)
    t1b = sb("t1b", 128, C, BF16)
    btw = sb("btw", 128, 3 * 2 * 2 * 128, BF16)
    stat = sb("stat", 128, 24)
    aggr = sb("aggr", 128, 2)
    ivar = sb("ivar", 128, 4)
    QSL = 4
    wqin = sb("wqin", 128, 5 * QW, BF16)
    awb = sb("awb", 128, 4 * QW, BF16)
    mkb = sb("mkb", 128, 2 * QW, BF16)
    m2b = sb("m2b", 128, 2 * QW, BF16)
    thrs = sb("thrs", 128, 2 * QW, BF16)
    thr2 = sb("thr2", 2, 4 * 1024, BF16)

    sems = ExitStack()
    def mk(name):
        return Ctr(sems, nc, name)
    s_z = [mk(f"s_z{i}") for i in range(NZ)]
    s_cst = mk("s_cst")
    s_btw = mk("s_btw")
    s_w = [mk(f"s_w{i}") for i in range(5)]
    s_qs = [mk(f"s_qs{i}") for i in range(2)]
    s_tsp = [mk(f"s_tsp{i}") for i in range(4)]
    s_tbc = [mk(f"s_tbc{i}") for i in range(4)]
    s_x1 = [mk(f"s_x1{i}") for i in range(2)]
    s_dg = mk("s_dg")
    cv = mk("cv"); cs = mk("cs"); ct = mk("ct"); cg = mk("cg")
    bar = mk("bar")

    ENGS = ("sync", "tensor", "vector", "scalar", "gpsimd")
    streams = {k: [] for k in ENGS}
    def em(eng, fn):
        streams[eng].append(fn)
    def wait(eng, ctr, val):
        if val is None or val <= 0:
            return
        streams[eng].append(lambda e, h=ctr.h, v=val: e.wait_ge(h, v))

    bank_ev = {}
    def bank_wait(eng, b):
        ev = bank_ev.get(b)
        if ev is not None:
            wait(eng, ev[0], ev[1])

    def _cst(e):
        e.dma_start(idfs[:], idf[:]).then_inc(s_cst.h, 16)
        e.dma_start(idbs[:], idb[:]).then_inc(s_cst.h, 16)
        e.dma_start(ind64s[:], ind64[:]).then_inc(s_cst.h, 16)
        e.dma_start(ind2s[:], ind2[:]).then_inc(s_cst.h, 16)
        e.dma_start(ones1s[:], ones1[:]).then_inc(s_cst.h, 16)
        e.dma_start(onerow_s[:], onerow[:]).then_inc(s_cst.h, 16)
        e.dma_start(rbc[0:1, 0:1], rl[0:1]).then_inc(s_cst.h, 16)
    em("sync", _cst); s_cst.n += 112

    def _btwload(e):
        for i, w in enumerate((bT, cT, aT)):
            e.dma_start(
                btw.ap().rearrange("p (w a b q) -> p w a b q", w=3, a=2, b=2)[:, i],
                w.rearrange("(a p) (b q) -> p a b q", p=128, q=128),
            ).then_inc(s_btw.h, 16)
    em("gpsimd", _btwload); s_btw.n += 48
    def _zs(e):
        e.memset(zsrc[:], 0.0)
        e.memset(sgnc[:], -2147450880)
        e.memset(absc[:], 2147450879)
        e.memset(qeps[:], 2.9802322387695312e-08).then_inc(cg.h, 1)
    em("gpsimd", _zs); cg.n += 1
    zsrc_cg = cg.n
    def btile(w, kt, mt):
        o = (w * 4 + kt * 2 + mt) * 128
        return btw[:, o:o + 128]

    wait("tensor", s_cst, s_cst.n)
    def _rbc(e):
        e.matmul(pp[:, 7 * 512:7 * 512 + 1], onerow_s[:], rbc[0:1, 0:1],
                 start=True, stop=True).then_inc(ct.h, 1)
    em("tensor", _rbc); ct.n += 1
    wait("vector", ct, ct.n)
    def _rbcev(e):
        e.tensor_copy(rbc[:], pp[:, 7 * 512:7 * 512 + 1]).then_inc(cv.h, 1)
    em("vector", _rbcev); cv.n += 1
    rbc_cv = cv.n
    b7_user = cv.n

    zround = [0] * NZ
    zfree = [None] * NZ

    def z_load(slot, blk_row):
        if zfree[slot] is not None:
            wait("gpsimd", zfree[slot][0], zfree[slot][1])
        def _f(e, sl=slot, r0=blk_row * 128):
            e.dma_start(zring[:, sl * C:(sl + 1) * C], z[r0:r0 + 128, :]).then_inc(s_z[sl].h, 16)
        em("gpsimd", _f)
        s_z[slot].n += 16
        zround[slot] += 1

    def ln_block(zslot, dst, dsttile):
        wait("vector", s_z[zslot], 16 * zround[zslot])
        zv = zring[:, zslot * C:(zslot + 1) * C]
        def _st(e, zv=zv):
            ii = None
            for i in range(4):
                ii = e.bn_stats(stat[:, i * 6:(i + 1) * 6], zv[:, i * 512:(i + 1) * 512])
            ii.then_inc(cv.h, 1)
        em("vector", _st); cv.n += 1
        wait("vector", cv, cv.n)
        def _ag(e):
            e.bn_aggr(aggr[:], stat[:]).then_inc(cv.h, 1)
        em("vector", _ag); cv.n += 1
        wait("vector", cv, cv.n)
        def _ve(e):
            e.tensor_scalar_add(ivar[:, 0:1], aggr[:, 1:2], LN_EPS).then_inc(cv.h, 1)
        em("vector", _ve); cv.n += 1
        wait("scalar", cv, cv.n)
        def _sq(e):
            e.activation(ivar[:, 1:2], ivar[:, 0:1], AF.Sqrt).then_inc(cs.h, 1)
        em("scalar", _sq); cs.n += 1
        wait("vector", cs, cs.n)
        def _rc(e):
            e.reciprocal(ivar[:, 2:3], ivar[:, 1:2]).then_inc(cv.h, 1)
        em("vector", _rc); cv.n += 1
        wait("vector", cv, cv.n)
        def _nm(e):
            e.scalar_tensor_tensor(ivar[:, 3:4], aggr[:, 0:1], -1.0, ivar[:, 2:3],
                                   ALU.mult, ALU.mult).then_inc(cv.h, 1)
        em("vector", _nm); cv.n += 1
        wait("scalar", cv, cv.n)
        def _ln(e, d=dst, i=dsttile, zv=zv):
            e.activation(d[:, i * C:(i + 1) * C], zv, AF.Identity,
                         bias=ivar[:, 3:4], scale=ivar[:, 2:3]).then_inc(cs.h, 1)
        em("scalar", _ln); cs.n += 1
        return cs.n

    tb45 = [None]

    def transpose_block(src, srcoff, dst, dstoff, ln_cs):
        wait("tensor", cs, ln_cs)
        bank_wait("tensor", 2)
        bank_wait("tensor", 3)
        def _tp(e, s=src, o=srcoff):
            ii = None
            for nt in range(16):
                ii = e.transpose(ppb[:, 2 * 1024 + nt * 128: 2 * 1024 + (nt + 1) * 128],
                                 s[:, o + nt * 128: o + (nt + 1) * 128], idbs[:])
            ii.then_inc(ct.h, 1)
        em("tensor", _tp); ct.n += 1
        wait("vector", ct, ct.n)
        def _ev(e, d=dst, o=dstoff):
            e.tensor_copy(d[:, o:o + 2048], ppb[:, 2 * 1024: 2 * 1024 + 2048]).then_inc(cv.h, 1)
        em("vector", _ev); cv.n += 1
        bank_ev[2] = (cv, cv.n)
        bank_ev[3] = (cv, cv.n)
        return cv.n

    qchunks = []
    for dt in range(16):
        for cc in range(2 * H // QW):
            dst = sqg[cc * 8:(cc + 1) * 8, :, dt, :].rearrange("h p w -> p h w")
            qchunks.append((wguT[dt * 128:(dt + 1) * 128, cc * QW:(cc + 1) * QW], dst))
    for jt in range(H // 128):
        qchunks.append((wpT[jt * 128:(jt + 1) * 128, :],
                        sqp[jt * 128:(jt + 1) * 128, :]))
    NQ = len(qchunks)

    qs = {"w_round": [0] * 4, "qs_round": [0] * 4,
          "aw_cv": [None] * 4, "mask_cv": [None] * 4, "aw_user": [None] * 4,
          "m2_cv": [None] * 4, "wq_cg": [None] * 4,
          "gs_ct": [None] * 4, "gs_user": {4: b7_user, 5: b7_user, 6: b7_user, 7: b7_user},
          "sp_n": [0] * 4, "bc_n": [0] * 4,
          "gs_pair_ct": {}, "th_cv": {}, "wqin_user": [None] * 5, "w_round4": [0] * 5}

    def q_load(j):
        if j < 0 or j >= NQ:
            return
        w4 = j % 5
        if qs["wqin_user"][w4] is not None:
            wait("gpsimd", cv, qs["wqin_user"][w4])
        if qs["w_round4"][w4] > 0:
            wait("gpsimd", s_w[w4], 16 * qs["w_round4"][w4])
        def _ld(e, w4=w4, src=qchunks[j][0]):
            e.dma_start(wqin[:, w4 * QW:(w4 + 1) * QW], src).then_inc(s_w[w4].h, 16)
        em("gpsimd", _ld)
        s_w[w4].n += 16
        qs["w_round4"][w4] += 1

    def q_aw(j):
        if j < 0 or j >= NQ:
            return
        sl = j % 4
        w4 = j % 5
        wait("vector", s_w[w4], 16 * qs["w_round4"][w4])
        if qs["gs_ct"][sl] is not None:
            wait("vector", ct, qs["gs_ct"][sl])
        if qs["aw_user"][sl] is not None:
            wait("vector", cv, qs["aw_user"][sl])
        def _aw(e, sl=sl, w4=w4):
            e.tensor_scalar(awb.ap().bitcast(I32)[:, sl * QW // 2:(sl + 1) * QW // 2],
                            wqin.ap().bitcast(I32)[:, w4 * QW // 2:(w4 + 1) * QW // 2],
                            absc[:, 0:1], None, ALU.bitwise_and).then_inc(cv.h, 1)
        em("vector", _aw); cv.n += 1
        qs["aw_cv"][sl] = cv.n

    def q_gs(j, pairi):
        if j < 0 or j >= NQ:
            return
        sl = j % 4
        b0 = 4 + 2 * ((j + pairi) % 2)
        wait("tensor", cv, qs["aw_cv"][j % 4])
        for b in (b0, b0 + 1):
            u = qs["gs_user"][b]
            if u is not None:
                wait("tensor", cv, u)
        def _gs(e, sl=sl, pairi=pairi, b0=b0):
            for i in (2 * pairi, 2 * pairi + 1):
                bnk = b0 + i % 2
                e.matmul(pp[0:2, bnk * 512:(bnk + 1) * 512], ind64s[:],
                         awb[:, sl * QW + i * 512: sl * QW + (i + 1) * 512],
                         start=True, stop=True).then_inc(ct.h, 1)
        em("tensor", _gs); ct.n += 2
        qs["gs_ct"][j % 4] = ct.n
        qs["gs_pair_ct"][(j, pairi)] = ct.n

    def q_th(j, pairi):
        if j < 0 or j >= NQ:
            return
        b0 = 4 + 2 * ((j + pairi) % 2)
        t2 = (j % 2) * 2 + pairi
        wait("vector", ct, qs["gs_pair_ct"][(j, pairi)])
        wait("vector", s_tsp[t2], 16 * qs["sp_n"][t2])
        def _th(e, t2=t2, b0=b0):
            e.tensor_scalar(thr2[0:2, t2 * 1024:(t2 + 1) * 1024],
                            pp[0:2, b0 * 512:(b0 + 2) * 512], 0.5 / 64.0, 0.5e-8,
                            ALU.mult, ALU.max).then_inc(cv.h, 1)
        em("vector", _th); cv.n += 1
        qs["th_cv"][(j, pairi)] = cv.n
        qs["gs_user"][b0 + 0] = cv.n
        qs["gs_user"][b0 + 1] = cv.n
        wait("scalar", cv, cv.n)
        if qs["bc_n"][t2] > 0:
            wait("scalar", s_tbc[t2], 32 * qs["bc_n"][t2])
        def _sp(e, t2=t2):
            e.dma_start(thrh[0:2, t2 * 1024:(t2 + 1) * 1024],
                        thr2[0:2, t2 * 1024:(t2 + 1) * 1024]).then_inc(s_tsp[t2].h, 16)
        em("scalar", _sp)
        s_tsp[t2].n += 16
        qs["sp_n"][t2] += 1

    def q_bc(j, pairi):
        if j < 0 or j >= NQ:
            return
        sl = j % 2
        t2 = (j % 2) * 2 + pairi
        wait("sync", s_tsp[t2], 16 * qs["sp_n"][t2])
        if pairi == 0 and qs["wq_cg"][sl] is not None:
            wait("sync", cg, qs["wq_cg"][sl])
        def _bc(e, sl=sl, t2=t2, pairi=pairi):
            for g in range(2):
                e.dma_start(thrs[g * 64:(g + 1) * 64, sl * QW + pairi * 1024: sl * QW + (pairi + 1) * 1024],
                            thrh[g:g + 1, t2 * 1024:(t2 + 1) * 1024].broadcast_to([64, 1024])
                            ).then_inc(s_tbc[t2].h, 16)
        em("sync", _bc)
        s_tbc[t2].n += 32
        qs["bc_n"][t2] += 1

    def q_mask(j):
        if j < 0 or j >= NQ:
            return
        sl = j % 2
        mksl = j % 2
        for pi in (0, 1):
            wait("vector", s_tbc[sl * 2 + pi], 32 * qs["bc_n"][sl * 2 + pi])
        if qs["qs_round"][mksl] > 0:
            wait("vector", s_qs[mksl], 16 * qs["qs_round"][mksl])
        def _mk(e, sl=sl, awsl=j % 4, mksl=mksl):
            e.tensor_tensor(mkb[:, mksl * QW:(mksl + 1) * QW], awb[:, awsl * QW:(awsl + 1) * QW],
                            thrs[:, sl * QW:(sl + 1) * QW], ALU.is_gt).then_inc(cv.h, 1)
        em("vector", _mk); cv.n += 1
        qs["mask_cv"][sl] = cv.n
        qs["aw_user"][j % 4] = cv.n
        wait("vector", cv, cv.n)
        if qs["wq_cg"][sl] is not None:
            wait("vector", cg, qs["wq_cg"][sl])
        def _m2(e, sl=j % 2, w4=j % 5):
            e.scalar_tensor_tensor(m2b.ap().bitcast(I32)[:, sl * QW // 2:(sl + 1) * QW // 2],
                                   wqin.ap().bitcast(I32)[:, w4 * QW // 2:(w4 + 1) * QW // 2],
                                   sgnc[:, 0:1],
                                   mkb.ap().bitcast(I32)[:, sl * QW // 2:(sl + 1) * QW // 2],
                                   ALU.bitwise_and, ALU.bitwise_or).then_inc(cv.h, 1)
        em("vector", _m2); cv.n += 1
        qs["m2_cv"][j % 2] = cv.n
        qs["wqin_user"][j % 5] = cv.n

    def q_wq(j):
        if j < 0 or j >= NQ:
            return
        sl = j % 2
        wait("gpsimd", cv, qs["m2_cv"][j % 2])
        def _wq(e, sl=sl, m2sl=j % 2):
            e.tensor_mul(mkb[:, m2sl * QW:(m2sl + 1) * QW], m2b[:, m2sl * QW:(m2sl + 1) * QW],
                         thrs[:, sl * QW:(sl + 1) * QW]).then_inc(cg.h, 1)
        em("gpsimd", _wq); cg.n += 1
        qs["wq_cg"][sl] = cg.n
        is_gu = j < 128
        qeng = "scalar" if is_gu else "sync"
        wait(qeng, cg, qs["wq_cg"][sl])
        mksl = j % 2
        def _stq(e, mksl=mksl, dst=qchunks[j][1], is_gu=is_gu):
            if is_gu:
                e.dma_start(dst, mkb.ap().rearrange("p (s h w) -> p s h w", s=2, h=8)[:, mksl]
                            ).then_inc(s_qs[mksl].h, 16)
            else:
                e.dma_start(dst, mkb[:, mksl * QW:(mksl + 1) * QW]).then_inc(s_qs[mksl].h, 16)
        em(qeng, _stq)
        s_qs[mksl].n += 16
        qs["qs_round"][mksl] += 1

    def emit_quant(k):
        if k == 0:
            q_load(0)
        q_th(k - 2, 0)
        q_th(k - 2, 1)
        q_bc(k - 3, 0)
        q_bc(k - 3, 1)
        q_mask(k - 4)
        q_wq(k - 4)
        q_load(k + 1)
        q_aw(k)
        q_gs(k - 1, 0)
        q_gs(k - 1, 1)


    z_load(0, 0)
    z_load(1, 1)
    for _k in range(4):
        emit_quant(_k)
    ln_cs0 = ln_block(0, xln, 0)
    transpose_block(xln, 0 * C, xlnT, 0 * 2048, ln_cs0)
    ln_cs1 = ln_block(1, xln, 1)
    xT_cv = transpose_block(xln, 1 * C, xlnT, 1 * 2048, ln_cs1)
    xln_cs = cs.n

    wait("tensor", s_btw, s_btw.n)
    wait("tensor", cs, xln_cs)
    def mm_uw(widx, dst):
        for dt in range(2):
            for ncc in range(4):
                b = (dt * 4 + ncc) % 2
                bank_wait("tensor", b)
                def _mm(e, dt=dt, ncc=ncc, b=b, w=widx):
                    ii = None
                    for kt in range(2):
                        ii = e.matmul(bank(b), btile(w, kt, dt),
                                      xln[:, kt * C + ncc * 512: kt * C + (ncc + 1) * 512],
                                      start=(kt == 0), stop=(kt == 1))
                    ii.then_inc(ct.h, 1)
                em("tensor", _mm); ct.n += 1
                wait("scalar", ct, ct.n)
                def _ev(e, dt=dt, ncc=ncc, b=b, d=dst):
                    e.activation(d[:, dt * C + ncc * 512: dt * C + (ncc + 1) * 512],
                                 bank(b), AF.Identity).then_inc(cs.h, 1)
                em("scalar", _ev); cs.n += 1
                bank_ev[b] = (cs, cs.n)
    emit_quant(4)
    emit_quant(5)
    mm_uw(0, uu)
    mm_uw(1, ww)
    emit_quant(6)
    emit_quant(7)
    uw_cs = cs.n

    for nt in range(16):
        for mc in range(4):
            b = (nt * 4 + mc) % 4
            bank_wait("tensor", b)
            def _mm(e, nt=nt, mc=mc, b=b):
                ii = None
                for kt in range(2):
                    ii = e.matmul(bank(b), uu[:, kt * C + nt * 128: kt * C + (nt + 1) * 128],
                                  ww[:, kt * C + mc * 512: kt * C + (mc + 1) * 512],
                                  start=(kt == 0), stop=(kt == 1))
                ii.then_inc(ct.h, 1)
            em("tensor", _mm); ct.n += 1
            wait("scalar", ct, ct.n)
            def _ev(e, nt=nt, mc=mc, b=b):
                e.activation(att[:, nt * C + mc * 512: nt * C + (mc + 1) * 512],
                             bank(b), AF.Relu).then_inc(cs.h, 1)
            em("scalar", _ev); cs.n += 1
            bank_ev[b] = (cs, cs.n)
    att_cs = cs.n
    scores_ct = ct.n
    wait("gpsimd", cs, att_cs)
    wait("gpsimd", cg, zsrc_cg)
    def _dg(e):
        e.dma_start(att[127:128, 15 * C + 2047: 15 * C + 2048], zsrc[0:1, 0:1]).then_inc(s_dg.h, 16)
    em("gpsimd", _dg); s_dg.n += 16

    wait("tensor", cv, xT_cv)
    wait("tensor", s_dg, s_dg.n)
    xatt = uu
    for dt in range(2):
        for mc in range(4):
            b = (dt * 4 + mc) % 2
            bank_wait("tensor", b)
            def _mm(e, dt=dt, mc=mc, b=b):
                ii = None
                for nt in range(16):
                    ii = e.matmul(bank(b), xlnT[:, (dt * 16 + nt) * 128: (dt * 16 + nt + 1) * 128],
                                  att[:, nt * C + mc * 512: nt * C + (mc + 1) * 512],
                                  start=(nt == 0), stop=(nt == 15))
                ii.then_inc(ct.h, 1)
            em("tensor", _mm); ct.n += 1
            wait("scalar", ct, ct.n)
            wait("scalar", ct, scores_ct)
            def _ev(e, dt=dt, mc=mc, b=b):
                e.activation(xatt[:, dt * C + mc * 512: dt * C + (mc + 1) * 512],
                             bank(b), AF.Identity).then_inc(cs.h, 1)
            em("scalar", _ev); cs.n += 1
            bank_ev[b] = (cs, cs.n)
    xatt_cs = cs.n
    for _k in range(8, 12):
        emit_quant(_k)

    wait("gpsimd", cs, xln_cs)
    z_load(0, 0)
    z_load(1, 1)
    x1_round = [0, 0]
    wait("tensor", cs, xatt_cs)
    for it in range(2):
        for mc in range(4):
            b = mc
            bank_wait("tensor", b)
            def _mm(e, it=it, mc=mc, b=b):
                ii = None
                for kd in range(2):
                    ii = e.matmul(bank(b), btile(2, kd, it),
                                  xatt[:, kd * C + mc * 512: kd * C + (mc + 1) * 512],
                                  start=(kd == 0), stop=(kd == 1))
                ii.then_inc(ct.h, 1)
            em("tensor", _mm); ct.n += 1
            bank_ev[b] = None
        ct_x = ct.n
        wait("vector", s_z[it], 16 * zround[it])
        wait("vector", cv, cv.n)
        def _t1(e, it=it):
            e.tensor_add(t1b[:, 0:C], zring[:, it * C:(it + 1) * C],
                         xln[:, it * C:(it + 1) * C]).then_inc(cv.h, 1)
        em("vector", _t1); cv.n += 1
        wait("vector", cv, cv.n)
        wait("vector", ct, ct_x)
        def _x1(e, it=it):
            ii = None
            for mc in range(4):
                ii = e.scalar_tensor_tensor(
                    x1b[:, it * C + mc * 512: it * C + (mc + 1) * 512],
                    bank(mc), 1.0, t1b[:, mc * 512:(mc + 1) * 512],
                    ALU.bypass, ALU.add)
            ii.then_inc(cv.h, 1)
        em("vector", _x1); cv.n += 1
        for mc in range(4):
            bank_ev[mc] = (cv, cv.n)
        zfree[it] = (cv, cv.n)
        wait("sync", cv, cv.n)
        def _sp(e, it=it):
            e.dma_start(x1s[it * 128:(it + 1) * 128, :],
                        x1b[:, it * C:(it + 1) * C]).then_inc(s_x1[it].h, 16)
        em("sync", _sp)
        s_x1[it].n += 16
        x1_round[it] += 1

    lnslot = [None, None]

    def emit_yblock(yi):
        blk = 2 + yi
        zsl = blk % NZ
        sl = yi % 2
        z_load(zsl, blk)
        if lnslot[sl] is not None:
            wait("scalar", cv, lnslot[sl])
        yield
        ln_cs = ln_block(zsl, lnb, sl)
        yield
        lnT_cv = transpose_block(lnb, sl * C, lnT, sl * C, ln_cs)
        wait("tensor", cv, lnT_cv)
        yield
        wait("vector", s_z[zsl], 16 * zround[zsl])
        wait("vector", cs, ln_cs)
        if lnslot[sl] is not None:
            wait("vector", cv, lnslot[sl])
        def _t1(e, sl=sl, zsl=zsl):
            e.tensor_add(t1b[:, 0:C], zring[:, zsl * C:(zsl + 1) * C],
                         lnb[:, sl * C:(sl + 1) * C]).then_inc(cv.h, 1)
        em("vector", _t1); cv.n += 1
        t1cv = cv.n
        zfree[zsl] = (cv, cv.n)
        if x1_round[sl] > 0:
            wait("vector", s_x1[sl], 16 * x1_round[sl])
        for half in range(2):
            for q in range(2):
                bank_wait("tensor", q)
                def _mm(e, sl=sl, half=half, q=q):
                    ii = None
                    for nt in range(16):
                        mc = 2 * half + q
                        ii = e.matmul(bank(q), lnT[:, sl * C + nt * 128: sl * C + (nt + 1) * 128],
                                      att[:, nt * C + mc * 512: nt * C + (mc + 1) * 512],
                                      start=(nt == 0), stop=(nt == 15))
                    ii.then_inc(ct.h, 1)
                em("tensor", _mm); ct.n += 1
                ct_y = ct.n
                yield
                wait("vector", cv, t1cv)
                wait("vector", ct, ct_y)
                def _x1(e, sl=sl, half=half, q=q):
                    mc = 2 * half + q
                    e.scalar_tensor_tensor(
                        x1b[:, sl * C + mc * 512: sl * C + (mc + 1) * 512],
                        bank(q), rbc[:, 0:1], t1b[:, mc * 512:(mc + 1) * 512],
                        ALU.mult, ALU.add).then_inc(cv.h, 1)
                em("vector", _x1); cv.n += 1
                bank_ev[q] = (cv, cv.n)
        lnslot[sl] = cv.n
        wait("sync", cv, cv.n)
        def _sp(e, sl=sl, blk=blk):
            e.dma_start(x1s[blk * 128:(blk + 1) * 128, :],
                        x1b[:, sl * C:(sl + 1) * C]).then_inc(s_x1[sl].h, 16)
        em("sync", _sp)
        s_x1[sl].n += 16
        x1_round[sl] += 1

    NQI = NQ + 4
    qi = 12
    for yi in range(NYB):
        ql = min(NQI, 12 + (yi + 1) * (NQI - 12) // NYB)
        g = emit_yblock(yi)
        for _ in g:
            if qi < ql:
                emit_quant(qi)
                qi += 1
        while qi < ql:
            emit_quant(qi)
            qi += 1
    while qi < NQI:
        emit_quant(qi)
        qi += 1

    for eng in ENGS:
        def _b(e):
            e.drain().then_inc(bar.h, 1)
        em(eng, _b)
    bar.n += 5
    for eng in ENGS:
        wait(eng, bar, bar.n)
    wait("sync", s_x1[0], 16 * x1_round[0])
    wait("sync", s_x1[1], 16 * x1_round[1])
    for _qi in range(2):
        wait("sync", s_qs[_qi], 16 * qs["qs_round"][_qi])
    def _b2(e):
        e.drain().then_inc(bar.h, 1)
    em("sync", _b2); bar.n += 1
    for eng in ("tensor", "vector", "scalar", "gpsimd"):
        wait(eng, bar, bar.n)

    es.close()
    es2 = ExitStack()
    def sb2(name, p, f, dt=F32):
        return es2.enter_context(nc.sbuf_tensor(name, [p, f], dt))
    HRW = 256
    NHR = H // HRW
    hsil = sb2("hsil", 128, 64 * 512, BF16)
    ln2T = sb2("ln2T", 128, 16 * 512, BF16)
    x1c = sb2("x1c", 128, 2 * 4 * C, BF16)
    wgb = sb2("wgb", 128, 4 * 16 * HRW, BF16)
    wpb = sb2("wpb", 128, 4 * 1024, BF16)
    sgl = sb2("sgl", 128, 2 * 512)
    hsq2 = sb2("hsq2", 128, 2 * 512, BF16)
    xo = sb2("xo", 128, 2 * 1024)
    invv = sb2("invv", 1, 512)
    invc = sb2("invc", 128, 4)
    st2 = sb2("st2", 128, 24)
    ag2 = sb2("ag2", 128, 2)
    iv2 = sb2("iv2", 128, 4)
    ln2b = sb2("ln2b", 128, 4 * C, BF16)

    s_xc = [mk("s_xc0"), mk("s_xc1")]
    s_wg = [mk(f"s_wg{i}") for i in range(4)]
    s_wp = [mk(f"s_wp{i}") for i in range(4)]
    s_out = [mk(f"s_out{i}") for i in range(2)]

    fx = {"xc_cons": [None, None], "wg_cons": [None] * 4, "wp_cons": [None] * 4,
          "hs_cv": [None] * 64, "sg_free": [None, None], "hq_ct": [None, None],
          "hq_cv": [None, None], "xo_user": [None] * 8, "ss_ct": None}

    def xc_load(ch):
        sl = ch % 2
        if fx["xc_cons"][sl] is not None:
            wait("sync", cv, fx["xc_cons"][sl])
        def _f(e, sl=sl, ch=ch):
            e.dma_start(x1c.ap().rearrange("p (s b c) -> p s b c", s=2, b=4)[:, sl],
                        x1s.rearrange("(b p) c -> p b c", p=128)[:, ch * 4:(ch + 1) * 4]
                        ).then_inc(s_xc[sl].h, 16)
        em("sync", _f)
        s_xc[sl].n += 16

    def wg_load(hrg):
        for half, hr in ((0, hrg), (1, hrg + NHR)):
            sl = (hrg % 2) * 2 + half
            if fx["wg_cons"][sl] is not None:
                wait("sync", ct, fx["wg_cons"][sl])
            def _f(e, sl=sl, hr=hr):
                e.dma_start(wgb.ap().rearrange("p (s d h) -> p s d h", s=4, d=16)[:, sl],
                            sqg[hr]).then_inc(s_wg[sl].h, 16)
            em("sync", _f)
            s_wg[sl].n += 16

    def wp_load(jt, dchalf):
        sl = jt % 4
        if fx["wp_cons"][sl] is not None:
            wait("scalar", ct, fx["wp_cons"][sl])
        def _f(e, sl=sl, jt=jt, dchalf=dchalf):
            e.dma_start(wpb[:, sl * 1024:(sl + 1) * 1024],
                        sqp[jt * 128:(jt + 1) * 128, dchalf * 1024:(dchalf + 1) * 1024]
                        ).then_inc(s_wp[sl].h, 16)
        em("scalar", _f)
        s_wp[sl].n += 16

    ln_cs_of = [None] * 4
    tp_ct_of = [None] * 4

    def ln2_chain(ch, tb):
        xsl = ch % 2
        xv = x1c.ap().rearrange("p (s b c) -> p s b c", s=2, b=4)[:, xsl, tb]
        wait("vector", s_xc[xsl], 16 * (ch // 2 + 1))
        def _st(e, xv=xv):
            ii = None
            for i in range(4):
                ii = e.bn_stats(st2[:, i * 6:(i + 1) * 6], xv[:, i * 512:(i + 1) * 512])
            ii.then_inc(cv.h, 1)
        em("vector", _st); cv.n += 1
        wait("vector", cv, cv.n)
        def _ag(e):
            e.bn_aggr(ag2[:], st2[:]).then_inc(cv.h, 1)
        em("vector", _ag); cv.n += 1
        wait("vector", cv, cv.n)
        def _ve(e):
            e.tensor_scalar(iv2[:, 0:1], ag2[:, 1:2], LN_EPS, 0.25,
                            ALU.add, ALU.mult).then_inc(cv.h, 1)
        em("vector", _ve); cv.n += 1
        wait("scalar", cv, cv.n)
        def _sq(e):
            e.activation(iv2[:, 1:2], iv2[:, 0:1], AF.Sqrt).then_inc(cs.h, 1)
        em("scalar", _sq); cs.n += 1
        wait("vector", cs, cs.n)
        def _rc(e):
            e.reciprocal(iv2[:, 2:3], iv2[:, 1:2]).then_inc(cv.h, 1)
        em("vector", _rc); cv.n += 1
        wait("vector", cv, cv.n)
        def _nm(e):
            e.scalar_tensor_tensor(iv2[:, 3:4], ag2[:, 0:1], -1.0, iv2[:, 2:3],
                                   ALU.mult, ALU.mult).then_inc(cv.h, 1)
        em("vector", _nm); cv.n += 1
        wait("scalar", cv, cv.n)
        if tp_ct_of[tb] is not None:
            wait("scalar", ct, tp_ct_of[tb])
        def _ln(e, xv=xv, tb=tb):
            e.activation(ln2b[:, tb * C:(tb + 1) * C], xv, AF.Identity, bias=iv2[:, 3:4],
                         scale=iv2[:, 2:3]).then_inc(cs.h, 1)
        em("scalar", _ln); cs.n += 1
        ln_cs_of[tb] = cs.n

    def ln2_tp(tb):
        wait("tensor", cs, ln_cs_of[tb])
        if tb45[0] is not None:
            wait("tensor", tb45[0][0], tb45[0][1])
        def _tp(e, tb=tb):
            ii = None
            for dt in range(16):
                ii = e.transpose(ppb[:, 4 * 1024 + dt * 128: 4 * 1024 + (dt + 1) * 128],
                                 ln2b[:, tb * C + dt * 128: tb * C + (dt + 1) * 128], idbs[:])
            ii.then_inc(ct.h, 1)
        em("tensor", _tp); ct.n += 1
        tp_ct_of[tb] = ct.n
        wait("vector", ct, ct.n)
        def _ev(e, tb=tb):
            e.tensor_copy(ln2T.ap().rearrange("p (d t) -> p d t", d=16)[:, :, tb * 128:(tb + 1) * 128],
                          ppb[:, 4 * 1024: 4 * 1024 + 2048].rearrange("p (d t) -> p d t", d=16)
                          ).then_inc(cv.h, 1)
        em("vector", _ev); cv.n += 1
        tb45[0] = (cv, cv.n)

    xc_load(0)
    for tb in range(4):
        ln2_chain(0, tb)
    for ch in range(NCH):
        sl = ch % 2
        if ch + 1 < NCH:
            xc_load(ch + 1)
        for tb in range(4):
            ln2_tp(tb)
        ln2T_cv = cv.n

        pend_ss = [None]
        wait("tensor", cv, ln2T_cv)
        wg_load(0)
        for hrg in range(NHR):
            if hrg + 1 < NHR:
                wg_load(hrg + 1)
            if ch + 1 < NCH and hrg in (6, 12, 18, 24):
                ln2_chain(ch + 1, (hrg - 6) // 6)
            slg = (hrg % 2) * 2
            slu = slg + 1
            wait("tensor", s_wg[slg], s_wg[slg].n)
            wait("tensor", s_wg[slu], s_wg[slu].n)
            for hi in range(HRW // 128):
                jt = hrg * (HRW // 128) + hi
                gb = jt % 2
                if fx["sg_free"][gb] is not None:
                    wait("tensor", cs, fx["sg_free"][gb])
                if fx["hq_cv"][gb] is not None:
                    wait("tensor", cv, fx["hq_cv"][gb])
                def _mmg(e, slg=slg, hi=hi, gb=gb):
                    w = wgb.ap().rearrange("p (s d h) -> p s d h", s=4, d=16)[:, slg]
                    ii = None
                    for dt in range(16):
                        ii = e.matmul(bank(gb), w[:, dt, hi * 128:(hi + 1) * 128],
                                      ln2T[:, dt * 512:(dt + 1) * 512],
                                      start=(dt == 0), stop=(dt == 15))
                    ii.then_inc(ct.h, 1)
                em("tensor", _mmg); ct.n += 1
                def _mmu(e, slu=slu, hi=hi, gb=gb):
                    w = wgb.ap().rearrange("p (s d h) -> p s d h", s=4, d=16)[:, slu]
                    ii = None
                    for dt in range(16):
                        ii = e.matmul(bank(2 + gb), w[:, dt, hi * 128:(hi + 1) * 128],
                                      ln2T[:, dt * 512:(dt + 1) * 512],
                                      start=(dt == 0), stop=(dt == 15))
                    ii.then_inc(ct.h, 1)
                em("tensor", _mmu); ct.n += 1
                ct_pair = ct.n
                if hi == HRW // 128 - 1:
                    fx["wg_cons"][slg] = ct.n
                    fx["wg_cons"][slu] = ct.n
                wait("scalar", ct, ct_pair - 1)
                if fx["hq_cv"][gb] is not None:
                    wait("scalar", cv, fx["hq_cv"][gb])
                def _si(e, gb=gb):
                    e.activation(sgl[:, gb * 512:(gb + 1) * 512], bank(gb), AF.Silu
                                 ).then_inc(cs.h, 1)
                em("scalar", _si); cs.n += 1
                sg_cs = cs.n
                fx["sg_free"][gb] = cs.n
                wait("vector", cs, sg_cs)
                wait("vector", ct, ct_pair)
                def _hs(e, jt=jt, gb=gb):
                    e.tensor_mul(hsil[:, jt * 512:(jt + 1) * 512], sgl[:, gb * 512:(gb + 1) * 512],
                                 bank(2 + gb)).then_inc(cv.h, 1)
                em("vector", _hs); cv.n += 1
                hs_cv = cv.n
                fx["hs_cv"][jt] = hs_cv
                wait("vector", cv, hs_cv)
                if fx["hq_ct"][gb] is not None:
                    wait("vector", ct, fx["hq_ct"][gb])
                def _hq(e, jt=jt, gb=gb):
                    e.tensor_mul(hsq2[:, gb * 512:(gb + 1) * 512], hsil[:, jt * 512:(jt + 1) * 512],
                                 hsil[:, jt * 512:(jt + 1) * 512]).then_inc(cv.h, 1)
                em("vector", _hq); cv.n += 1
                fx["hq_cv"][gb] = cv.n
                def emit_ss(jt=jt, gb=gb, hqv=cv.n):
                    wait("tensor", cv, hqv)
                    def _ss(e, jt=jt, gb=gb):
                        e.matmul(pp[0:1, 6 * 512:7 * 512], ones1s[:], hsq2[:, gb * 512:(gb + 1) * 512],
                                 start=(jt == 0), stop=(jt == 63)).then_inc(ct.h, 1)
                    em("tensor", _ss); ct.n += 1
                    fx["hq_ct"][gb] = ct.n
                if pend_ss[0] is not None:
                    pend_ss[0]()
                pend_ss[0] = emit_ss
        if pend_ss[0] is not None:
            pend_ss[0]()
            pend_ss[0] = None
        gu_ct = ct.n

        wait("vector", ct, gu_ct)
        def _ms(e):
            e.tensor_scalar(invv[0:1, :], pp[0:1, 6 * 512:7 * 512], 0.25 / 8192.0,
                            2.9802322387695312e-08, ALU.mult, ALU.add).then_inc(cv.h, 1)
        em("vector", _ms); cv.n += 1
        wait("scalar", cv, cv.n)
        def _sq2(e):
            e.activation(invv[0:1, :], invv[0:1, :], AF.Sqrt).then_inc(cs.h, 1)
        em("scalar", _sq2); cs.n += 1
        wait("vector", cs, cs.n)
        def _rc2(e):
            e.reciprocal(invv[0:1, :], invv[0:1, :]).then_inc(cv.h, 1)
        em("vector", _rc2); cv.n += 1
        inv_cv = cv.n
        wait("tensor", cv, inv_cv)
        def _ti(e):
            ii = None
            for tb in range(4):
                ii = e.transpose(pp[:, 7 * 512 + tb: 7 * 512 + tb + 1],
                                 invv[0:1, tb * 128:(tb + 1) * 128], idfs[0:1, 0:1])
            ii.then_inc(ct.h, 1)
        em("tensor", _ti); ct.n += 1
        wait("vector", ct, ct.n)
        def _ic(e):
            e.tensor_copy(invc[:], pp[:, 7 * 512: 7 * 512 + 4]).then_inc(cv.h, 1)
        em("vector", _ic); cv.n += 1
        invc_cv = cv.n

        for dchalf in range(2):
            wait("tensor", cv, invc_cv)
            if dchalf == 1:
                for u in range(8):
                    if fx["xo_user"][u] is not None:
                        wait("tensor", cv, fx["xo_user"][u])
            wp_load(0, dchalf)
            wp_load(1, dchalf)
            for jt in range(64):
                if jt + 2 < 64:
                    wp_load(jt + 2, dchalf)
                wsl = jt % 4
                wait("tensor", s_wp[wsl], s_wp[wsl].n)
                wait("tensor", cv, fx["hs_cv"][jt])
                def _mm(e, jt=jt, wsl=wsl):
                    ii = None
                    for tb in range(4):
                        for dc in range(2):
                            ii = e.matmul(bank(tb * 2 + dc),
                                          hsil[:, jt * 512 + tb * 128: jt * 512 + (tb + 1) * 128],
                                          wpb[:, wsl * 1024 + dc * 512: wsl * 1024 + (dc + 1) * 512],
                                          start=(jt == 0), stop=(jt == 63))
                    ii.then_inc(ct.h, 1)
                em("tensor", _mm); ct.n += 1
                fx["wp_cons"][wsl] = ct.n
            mm2_ct = ct.n
            wait("vector", ct, mm2_ct)
            for tb in range(4):
                xsl = tb % 2
                if s_out[xsl].n > 0:
                    wait("vector", s_out[xsl], s_out[xsl].n)
                def _xo(e, tb=tb, sl=sl, dchalf=dchalf, xsl=xsl):
                    xv = x1c.ap().rearrange("p (s b c) -> p s b c", s=2, b=4)[:, sl, tb]
                    ii = None
                    for dc in range(2):
                        d0 = dchalf * 1024 + dc * 512
                        ii = e.scalar_tensor_tensor(
                            xo[:, xsl * 1024 + dc * 512: xsl * 1024 + (dc + 1) * 512],
                            bank(tb * 2 + dc), invc[:, tb:tb + 1], xv[:, d0:d0 + 512],
                            ALU.mult, ALU.add)
                    ii.then_inc(cv.h, 1)
                em("vector", _xo); cv.n += 1
                fx["xo_user"][dchalf * 4 + tb] = cv.n
                wait("gpsimd", cv, cv.n)
                def _ow(e, tb=tb, ch=ch, dchalf=dchalf, xsl=xsl):
                    e.dma_start(out[(ch * 4 + tb) * 128:(ch * 4 + tb + 1) * 128,
                                    dchalf * 1024:(dchalf + 1) * 1024],
                                xo[:, xsl * 1024:(xsl + 1) * 1024]).then_inc(s_out[xsl].h, 16)
                em("gpsimd", _ow)
                s_out[xsl].n += 16
        fx["xc_cons"][sl] = cv.n
        tb45[0] = (cv, cv.n)

    for q in range(2):
        wait("gpsimd", s_out[q], s_out[q].n)

    blk = es2.enter_context(nc.Block())
    for name, attr in (("sync", blk.sync), ("tensor", blk.tensor), ("vector", blk.vector),
                       ("scalar", blk.scalar), ("gpsimd", blk.gpsimd)):
        def _run(e, ops=streams[name]):
            for op in ops:
                op(e)
        attr(_run)
    es2.close()
    keep.close()
    sems.close()
    return nc


def kernel(L0_attn_Z_in, L0_attn_A_l, L0_attn_r_l, L0_attn_B_l, L0_attn_C_l,
           L0_ffn_w_gate_up, L0_ffn_w_proj):
    bf = ml_dtypes.bfloat16
    Z = np.asarray(L0_attn_Z_in, dtype=np.float32)
    A = np.ascontiguousarray(np.asarray(L0_attn_A_l, dtype=np.float32).T)
    B = np.ascontiguousarray(np.asarray(L0_attn_B_l, dtype=np.float32).T)
    Cm = np.ascontiguousarray(np.asarray(L0_attn_C_l, dtype=np.float32).T)
    r = np.asarray(L0_attn_r_l, dtype=np.float32).reshape(1)
    WguT = np.ascontiguousarray(np.asarray(L0_ffn_w_gate_up, dtype=np.float32).T)
    WpT = np.ascontiguousarray(np.asarray(L0_ffn_w_proj, dtype=np.float32).T)

    idf = np.eye(128, dtype=np.float32)
    idb = np.eye(128, dtype=np.float32).astype(bf)
    ind64 = np.zeros((128, 2), dtype=np.float32)
    ind64[0:64, 0] = 1.0
    ind64[64:128, 1] = 1.0
    ind2 = np.ascontiguousarray(ind64.T).astype(bf)
    ind64 = ind64.astype(bf)
    ones1 = np.ones((128, 1), dtype=np.float32).astype(bf)
    onerow = np.ones((1, 128), dtype=np.float32)

    nc = build_program(R=Z.shape[1])
    common = dict(bT=B, cT=Cm, aT=A, rl=r, wguT=WguT, wpT=WpT, idf=idf, idb=idb,
                  ind64=ind64, ind2=ind2, ones1=ones1, onerow=onerow)
    in_maps = [dict(common, z=np.ascontiguousarray(Z[b])) for b in range(Z.shape[0])]
    res = run_bass_kernel_spmd(nc, in_maps, core_ids=list(range(8)))
    return np.stack([np.asarray(res.results[b]["out"]) for b in range(Z.shape[0])], axis=0)


# version 38
# speedup vs baseline: 1.0363x; 1.0363x over previous
import numpy as np
import ml_dtypes
from contextlib import ExitStack

import concourse.bass as bass
import concourse.mybir as mybir
from concourse.bass_utils import run_bass_kernel_spmd

F32 = mybir.dt.float32
BF16 = mybir.dt.bfloat16
I16 = mybir.dt.int16
I32 = mybir.dt.int32
AF = mybir.ActivationFunctionType
ALU = mybir.AluOpType

C = 2048
D = 256
H = 8192
RMS_EPS = 1.1920928955078125e-07
LN_EPS = 1e-5
QW = 2048


class Ctr:
    def __init__(self, es, nc, name):
        self.h = es.enter_context(nc.semaphore(name))
        self.n = 0


def build_program(R=4096):
    NB = R // 128
    NYB = NB - 2
    NCH = R // 512
    nc = bass.Bass("TRN2", target_bir_lowering=False)

    z = nc.dram_tensor("z", [R, C], F32, kind="ExternalInput").ap()
    bT = nc.dram_tensor("bT", [D, D], F32, kind="ExternalInput").ap()
    cT = nc.dram_tensor("cT", [D, D], F32, kind="ExternalInput").ap()
    aT = nc.dram_tensor("aT", [D, D], F32, kind="ExternalInput").ap()
    rl = nc.dram_tensor("rl", [1], F32, kind="ExternalInput").ap()
    wguT = nc.dram_tensor("wguT", [C, 2 * H], F32, kind="ExternalInput").ap()
    wpT = nc.dram_tensor("wpT", [H, C], F32, kind="ExternalInput").ap()
    idf = nc.dram_tensor("idf", [128, 128], F32, kind="ExternalInput").ap()
    idb = nc.dram_tensor("idb", [128, 128], BF16, kind="ExternalInput").ap()
    ind64 = nc.dram_tensor("ind64", [128, 2], BF16, kind="ExternalInput").ap()
    ind2 = nc.dram_tensor("ind2", [2, 128], BF16, kind="ExternalInput").ap()
    ones1 = nc.dram_tensor("ones1", [128, 1], BF16, kind="ExternalInput").ap()
    onerow = nc.dram_tensor("onerow", [1, 128], F32, kind="ExternalInput").ap()
    out = nc.dram_tensor("out", [R, C], F32, kind="ExternalOutput").ap()
    sqg = nc.dram_tensor("sqg", [2 * H // 256, 128, 16, 256], BF16, kind="Internal").ap()
    sqp = nc.dram_tensor("sqp", [H, C], BF16, kind="Internal").ap()
    x1s = nc.dram_tensor("x1s", [R, C], BF16, kind="Internal").ap()
    thrh = nc.dram_tensor("thrh", [2, 4 * 1024], BF16, kind="Internal").ap()

    keep = ExitStack()
    def sbk(name, p, f, dt=F32):
        return keep.enter_context(nc.sbuf_tensor(name, [p, f], dt))

    pp = keep.enter_context(nc.psum_tensor("pp", [128, 4096], F32))
    ppb = pp.ap().bitcast(BF16)
    def bank(i):
        return pp[:, i * 512:(i + 1) * 512]

    idfs = sbk("idfs", 128, 128)
    idbs = sbk("idbs", 128, 128, BF16)
    ind64s = sbk("ind64s", 128, 2, BF16)
    ind2s = sbk("ind2s", 2, 128, BF16)
    ones1s = sbk("ones1s", 128, 1, BF16)
    onerow_s = sbk("onerow_s", 1, 128)
    rbc = sbk("rbc", 128, 1)
    zsrc = sbk("zsrc", 1, 8, BF16)
    sgnc = sbk("sgnc", 128, 1, I32)
    absc = sbk("absc", 128, 1, I32)
    qeps = sbk("qeps", 1, 1)
    thsc = sbk("thsc", 128, 1)
    nege = sbk("nege", 128, 1)
    pose = sbk("pose", 128, 1)

    es = ExitStack()
    def sb(name, p, f, dt=F32):
        return es.enter_context(nc.sbuf_tensor(name, [p, f], dt))

    att = sb("att", 128, 16 * C, BF16)
    xln = sb("xln", 128, 2 * C, BF16)
    xlnT = sb("xlnT", 128, 2 * 16 * 128, BF16)
    uu = sb("uu", 128, 2 * C, BF16)
    ww = sb("ww", 128, 2 * C, BF16)
    NZ = 2
    zring = sb("zring", 128, NZ * C, BF16)
    lnb = sb("lnb", 128, 2 * C, BF16)
    lnT = sb("lnT", 128, 2 * C, BF16)
    x1b = sb("x1b", 128, 2 * C, BF16)
    t1b = sb("t1b", 128, C, BF16)
    btw = sb("btw", 128, 3 * 2 * 2 * 128, BF16)
    stat = sb("stat", 128, 24)
    aggr = sb("aggr", 128, 2)
    ivar = sb("ivar", 128, 4)
    QSL = 4
    wqin = sb("wqin", 128, 6 * QW, BF16)
    awb = sb("awb", 128, 4 * QW, BF16)
    mkb = sb("mkb", 128, 2 * QW, BF16)
    m2b = sb("m2b", 128, 2 * QW, BF16)
    thrs = sb("thrs", 128, 2 * QW, BF16)
    thr2 = sb("thr2", 2, 2 * 1024, BF16)
    tmpth = sb("tmpth", 2, 1024, BF16)

    sems = ExitStack()
    def mk(name):
        return Ctr(sems, nc, name)
    s_z = [mk(f"s_z{i}") for i in range(NZ)]
    s_cst = mk("s_cst")
    s_btw = mk("s_btw")
    s_w = [mk(f"s_w{i}") for i in range(6)]
    s_qs = [mk(f"s_qs{i}") for i in range(2)]
    s_tsp = [mk(f"s_tsp{i}") for i in range(4)]
    s_tbc = [mk(f"s_tbc{i}") for i in range(4)]
    s_x1 = [mk(f"s_x1{i}") for i in range(2)]
    s_dg = mk("s_dg")
    cv = mk("cv"); cs = mk("cs"); ct = mk("ct"); cg = mk("cg")
    bar = mk("bar")

    ENGS = ("sync", "tensor", "vector", "scalar", "gpsimd")
    streams = {k: [] for k in ENGS}
    def em(eng, fn):
        streams[eng].append(fn)
    def wait(eng, ctr, val):
        if val is None or val <= 0:
            return
        streams[eng].append(lambda e, h=ctr.h, v=val: e.wait_ge(h, v))

    bank_ev = {}
    def bank_wait(eng, b):
        ev = bank_ev.get(b)
        if ev is not None:
            wait(eng, ev[0], ev[1])

    def _cst(e):
        e.dma_start(idfs[:], idf[:]).then_inc(s_cst.h, 16)
        e.dma_start(idbs[:], idb[:]).then_inc(s_cst.h, 16)
        e.dma_start(ind64s[:], ind64[:]).then_inc(s_cst.h, 16)
        e.dma_start(ind2s[:], ind2[:]).then_inc(s_cst.h, 16)
        e.dma_start(ones1s[:], ones1[:]).then_inc(s_cst.h, 16)
        e.dma_start(onerow_s[:], onerow[:]).then_inc(s_cst.h, 16)
        e.dma_start(rbc[0:1, 0:1], rl[0:1]).then_inc(s_cst.h, 16)
    em("sync", _cst); s_cst.n += 112

    def _btwload(e):
        for i, w in enumerate((bT, cT, aT)):
            e.dma_start(
                btw.ap().rearrange("p (w a b q) -> p w a b q", w=3, a=2, b=2)[:, i],
                w.rearrange("(a p) (b q) -> p a b q", p=128, q=128),
            ).then_inc(s_btw.h, 16)
    em("gpsimd", _btwload); s_btw.n += 48
    def _zs(e):
        e.memset(zsrc[:], 0.0)
        e.memset(sgnc[:], -2147450880)
        e.memset(absc[:], 2147450879)
        e.memset(thsc[:], 0.5 / 64.0)
        e.memset(nege[:], -0.5e-8)
        e.memset(pose[:], 0.5e-8)
        e.memset(qeps[:], 2.9802322387695312e-08).then_inc(cg.h, 1)
    em("gpsimd", _zs); cg.n += 1
    zsrc_cg = cg.n
    def btile(w, kt, mt):
        o = (w * 4 + kt * 2 + mt) * 128
        return btw[:, o:o + 128]

    wait("tensor", s_cst, s_cst.n)
    def _rbc(e):
        e.matmul(pp[:, 7 * 512:7 * 512 + 1], onerow_s[:], rbc[0:1, 0:1],
                 start=True, stop=True).then_inc(ct.h, 1)
    em("tensor", _rbc); ct.n += 1
    wait("vector", ct, ct.n)
    def _rbcev(e):
        e.tensor_copy(rbc[:], pp[:, 7 * 512:7 * 512 + 1]).then_inc(cv.h, 1)
    em("vector", _rbcev); cv.n += 1
    rbc_cv = cv.n
    b7_user = cv.n

    zround = [0] * NZ
    zfree = [None] * NZ

    def z_load(slot, blk_row):
        if zfree[slot] is not None:
            wait("gpsimd", zfree[slot][0], zfree[slot][1])
        def _f(e, sl=slot, r0=blk_row * 128):
            e.dma_start(zring[:, sl * C:(sl + 1) * C], z[r0:r0 + 128, :]).then_inc(s_z[sl].h, 16)
        em("gpsimd", _f)
        s_z[slot].n += 16
        zround[slot] += 1

    def ln_block(zslot, dst, dsttile):
        wait("vector", s_z[zslot], 16 * zround[zslot])
        zv = zring[:, zslot * C:(zslot + 1) * C]
        def _st(e, zv=zv):
            ii = None
            for i in range(4):
                ii = e.bn_stats(stat[:, i * 6:(i + 1) * 6], zv[:, i * 512:(i + 1) * 512])
            ii.then_inc(cv.h, 1)
        em("vector", _st); cv.n += 1
        wait("vector", cv, cv.n)
        def _ag(e):
            e.bn_aggr(aggr[:], stat[:]).then_inc(cv.h, 1)
        em("vector", _ag); cv.n += 1
        wait("vector", cv, cv.n)
        def _ve(e):
            e.tensor_scalar_add(ivar[:, 0:1], aggr[:, 1:2], LN_EPS).then_inc(cv.h, 1)
        em("vector", _ve); cv.n += 1
        wait("scalar", cv, cv.n)
        def _sq(e):
            e.activation(ivar[:, 1:2], ivar[:, 0:1], AF.Sqrt).then_inc(cs.h, 1)
        em("scalar", _sq); cs.n += 1
        wait("vector", cs, cs.n)
        def _rc(e):
            e.reciprocal(ivar[:, 2:3], ivar[:, 1:2]).then_inc(cv.h, 1)
        em("vector", _rc); cv.n += 1
        wait("vector", cv, cv.n)
        def _nm(e):
            e.scalar_tensor_tensor(ivar[:, 3:4], aggr[:, 0:1], -1.0, ivar[:, 2:3],
                                   ALU.mult, ALU.mult).then_inc(cv.h, 1)
        em("vector", _nm); cv.n += 1
        wait("scalar", cv, cv.n)
        def _ln(e, d=dst, i=dsttile, zv=zv):
            e.activation(d[:, i * C:(i + 1) * C], zv, AF.Identity,
                         bias=ivar[:, 3:4], scale=ivar[:, 2:3]).then_inc(cs.h, 1)
        em("scalar", _ln); cs.n += 1
        return cs.n

    tb45 = [None]

    def transpose_block(src, srcoff, dst, dstoff, ln_cs):
        wait("tensor", cs, ln_cs)
        bank_wait("tensor", 2)
        bank_wait("tensor", 3)
        def _tp(e, s=src, o=srcoff):
            ii = None
            for nt in range(16):
                ii = e.transpose(ppb[:, 2 * 1024 + nt * 128: 2 * 1024 + (nt + 1) * 128],
                                 s[:, o + nt * 128: o + (nt + 1) * 128], idbs[:])
            ii.then_inc(ct.h, 1)
        em("tensor", _tp); ct.n += 1
        wait("vector", ct, ct.n)
        def _ev(e, d=dst, o=dstoff):
            e.tensor_copy(d[:, o:o + 2048], ppb[:, 2 * 1024: 2 * 1024 + 2048]).then_inc(cv.h, 1)
        em("vector", _ev); cv.n += 1
        bank_ev[2] = (cv, cv.n)
        bank_ev[3] = (cv, cv.n)
        return cv.n

    qchunks = []
    for dt in range(16):
        for cc in range(2 * H // QW):
            dst = sqg[cc * 8:(cc + 1) * 8, :, dt, :].rearrange("h p w -> p h w")
            qchunks.append((wguT[dt * 128:(dt + 1) * 128, cc * QW:(cc + 1) * QW], dst))
    for jt in range(H // 128):
        qchunks.append((wpT[jt * 128:(jt + 1) * 128, :],
                        sqp[jt * 128:(jt + 1) * 128, :]))
    NQ = len(qchunks)

    qs = {"w_round": [0] * 4, "qs_round": [0] * 4,
          "aw_cv": [None] * 4, "mask_cv": [None] * 4, "aw_user": [None] * 4,
          "m2_cv": [None] * 4, "wq_cg": [None] * 4, "th0": True, "tmp_free": 0,
          "gs_ct": [None] * 4,
          "gs_user": {4: (cv, b7_user), 5: (cv, b7_user), 6: (cv, b7_user), 7: (cv, b7_user)},
          "sp_n": [0] * 4, "bc_n": [0] * 4,
          "gs_pair_ct": {}, "wqin_user": [None] * 6, "w_round4": [0] * 6}

    def q_load(j):
        if j < 0 or j >= NQ:
            return
        w4 = j % 6
        if qs["wqin_user"][w4] is not None:
            wait("gpsimd", cv, qs["wqin_user"][w4])
        if qs["w_round4"][w4] > 0:
            wait("gpsimd", s_w[w4], 16 * qs["w_round4"][w4])
        def _ld(e, w4=w4, src=qchunks[j][0]):
            e.dma_start(wqin[:, w4 * QW:(w4 + 1) * QW], src).then_inc(s_w[w4].h, 16)
        em("gpsimd", _ld)
        s_w[w4].n += 16
        qs["w_round4"][w4] += 1

    def q_aw(j):
        if j < 0 or j >= NQ:
            return
        sl = j % 4
        w4 = j % 6
        wait("vector", s_w[w4], 16 * qs["w_round4"][w4])
        if qs["gs_ct"][sl] is not None:
            wait("vector", ct, qs["gs_ct"][sl])
        if qs["aw_user"][sl] is not None:
            wait("vector", cv, qs["aw_user"][sl])
        def _aw(e, sl=sl, w4=w4):
            e.tensor_scalar(awb.ap().bitcast(I32)[:, sl * QW // 2:(sl + 1) * QW // 2],
                            wqin.ap().bitcast(I32)[:, w4 * QW // 2:(w4 + 1) * QW // 2],
                            absc[:, 0:1], None, ALU.bitwise_and).then_inc(cv.h, 1)
        em("vector", _aw); cv.n += 1
        qs["aw_cv"][sl] = cv.n

    def q_gs(j, pairi):
        if j < 0 or j >= NQ:
            return
        sl = j % 4
        b0 = 4 + 2 * ((j + pairi) % 2)
        wait("tensor", cv, qs["aw_cv"][j % 4])
        for b in (b0, b0 + 1):
            u = qs["gs_user"][b]
            if u is not None:
                wait("tensor", u[0], u[1])
        def _gs(e, sl=sl, pairi=pairi, b0=b0):
            for i in (2 * pairi, 2 * pairi + 1):
                bnk = b0 + i % 2
                e.matmul(pp[0:2, bnk * 512:(bnk + 1) * 512], ind64s[:],
                         awb[:, sl * QW + i * 512: sl * QW + (i + 1) * 512],
                         start=True, stop=True).then_inc(ct.h, 1)
        em("tensor", _gs); ct.n += 2
        qs["gs_ct"][j % 4] = ct.n
        qs["gs_pair_ct"][(j, pairi)] = ct.n

    def q_th(j, pairi):
        if j < 0 or j >= NQ:
            return
        b0 = 4 + 2 * ((j + pairi) % 2)
        t2 = (j % 2) * 2 + pairi
        t2p = ((j + 1) % 2) * 2 + pairi
        if qs["th0"]:
            wait("scalar", cg, zsrc_cg)
            qs["th0"] = False
        wait("scalar", ct, qs["gs_pair_ct"][(j, pairi)])
        wait("scalar", s_tsp[t2p], 16 * qs["sp_n"][t2p])
        wait("scalar", cs, qs["tmp_free"])
        def _t1(e, b0=b0):
            e.activation(tmpth[:], pp[0:2, b0 * 512:(b0 + 2) * 512], AF.Relu,
                         bias=nege[0:2, 0:1], scale=thsc[0:2, 0:1]).then_inc(cs.h, 1)
        em("scalar", _t1); cs.n += 1
        wait("scalar", cs, cs.n)
        def _t2(e, pairi=pairi):
            e.activation(thr2[0:2, pairi * 1024:(pairi + 1) * 1024], tmpth[:], AF.Identity,
                         bias=pose[0:2, 0:1]).then_inc(cs.h, 1)
        em("scalar", _t2); cs.n += 1
        qs["tmp_free"] = cs.n
        qs["gs_user"][b0 + 0] = (cs, cs.n)
        qs["gs_user"][b0 + 1] = (cs, cs.n)
        wait("scalar", cs, cs.n)
        if qs["bc_n"][t2] > 0:
            wait("scalar", s_tbc[t2], 32 * qs["bc_n"][t2])
        def _sp(e, t2=t2, pairi=pairi):
            e.dma_start(thrh[0:2, t2 * 1024:(t2 + 1) * 1024],
                        thr2[0:2, pairi * 1024:(pairi + 1) * 1024]).then_inc(s_tsp[t2].h, 16)
        em("scalar", _sp)
        s_tsp[t2].n += 16
        qs["sp_n"][t2] += 1

    def q_bc(j, pairi):
        if j < 0 or j >= NQ:
            return
        sl = j % 2
        t2 = (j % 2) * 2 + pairi
        wait("sync", s_tsp[t2], 16 * qs["sp_n"][t2])
        if pairi == 0 and qs["wq_cg"][sl] is not None:
            wait("sync", cg, qs["wq_cg"][sl])
        def _bc(e, sl=sl, t2=t2, pairi=pairi):
            for g in range(2):
                e.dma_start(thrs[g * 64:(g + 1) * 64, sl * QW + pairi * 1024: sl * QW + (pairi + 1) * 1024],
                            thrh[g:g + 1, t2 * 1024:(t2 + 1) * 1024].broadcast_to([64, 1024])
                            ).then_inc(s_tbc[t2].h, 16)
        em("sync", _bc)
        s_tbc[t2].n += 32
        qs["bc_n"][t2] += 1

    def q_mask(j):
        if j < 0 or j >= NQ:
            return
        sl = j % 2
        mksl = j % 2
        for pi in (0, 1):
            wait("vector", s_tbc[sl * 2 + pi], 32 * qs["bc_n"][sl * 2 + pi])
        if qs["qs_round"][mksl] > 0:
            wait("vector", s_qs[mksl], 16 * qs["qs_round"][mksl])
        def _mk(e, sl=sl, awsl=j % 4, mksl=mksl):
            e.tensor_tensor(mkb[:, mksl * QW:(mksl + 1) * QW], awb[:, awsl * QW:(awsl + 1) * QW],
                            thrs[:, sl * QW:(sl + 1) * QW], ALU.is_gt).then_inc(cv.h, 1)
        em("vector", _mk); cv.n += 1
        qs["mask_cv"][sl] = cv.n
        qs["aw_user"][j % 4] = cv.n
        wait("vector", cv, cv.n)
        if qs["wq_cg"][sl] is not None:
            wait("vector", cg, qs["wq_cg"][sl])
        def _m2(e, sl=j % 2, w4=j % 6):
            e.scalar_tensor_tensor(m2b.ap().bitcast(I32)[:, sl * QW // 2:(sl + 1) * QW // 2],
                                   wqin.ap().bitcast(I32)[:, w4 * QW // 2:(w4 + 1) * QW // 2],
                                   sgnc[:, 0:1],
                                   mkb.ap().bitcast(I32)[:, sl * QW // 2:(sl + 1) * QW // 2],
                                   ALU.bitwise_and, ALU.bitwise_or).then_inc(cv.h, 1)
        em("vector", _m2); cv.n += 1
        qs["m2_cv"][j % 2] = cv.n
        qs["wqin_user"][j % 6] = cv.n

    def q_wq(j):
        if j < 0 or j >= NQ:
            return
        sl = j % 2
        wait("gpsimd", cv, qs["m2_cv"][j % 2])
        def _wq(e, sl=sl, m2sl=j % 2):
            e.tensor_mul(mkb[:, m2sl * QW:(m2sl + 1) * QW], m2b[:, m2sl * QW:(m2sl + 1) * QW],
                         thrs[:, sl * QW:(sl + 1) * QW]).then_inc(cg.h, 1)
        em("gpsimd", _wq); cg.n += 1
        qs["wq_cg"][sl] = cg.n
        is_gu = j < 128
        qeng = "scalar" if is_gu else "sync"
        wait(qeng, cg, qs["wq_cg"][sl])
        mksl = j % 2
        def _stq(e, mksl=mksl, dst=qchunks[j][1], is_gu=is_gu):
            if is_gu:
                e.dma_start(dst, mkb.ap().rearrange("p (s h w) -> p s h w", s=2, h=8)[:, mksl]
                            ).then_inc(s_qs[mksl].h, 16)
            else:
                e.dma_start(dst, mkb[:, mksl * QW:(mksl + 1) * QW]).then_inc(s_qs[mksl].h, 16)
        em(qeng, _stq)
        s_qs[mksl].n += 16
        qs["qs_round"][mksl] += 1

    def emit_quant(k):
        if k == 0:
            q_load(0)
            q_load(1)
        q_th(k - 2, 0)
        q_th(k - 2, 1)
        q_bc(k - 3, 0)
        q_bc(k - 3, 1)
        q_mask(k - 4)
        q_wq(k - 4)
        q_load(k + 2)
        q_aw(k)
        q_gs(k - 1, 0)
        q_gs(k - 1, 1)


    z_load(0, 0)
    z_load(1, 1)
    for _k in range(4):
        emit_quant(_k)
    ln_cs0 = ln_block(0, xln, 0)
    transpose_block(xln, 0 * C, xlnT, 0 * 2048, ln_cs0)
    ln_cs1 = ln_block(1, xln, 1)
    xT_cv = transpose_block(xln, 1 * C, xlnT, 1 * 2048, ln_cs1)
    xln_cs = cs.n

    wait("tensor", s_btw, s_btw.n)
    wait("tensor", cs, xln_cs)
    def mm_uw(widx, dst):
        for dt in range(2):
            for ncc in range(4):
                b = (dt * 4 + ncc) % 2
                bank_wait("tensor", b)
                def _mm(e, dt=dt, ncc=ncc, b=b, w=widx):
                    ii = None
                    for kt in range(2):
                        ii = e.matmul(bank(b), btile(w, kt, dt),
                                      xln[:, kt * C + ncc * 512: kt * C + (ncc + 1) * 512],
                                      start=(kt == 0), stop=(kt == 1))
                    ii.then_inc(ct.h, 1)
                em("tensor", _mm); ct.n += 1
                wait("scalar", ct, ct.n)
                def _ev(e, dt=dt, ncc=ncc, b=b, d=dst):
                    e.activation(d[:, dt * C + ncc * 512: dt * C + (ncc + 1) * 512],
                                 bank(b), AF.Identity).then_inc(cs.h, 1)
                em("scalar", _ev); cs.n += 1
                bank_ev[b] = (cs, cs.n)
    emit_quant(4)
    emit_quant(5)
    mm_uw(0, uu)
    mm_uw(1, ww)
    emit_quant(6)
    emit_quant(7)
    uw_cs = cs.n

    for nt in range(16):
        for mc in range(4):
            b = (nt * 4 + mc) % 4
            bank_wait("tensor", b)
            def _mm(e, nt=nt, mc=mc, b=b):
                ii = None
                for kt in range(2):
                    ii = e.matmul(bank(b), uu[:, kt * C + nt * 128: kt * C + (nt + 1) * 128],
                                  ww[:, kt * C + mc * 512: kt * C + (mc + 1) * 512],
                                  start=(kt == 0), stop=(kt == 1))
                ii.then_inc(ct.h, 1)
            em("tensor", _mm); ct.n += 1
            wait("scalar", ct, ct.n)
            def _ev(e, nt=nt, mc=mc, b=b):
                e.activation(att[:, nt * C + mc * 512: nt * C + (mc + 1) * 512],
                             bank(b), AF.Relu).then_inc(cs.h, 1)
            em("scalar", _ev); cs.n += 1
            bank_ev[b] = (cs, cs.n)
    att_cs = cs.n
    scores_ct = ct.n
    wait("gpsimd", cs, att_cs)
    wait("gpsimd", cg, zsrc_cg)
    def _dg(e):
        e.dma_start(att[127:128, 15 * C + 2047: 15 * C + 2048], zsrc[0:1, 0:1]).then_inc(s_dg.h, 16)
    em("gpsimd", _dg); s_dg.n += 16

    wait("tensor", cv, xT_cv)
    wait("tensor", s_dg, s_dg.n)
    xatt = uu
    for dt in range(2):
        for mc in range(4):
            b = (dt * 4 + mc) % 2
            bank_wait("tensor", b)
            def _mm(e, dt=dt, mc=mc, b=b):
                ii = None
                for nt in range(16):
                    ii = e.matmul(bank(b), xlnT[:, (dt * 16 + nt) * 128: (dt * 16 + nt + 1) * 128],
                                  att[:, nt * C + mc * 512: nt * C + (mc + 1) * 512],
                                  start=(nt == 0), stop=(nt == 15))
                ii.then_inc(ct.h, 1)
            em("tensor", _mm); ct.n += 1
            wait("scalar", ct, ct.n)
            wait("scalar", ct, scores_ct)
            def _ev(e, dt=dt, mc=mc, b=b):
                e.activation(xatt[:, dt * C + mc * 512: dt * C + (mc + 1) * 512],
                             bank(b), AF.Identity).then_inc(cs.h, 1)
            em("scalar", _ev); cs.n += 1
            bank_ev[b] = (cs, cs.n)
    xatt_cs = cs.n
    for _k in range(8, 12):
        emit_quant(_k)

    wait("gpsimd", cs, xln_cs)
    z_load(0, 0)
    z_load(1, 1)
    x1_round = [0, 0]
    wait("tensor", cs, xatt_cs)
    for it in range(2):
        for mc in range(4):
            b = mc
            bank_wait("tensor", b)
            def _mm(e, it=it, mc=mc, b=b):
                ii = None
                for kd in range(2):
                    ii = e.matmul(bank(b), btile(2, kd, it),
                                  xatt[:, kd * C + mc * 512: kd * C + (mc + 1) * 512],
                                  start=(kd == 0), stop=(kd == 1))
                ii.then_inc(ct.h, 1)
            em("tensor", _mm); ct.n += 1
            bank_ev[b] = None
        ct_x = ct.n
        wait("vector", s_z[it], 16 * zround[it])
        wait("vector", cv, cv.n)
        def _t1(e, it=it):
            e.tensor_add(t1b[:, 0:C], zring[:, it * C:(it + 1) * C],
                         xln[:, it * C:(it + 1) * C]).then_inc(cv.h, 1)
        em("vector", _t1); cv.n += 1
        wait("vector", cv, cv.n)
        wait("vector", ct, ct_x)
        def _x1(e, it=it):
            ii = None
            for mc in range(4):
                ii = e.scalar_tensor_tensor(
                    x1b[:, it * C + mc * 512: it * C + (mc + 1) * 512],
                    bank(mc), 1.0, t1b[:, mc * 512:(mc + 1) * 512],
                    ALU.bypass, ALU.add)
            ii.then_inc(cv.h, 1)
        em("vector", _x1); cv.n += 1
        for mc in range(4):
            bank_ev[mc] = (cv, cv.n)
        zfree[it] = (cv, cv.n)
        wait("sync", cv, cv.n)
        def _sp(e, it=it):
            e.dma_start(x1s[it * 128:(it + 1) * 128, :],
                        x1b[:, it * C:(it + 1) * C]).then_inc(s_x1[it].h, 16)
        em("sync", _sp)
        s_x1[it].n += 16
        x1_round[it] += 1

    lnslot = [None, None]

    def emit_yblock(yi):
        blk = 2 + yi
        zsl = blk % NZ
        sl = yi % 2
        z_load(zsl, blk)
        if lnslot[sl] is not None:
            wait("scalar", cv, lnslot[sl])
        yield
        ln_cs = ln_block(zsl, lnb, sl)
        yield
        lnT_cv = transpose_block(lnb, sl * C, lnT, sl * C, ln_cs)
        wait("tensor", cv, lnT_cv)
        yield
        wait("vector", s_z[zsl], 16 * zround[zsl])
        wait("vector", cs, ln_cs)
        if lnslot[sl] is not None:
            wait("vector", cv, lnslot[sl])
        def _t1(e, sl=sl, zsl=zsl):
            e.tensor_add(t1b[:, 0:C], zring[:, zsl * C:(zsl + 1) * C],
                         lnb[:, sl * C:(sl + 1) * C]).then_inc(cv.h, 1)
        em("vector", _t1); cv.n += 1
        t1cv = cv.n
        zfree[zsl] = (cv, cv.n)
        if x1_round[sl] > 0:
            wait("vector", s_x1[sl], 16 * x1_round[sl])
        for half in range(2):
            for q in range(2):
                bank_wait("tensor", q)
                def _mm(e, sl=sl, half=half, q=q):
                    ii = None
                    for nt in range(16):
                        mc = 2 * half + q
                        ii = e.matmul(bank(q), lnT[:, sl * C + nt * 128: sl * C + (nt + 1) * 128],
                                      att[:, nt * C + mc * 512: nt * C + (mc + 1) * 512],
                                      start=(nt == 0), stop=(nt == 15))
                    ii.then_inc(ct.h, 1)
                em("tensor", _mm); ct.n += 1
                ct_y = ct.n
                yield
                wait("vector", cv, t1cv)
                wait("vector", ct, ct_y)
                def _x1(e, sl=sl, half=half, q=q):
                    mc = 2 * half + q
                    e.scalar_tensor_tensor(
                        x1b[:, sl * C + mc * 512: sl * C + (mc + 1) * 512],
                        bank(q), rbc[:, 0:1], t1b[:, mc * 512:(mc + 1) * 512],
                        ALU.mult, ALU.add).then_inc(cv.h, 1)
                em("vector", _x1); cv.n += 1
                bank_ev[q] = (cv, cv.n)
        lnslot[sl] = cv.n
        wait("sync", cv, cv.n)
        def _sp(e, sl=sl, blk=blk):
            e.dma_start(x1s[blk * 128:(blk + 1) * 128, :],
                        x1b[:, sl * C:(sl + 1) * C]).then_inc(s_x1[sl].h, 16)
        em("sync", _sp)
        s_x1[sl].n += 16
        x1_round[sl] += 1

    NQI = NQ + 4
    qi = 12
    for yi in range(NYB):
        ql = min(NQI, 12 + (yi + 1) * (NQI - 12) // NYB)
        g = emit_yblock(yi)
        for _ in g:
            if qi < ql:
                emit_quant(qi)
                qi += 1
        while qi < ql:
            emit_quant(qi)
            qi += 1
    while qi < NQI:
        emit_quant(qi)
        qi += 1

    for eng in ENGS:
        def _b(e):
            e.drain().then_inc(bar.h, 1)
        em(eng, _b)
    bar.n += 5
    for eng in ENGS:
        wait(eng, bar, bar.n)
    wait("sync", s_x1[0], 16 * x1_round[0])
    wait("sync", s_x1[1], 16 * x1_round[1])
    for _qi in range(2):
        wait("sync", s_qs[_qi], 16 * qs["qs_round"][_qi])
    def _b2(e):
        e.drain().then_inc(bar.h, 1)
    em("sync", _b2); bar.n += 1
    for eng in ("tensor", "vector", "scalar", "gpsimd"):
        wait(eng, bar, bar.n)

    es.close()
    es2 = ExitStack()
    def sb2(name, p, f, dt=F32):
        return es2.enter_context(nc.sbuf_tensor(name, [p, f], dt))
    HRW = 256
    NHR = H // HRW
    hsil = sb2("hsil", 128, 64 * 512, BF16)
    ln2T = sb2("ln2T", 128, 16 * 512, BF16)
    x1c = sb2("x1c", 128, 2 * 4 * C, BF16)
    wgb = sb2("wgb", 128, 4 * 16 * HRW, BF16)
    wpb = sb2("wpb", 128, 4 * 1024, BF16)
    sgl = sb2("sgl", 128, 2 * 512)
    hsq2 = sb2("hsq2", 128, 2 * 512, BF16)
    xo = sb2("xo", 128, 2 * 1024)
    invv = sb2("invv", 1, 512)
    invc = sb2("invc", 128, 4)
    st2 = sb2("st2", 128, 24)
    ag2 = sb2("ag2", 128, 2)
    iv2 = sb2("iv2", 128, 4)
    ln2b = sb2("ln2b", 128, 4 * C, BF16)

    s_xc = [mk("s_xc0"), mk("s_xc1")]
    s_wg = [mk(f"s_wg{i}") for i in range(4)]
    s_wp = [mk(f"s_wp{i}") for i in range(4)]
    s_out = [mk(f"s_out{i}") for i in range(2)]

    fx = {"xc_cons": [None, None], "wg_cons": [None] * 4, "wp_cons": [None] * 4,
          "hs_cv": [None] * 64, "sg_free": [None, None], "hq_ct": [None, None],
          "hq_cv": [None, None], "xo_user": [None] * 8, "ss_ct": None}

    def xc_load(ch):
        sl = ch % 2
        if fx["xc_cons"][sl] is not None:
            wait("sync", cv, fx["xc_cons"][sl])
        def _f(e, sl=sl, ch=ch):
            e.dma_start(x1c.ap().rearrange("p (s b c) -> p s b c", s=2, b=4)[:, sl],
                        x1s.rearrange("(b p) c -> p b c", p=128)[:, ch * 4:(ch + 1) * 4]
                        ).then_inc(s_xc[sl].h, 16)
        em("sync", _f)
        s_xc[sl].n += 16

    def wg_load(hrg):
        for half, hr in ((0, hrg), (1, hrg + NHR)):
            sl = (hrg % 2) * 2 + half
            if fx["wg_cons"][sl] is not None:
                wait("sync", ct, fx["wg_cons"][sl])
            def _f(e, sl=sl, hr=hr):
                e.dma_start(wgb.ap().rearrange("p (s d h) -> p s d h", s=4, d=16)[:, sl],
                            sqg[hr]).then_inc(s_wg[sl].h, 16)
            em("sync", _f)
            s_wg[sl].n += 16

    def wp_load(jt, dchalf):
        sl = jt % 4
        if fx["wp_cons"][sl] is not None:
            wait("scalar", ct, fx["wp_cons"][sl])
        def _f(e, sl=sl, jt=jt, dchalf=dchalf):
            e.dma_start(wpb[:, sl * 1024:(sl + 1) * 1024],
                        sqp[jt * 128:(jt + 1) * 128, dchalf * 1024:(dchalf + 1) * 1024]
                        ).then_inc(s_wp[sl].h, 16)
        em("scalar", _f)
        s_wp[sl].n += 16

    ln_cs_of = [None] * 4
    tp_ct_of = [None] * 4

    def ln2_chain(ch, tb):
        xsl = ch % 2
        xv = x1c.ap().rearrange("p (s b c) -> p s b c", s=2, b=4)[:, xsl, tb]
        wait("vector", s_xc[xsl], 16 * (ch // 2 + 1))
        def _st(e, xv=xv):
            ii = None
            for i in range(4):
                ii = e.bn_stats(st2[:, i * 6:(i + 1) * 6], xv[:, i * 512:(i + 1) * 512])
            ii.then_inc(cv.h, 1)
        em("vector", _st); cv.n += 1
        wait("vector", cv, cv.n)
        def _ag(e):
            e.bn_aggr(ag2[:], st2[:]).then_inc(cv.h, 1)
        em("vector", _ag); cv.n += 1
        wait("vector", cv, cv.n)
        def _ve(e):
            e.tensor_scalar(iv2[:, 0:1], ag2[:, 1:2], LN_EPS, 0.25,
                            ALU.add, ALU.mult).then_inc(cv.h, 1)
        em("vector", _ve); cv.n += 1
        wait("scalar", cv, cv.n)
        def _sq(e):
            e.activation(iv2[:, 1:2], iv2[:, 0:1], AF.Sqrt).then_inc(cs.h, 1)
        em("scalar", _sq); cs.n += 1
        wait("vector", cs, cs.n)
        def _rc(e):
            e.reciprocal(iv2[:, 2:3], iv2[:, 1:2]).then_inc(cv.h, 1)
        em("vector", _rc); cv.n += 1
        wait("vector", cv, cv.n)
        def _nm(e):
            e.scalar_tensor_tensor(iv2[:, 3:4], ag2[:, 0:1], -1.0, iv2[:, 2:3],
                                   ALU.mult, ALU.mult).then_inc(cv.h, 1)
        em("vector", _nm); cv.n += 1
        wait("scalar", cv, cv.n)
        if tp_ct_of[tb] is not None:
            wait("scalar", ct, tp_ct_of[tb])
        def _ln(e, xv=xv, tb=tb):
            e.activation(ln2b[:, tb * C:(tb + 1) * C], xv, AF.Identity, bias=iv2[:, 3:4],
                         scale=iv2[:, 2:3]).then_inc(cs.h, 1)
        em("scalar", _ln); cs.n += 1
        ln_cs_of[tb] = cs.n

    def ln2_tp(tb):
        wait("tensor", cs, ln_cs_of[tb])
        if tb45[0] is not None:
            wait("tensor", tb45[0][0], tb45[0][1])
        def _tp(e, tb=tb):
            ii = None
            for dt in range(16):
                ii = e.transpose(ppb[:, 4 * 1024 + dt * 128: 4 * 1024 + (dt + 1) * 128],
                                 ln2b[:, tb * C + dt * 128: tb * C + (dt + 1) * 128], idbs[:])
            ii.then_inc(ct.h, 1)
        em("tensor", _tp); ct.n += 1
        tp_ct_of[tb] = ct.n
        wait("vector", ct, ct.n)
        def _ev(e, tb=tb):
            e.tensor_copy(ln2T.ap().rearrange("p (d t) -> p d t", d=16)[:, :, tb * 128:(tb + 1) * 128],
                          ppb[:, 4 * 1024: 4 * 1024 + 2048].rearrange("p (d t) -> p d t", d=16)
                          ).then_inc(cv.h, 1)
        em("vector", _ev); cv.n += 1
        tb45[0] = (cv, cv.n)

    xc_load(0)
    for tb in range(4):
        ln2_chain(0, tb)
    for ch in range(NCH):
        sl = ch % 2
        if ch + 1 < NCH:
            xc_load(ch + 1)
        for tb in range(4):
            ln2_tp(tb)
        ln2T_cv = cv.n

        pend_ss = [None]
        wait("tensor", cv, ln2T_cv)
        wg_load(0)
        for hrg in range(NHR):
            if hrg + 1 < NHR:
                wg_load(hrg + 1)
            if ch + 1 < NCH and hrg in (6, 12, 18, 24):
                ln2_chain(ch + 1, (hrg - 6) // 6)
            slg = (hrg % 2) * 2
            slu = slg + 1
            wait("tensor", s_wg[slg], s_wg[slg].n)
            wait("tensor", s_wg[slu], s_wg[slu].n)
            for hi in range(HRW // 128):
                jt = hrg * (HRW // 128) + hi
                gb = jt % 2
                if fx["sg_free"][gb] is not None:
                    wait("tensor", cs, fx["sg_free"][gb])
                if fx["hq_cv"][gb] is not None:
                    wait("tensor", cv, fx["hq_cv"][gb])
                def _mmg(e, slg=slg, hi=hi, gb=gb):
                    w = wgb.ap().rearrange("p (s d h) -> p s d h", s=4, d=16)[:, slg]
                    ii = None
                    for dt in range(16):
                        ii = e.matmul(bank(gb), w[:, dt, hi * 128:(hi + 1) * 128],
                                      ln2T[:, dt * 512:(dt + 1) * 512],
                                      start=(dt == 0), stop=(dt == 15))
                    ii.then_inc(ct.h, 1)
                em("tensor", _mmg); ct.n += 1
                def _mmu(e, slu=slu, hi=hi, gb=gb):
                    w = wgb.ap().rearrange("p (s d h) -> p s d h", s=4, d=16)[:, slu]
                    ii = None
                    for dt in range(16):
                        ii = e.matmul(bank(2 + gb), w[:, dt, hi * 128:(hi + 1) * 128],
                                      ln2T[:, dt * 512:(dt + 1) * 512],
                                      start=(dt == 0), stop=(dt == 15))
                    ii.then_inc(ct.h, 1)
                em("tensor", _mmu); ct.n += 1
                ct_pair = ct.n
                if hi == HRW // 128 - 1:
                    fx["wg_cons"][slg] = ct.n
                    fx["wg_cons"][slu] = ct.n
                wait("scalar", ct, ct_pair - 1)
                if fx["hq_cv"][gb] is not None:
                    wait("scalar", cv, fx["hq_cv"][gb])
                def _si(e, gb=gb):
                    e.activation(sgl[:, gb * 512:(gb + 1) * 512], bank(gb), AF.Silu
                                 ).then_inc(cs.h, 1)
                em("scalar", _si); cs.n += 1
                sg_cs = cs.n
                fx["sg_free"][gb] = cs.n
                wait("vector", cs, sg_cs)
                wait("vector", ct, ct_pair)
                def _hs(e, jt=jt, gb=gb):
                    e.tensor_mul(hsil[:, jt * 512:(jt + 1) * 512], sgl[:, gb * 512:(gb + 1) * 512],
                                 bank(2 + gb)).then_inc(cv.h, 1)
                em("vector", _hs); cv.n += 1
                hs_cv = cv.n
                fx["hs_cv"][jt] = hs_cv
                wait("vector", cv, hs_cv)
                if fx["hq_ct"][gb] is not None:
                    wait("vector", ct, fx["hq_ct"][gb])
                def _hq(e, jt=jt, gb=gb):
                    e.tensor_mul(hsq2[:, gb * 512:(gb + 1) * 512], hsil[:, jt * 512:(jt + 1) * 512],
                                 hsil[:, jt * 512:(jt + 1) * 512]).then_inc(cv.h, 1)
                em("vector", _hq); cv.n += 1
                fx["hq_cv"][gb] = cv.n
                def emit_ss(jt=jt, gb=gb, hqv=cv.n):
                    wait("tensor", cv, hqv)
                    def _ss(e, jt=jt, gb=gb):
                        e.matmul(pp[0:1, 6 * 512:7 * 512], ones1s[:], hsq2[:, gb * 512:(gb + 1) * 512],
                                 start=(jt == 0), stop=(jt == 63)).then_inc(ct.h, 1)
                    em("tensor", _ss); ct.n += 1
                    fx["hq_ct"][gb] = ct.n
                if pend_ss[0] is not None:
                    pend_ss[0]()
                pend_ss[0] = emit_ss
        if pend_ss[0] is not None:
            pend_ss[0]()
            pend_ss[0] = None
        gu_ct = ct.n

        wait("vector", ct, gu_ct)
        def _ms(e):
            e.tensor_scalar(invv[0:1, :], pp[0:1, 6 * 512:7 * 512], 0.25 / 8192.0,
                            2.9802322387695312e-08, ALU.mult, ALU.add).then_inc(cv.h, 1)
        em("vector", _ms); cv.n += 1
        wait("scalar", cv, cv.n)
        def _sq2(e):
            e.activation(invv[0:1, :], invv[0:1, :], AF.Sqrt).then_inc(cs.h, 1)
        em("scalar", _sq2); cs.n += 1
        wait("vector", cs, cs.n)
        def _rc2(e):
            e.reciprocal(invv[0:1, :], invv[0:1, :]).then_inc(cv.h, 1)
        em("vector", _rc2); cv.n += 1
        inv_cv = cv.n
        wait("tensor", cv, inv_cv)
        def _ti(e):
            ii = None
            for tb in range(4):
                ii = e.transpose(pp[:, 7 * 512 + tb: 7 * 512 + tb + 1],
                                 invv[0:1, tb * 128:(tb + 1) * 128], idfs[0:1, 0:1])
            ii.then_inc(ct.h, 1)
        em("tensor", _ti); ct.n += 1
        wait("vector", ct, ct.n)
        def _ic(e):
            e.tensor_copy(invc[:], pp[:, 7 * 512: 7 * 512 + 4]).then_inc(cv.h, 1)
        em("vector", _ic); cv.n += 1
        invc_cv = cv.n

        for dchalf in range(2):
            wait("tensor", cv, invc_cv)
            if dchalf == 1:
                for u in range(8):
                    if fx["xo_user"][u] is not None:
                        wait("tensor", cv, fx["xo_user"][u])
            wp_load(0, dchalf)
            wp_load(1, dchalf)
            for jt in range(64):
                if jt + 2 < 64:
                    wp_load(jt + 2, dchalf)
                wsl = jt % 4
                wait("tensor", s_wp[wsl], s_wp[wsl].n)
                wait("tensor", cv, fx["hs_cv"][jt])
                def _mm(e, jt=jt, wsl=wsl):
                    ii = None
                    for tb in range(4):
                        for dc in range(2):
                            ii = e.matmul(bank(tb * 2 + dc),
                                          hsil[:, jt * 512 + tb * 128: jt * 512 + (tb + 1) * 128],
                                          wpb[:, wsl * 1024 + dc * 512: wsl * 1024 + (dc + 1) * 512],
                                          start=(jt == 0), stop=(jt == 63))
                    ii.then_inc(ct.h, 1)
                em("tensor", _mm); ct.n += 1
                fx["wp_cons"][wsl] = ct.n
            mm2_ct = ct.n
            wait("vector", ct, mm2_ct)
            for tb in range(4):
                xsl = tb % 2
                if s_out[xsl].n > 0:
                    wait("vector", s_out[xsl], s_out[xsl].n)
                def _xo(e, tb=tb, sl=sl, dchalf=dchalf, xsl=xsl):
                    xv = x1c.ap().rearrange("p (s b c) -> p s b c", s=2, b=4)[:, sl, tb]
                    ii = None
                    for dc in range(2):
                        d0 = dchalf * 1024 + dc * 512
                        ii = e.scalar_tensor_tensor(
                            xo[:, xsl * 1024 + dc * 512: xsl * 1024 + (dc + 1) * 512],
                            bank(tb * 2 + dc), invc[:, tb:tb + 1], xv[:, d0:d0 + 512],
                            ALU.mult, ALU.add)
                    ii.then_inc(cv.h, 1)
                em("vector", _xo); cv.n += 1
                fx["xo_user"][dchalf * 4 + tb] = cv.n
                wait("gpsimd", cv, cv.n)
                def _ow(e, tb=tb, ch=ch, dchalf=dchalf, xsl=xsl):
                    e.dma_start(out[(ch * 4 + tb) * 128:(ch * 4 + tb + 1) * 128,
                                    dchalf * 1024:(dchalf + 1) * 1024],
                                xo[:, xsl * 1024:(xsl + 1) * 1024]).then_inc(s_out[xsl].h, 16)
                em("gpsimd", _ow)
                s_out[xsl].n += 16
        fx["xc_cons"][sl] = cv.n
        tb45[0] = (cv, cv.n)

    for q in range(2):
        wait("gpsimd", s_out[q], s_out[q].n)

    blk = es2.enter_context(nc.Block())
    for name, attr in (("sync", blk.sync), ("tensor", blk.tensor), ("vector", blk.vector),
                       ("scalar", blk.scalar), ("gpsimd", blk.gpsimd)):
        def _run(e, ops=streams[name]):
            for op in ops:
                op(e)
        attr(_run)
    es2.close()
    keep.close()
    sems.close()
    return nc


def kernel(L0_attn_Z_in, L0_attn_A_l, L0_attn_r_l, L0_attn_B_l, L0_attn_C_l,
           L0_ffn_w_gate_up, L0_ffn_w_proj):
    bf = ml_dtypes.bfloat16
    Z = np.asarray(L0_attn_Z_in, dtype=np.float32)
    A = np.ascontiguousarray(np.asarray(L0_attn_A_l, dtype=np.float32).T)
    B = np.ascontiguousarray(np.asarray(L0_attn_B_l, dtype=np.float32).T)
    Cm = np.ascontiguousarray(np.asarray(L0_attn_C_l, dtype=np.float32).T)
    r = np.asarray(L0_attn_r_l, dtype=np.float32).reshape(1)
    WguT = np.ascontiguousarray(np.asarray(L0_ffn_w_gate_up, dtype=np.float32).T)
    WpT = np.ascontiguousarray(np.asarray(L0_ffn_w_proj, dtype=np.float32).T)

    idf = np.eye(128, dtype=np.float32)
    idb = np.eye(128, dtype=np.float32).astype(bf)
    ind64 = np.zeros((128, 2), dtype=np.float32)
    ind64[0:64, 0] = 1.0
    ind64[64:128, 1] = 1.0
    ind2 = np.ascontiguousarray(ind64.T).astype(bf)
    ind64 = ind64.astype(bf)
    ones1 = np.ones((128, 1), dtype=np.float32).astype(bf)
    onerow = np.ones((1, 128), dtype=np.float32)

    nc = build_program(R=Z.shape[1])
    common = dict(bT=B, cT=Cm, aT=A, rl=r, wguT=WguT, wpT=WpT, idf=idf, idb=idb,
                  ind64=ind64, ind2=ind2, ones1=ones1, onerow=onerow)
    in_maps = [dict(common, z=np.ascontiguousarray(Z[b])) for b in range(Z.shape[0])]
    res = run_bass_kernel_spmd(nc, in_maps, core_ids=list(range(8)))
    return np.stack([np.asarray(res.results[b]["out"]) for b in range(Z.shape[0])], axis=0)


# version 39
# speedup vs baseline: 1.1106x; 1.0717x over previous
import numpy as np
import ml_dtypes
from contextlib import ExitStack

import concourse.bass as bass
import concourse.mybir as mybir
from concourse.bass_utils import run_bass_kernel_spmd

F32 = mybir.dt.float32
BF16 = mybir.dt.bfloat16
I16 = mybir.dt.int16
I32 = mybir.dt.int32
AF = mybir.ActivationFunctionType
ALU = mybir.AluOpType

C = 2048
D = 256
H = 8192
RMS_EPS = 1.1920928955078125e-07
LN_EPS = 1e-5
QW = 2048


class Ctr:
    def __init__(self, es, nc, name):
        self.h = es.enter_context(nc.semaphore(name))
        self.n = 0


def build_program(R=4096):
    NB = R // 128
    NYB = NB - 2
    NCH = R // 512
    nc = bass.Bass("TRN2", target_bir_lowering=False)

    z = nc.dram_tensor("z", [R, C], F32, kind="ExternalInput").ap()
    bT = nc.dram_tensor("bT", [D, D], F32, kind="ExternalInput").ap()
    cT = nc.dram_tensor("cT", [D, D], F32, kind="ExternalInput").ap()
    aT = nc.dram_tensor("aT", [D, D], F32, kind="ExternalInput").ap()
    rl = nc.dram_tensor("rl", [1], F32, kind="ExternalInput").ap()
    wguT = nc.dram_tensor("wguT", [C, 2 * H], F32, kind="ExternalInput").ap()
    wpT = nc.dram_tensor("wpT", [H, C], F32, kind="ExternalInput").ap()
    idf = nc.dram_tensor("idf", [128, 128], F32, kind="ExternalInput").ap()
    idb = nc.dram_tensor("idb", [128, 128], BF16, kind="ExternalInput").ap()
    ind64 = nc.dram_tensor("ind64", [128, 2], BF16, kind="ExternalInput").ap()
    ind2 = nc.dram_tensor("ind2", [2, 128], BF16, kind="ExternalInput").ap()
    ones1 = nc.dram_tensor("ones1", [128, 1], BF16, kind="ExternalInput").ap()
    onerow = nc.dram_tensor("onerow", [1, 128], F32, kind="ExternalInput").ap()
    out = nc.dram_tensor("out", [R, C], F32, kind="ExternalOutput").ap()
    sqg = nc.dram_tensor("sqg", [2 * H // 256, 128, 16, 256], BF16, kind="Internal").ap()
    sqp = nc.dram_tensor("sqp", [H, C], BF16, kind="Internal").ap()
    x1s = nc.dram_tensor("x1s", [R, C], BF16, kind="Internal").ap()
    thrh = nc.dram_tensor("thrh", [2, 4 * 1024], BF16, kind="Internal").ap()

    keep = ExitStack()
    def sbk(name, p, f, dt=F32):
        return keep.enter_context(nc.sbuf_tensor(name, [p, f], dt))

    pp = keep.enter_context(nc.psum_tensor("pp", [128, 4096], F32))
    ppb = pp.ap().bitcast(BF16)
    def bank(i):
        return pp[:, i * 512:(i + 1) * 512]

    idfs = sbk("idfs", 128, 128)
    idbs = sbk("idbs", 128, 128, BF16)
    ind64s = sbk("ind64s", 128, 2, BF16)
    ind2s = sbk("ind2s", 2, 128, BF16)
    ones1s = sbk("ones1s", 128, 1, BF16)
    onerow_s = sbk("onerow_s", 1, 128)
    rbc = sbk("rbc", 128, 1)
    zsrc = sbk("zsrc", 1, 8, BF16)
    sgnc = sbk("sgnc", 128, 1, I32)
    absc = sbk("absc", 128, 1, I32)
    qeps = sbk("qeps", 1, 1)
    thsc = sbk("thsc", 128, 1)
    zeroc = sbk("zeroc", 128, 1, I32)
    nege = sbk("nege", 128, 1)
    pose = sbk("pose", 128, 1)

    es = ExitStack()
    def sb(name, p, f, dt=F32):
        return es.enter_context(nc.sbuf_tensor(name, [p, f], dt))

    att = sb("att", 128, 16 * C, BF16)
    xln = sb("xln", 128, 2 * C, BF16)
    xlnT = sb("xlnT", 128, 2 * 16 * 128, BF16)
    uu = sb("uu", 128, 2 * C, BF16)
    ww = sb("ww", 128, 2 * C, BF16)
    NZ = 2
    zring = sb("zring", 128, NZ * C, BF16)
    lnb = sb("lnb", 128, 2 * C, BF16)
    lnT = sb("lnT", 128, 2 * C, BF16)
    x1b = sb("x1b", 128, 2 * C, BF16)
    t1b = sb("t1b", 128, C, BF16)
    btw = sb("btw", 128, 3 * 2 * 2 * 128, BF16)
    stat = sb("stat", 128, 24)
    aggr = sb("aggr", 128, 2)
    ivar = sb("ivar", 128, 4)
    QSL = 4
    wqin = sb("wqin", 128, 6 * QW, BF16)
    awb = sb("awb", 128, 4 * QW, BF16)
    mkb = sb("mkb", 128, 2 * QW, BF16)
    m2b = sb("m2b", 128, 2 * QW, BF16)
    thrs = sb("thrs", 128, 2 * QW, BF16)
    thr2 = sb("thr2", 2, 2 * 1024, BF16)
    tmpth = sb("tmpth", 2, 1024, BF16)

    sems = ExitStack()
    def mk(name):
        return Ctr(sems, nc, name)
    s_z = [mk(f"s_z{i}") for i in range(NZ)]
    s_cst = mk("s_cst")
    s_btw = mk("s_btw")
    s_w = [mk(f"s_w{i}") for i in range(6)]
    s_qs = [mk(f"s_qs{i}") for i in range(2)]
    s_tsp = [mk(f"s_tsp{i}") for i in range(4)]
    s_tbc = [mk(f"s_tbc{i}") for i in range(4)]
    s_x1 = [mk(f"s_x1{i}") for i in range(2)]
    s_dg = mk("s_dg")
    cv = mk("cv"); cs = mk("cs"); ct = mk("ct"); cg = mk("cg")
    bar = mk("bar")

    ENGS = ("sync", "tensor", "vector", "scalar", "gpsimd")
    streams = {k: [] for k in ENGS}
    def em(eng, fn):
        streams[eng].append(fn)
    def wait(eng, ctr, val):
        if val is None or val <= 0:
            return
        streams[eng].append(lambda e, h=ctr.h, v=val: e.wait_ge(h, v))

    bank_ev = {}
    def bank_wait(eng, b):
        ev = bank_ev.get(b)
        if ev is not None:
            wait(eng, ev[0], ev[1])

    def _cst(e):
        e.dma_start(idfs[:], idf[:]).then_inc(s_cst.h, 16)
        e.dma_start(idbs[:], idb[:]).then_inc(s_cst.h, 16)
        e.dma_start(ind64s[:], ind64[:]).then_inc(s_cst.h, 16)
        e.dma_start(ind2s[:], ind2[:]).then_inc(s_cst.h, 16)
        e.dma_start(ones1s[:], ones1[:]).then_inc(s_cst.h, 16)
        e.dma_start(onerow_s[:], onerow[:]).then_inc(s_cst.h, 16)
        e.dma_start(rbc[0:1, 0:1], rl[0:1]).then_inc(s_cst.h, 16)
    em("sync", _cst); s_cst.n += 112

    def _btwload(e):
        for i, w in enumerate((bT, cT, aT)):
            e.dma_start(
                btw.ap().rearrange("p (w a b q) -> p w a b q", w=3, a=2, b=2)[:, i],
                w.rearrange("(a p) (b q) -> p a b q", p=128, q=128),
            ).then_inc(s_btw.h, 16)
    em("gpsimd", _btwload); s_btw.n += 48
    def _zs(e):
        e.memset(zsrc[:], 0.0)
        e.memset(sgnc[:], -2147450880)
        e.memset(absc[:], 2147450879)
        e.memset(thsc[:], 0.5 / 64.0)
        e.memset(zeroc[:], 0)
        e.memset(nege[:], -0.5e-8)
        e.memset(pose[:], 0.5e-8)
        e.memset(qeps[:], 2.9802322387695312e-08).then_inc(cg.h, 1)
    em("gpsimd", _zs); cg.n += 1
    zsrc_cg = cg.n
    def btile(w, kt, mt):
        o = (w * 4 + kt * 2 + mt) * 128
        return btw[:, o:o + 128]

    wait("tensor", s_cst, s_cst.n)
    def _rbc(e):
        e.matmul(pp[:, 7 * 512:7 * 512 + 1], onerow_s[:], rbc[0:1, 0:1],
                 start=True, stop=True).then_inc(ct.h, 1)
    em("tensor", _rbc); ct.n += 1
    wait("vector", ct, ct.n)
    def _rbcev(e):
        e.tensor_copy(rbc[:], pp[:, 7 * 512:7 * 512 + 1]).then_inc(cv.h, 1)
    em("vector", _rbcev); cv.n += 1
    rbc_cv = cv.n
    b7_user = cv.n

    zround = [0] * NZ
    zfree = [None] * NZ

    def z_load(slot, blk_row):
        if zfree[slot] is not None:
            wait("gpsimd", zfree[slot][0], zfree[slot][1])
        def _f(e, sl=slot, r0=blk_row * 128):
            e.dma_start(zring[:, sl * C:(sl + 1) * C], z[r0:r0 + 128, :]).then_inc(s_z[sl].h, 16)
        em("gpsimd", _f)
        s_z[slot].n += 16
        zround[slot] += 1

    def ln_block(zslot, dst, dsttile):
        wait("vector", s_z[zslot], 16 * zround[zslot])
        zv = zring[:, zslot * C:(zslot + 1) * C]
        def _st(e, zv=zv):
            ii = None
            for i in range(4):
                ii = e.bn_stats(stat[:, i * 6:(i + 1) * 6], zv[:, i * 512:(i + 1) * 512])
            ii.then_inc(cv.h, 1)
        em("vector", _st); cv.n += 1
        wait("vector", cv, cv.n)
        def _ag(e):
            e.bn_aggr(aggr[:], stat[:]).then_inc(cv.h, 1)
        em("vector", _ag); cv.n += 1
        wait("vector", cv, cv.n)
        def _ve(e):
            e.tensor_scalar_add(ivar[:, 0:1], aggr[:, 1:2], LN_EPS).then_inc(cv.h, 1)
        em("vector", _ve); cv.n += 1
        wait("scalar", cv, cv.n)
        def _sq(e):
            e.activation(ivar[:, 1:2], ivar[:, 0:1], AF.Sqrt).then_inc(cs.h, 1)
        em("scalar", _sq); cs.n += 1
        wait("vector", cs, cs.n)
        def _rc(e):
            e.reciprocal(ivar[:, 2:3], ivar[:, 1:2]).then_inc(cv.h, 1)
        em("vector", _rc); cv.n += 1
        wait("vector", cv, cv.n)
        def _nm(e):
            e.scalar_tensor_tensor(ivar[:, 3:4], aggr[:, 0:1], -1.0, ivar[:, 2:3],
                                   ALU.mult, ALU.mult).then_inc(cv.h, 1)
        em("vector", _nm); cv.n += 1
        wait("scalar", cv, cv.n)
        def _ln(e, d=dst, i=dsttile, zv=zv):
            e.activation(d[:, i * C:(i + 1) * C], zv, AF.Identity,
                         bias=ivar[:, 3:4], scale=ivar[:, 2:3]).then_inc(cs.h, 1)
        em("scalar", _ln); cs.n += 1
        return cs.n

    tb45 = [None]

    def transpose_block(src, srcoff, dst, dstoff, ln_cs):
        wait("tensor", cs, ln_cs)
        bank_wait("tensor", 2)
        bank_wait("tensor", 3)
        def _tp(e, s=src, o=srcoff):
            ii = None
            for nt in range(16):
                ii = e.transpose(ppb[:, 2 * 1024 + nt * 128: 2 * 1024 + (nt + 1) * 128],
                                 s[:, o + nt * 128: o + (nt + 1) * 128], idbs[:])
            ii.then_inc(ct.h, 1)
        em("tensor", _tp); ct.n += 1
        wait("vector", ct, ct.n)
        def _ev(e, d=dst, o=dstoff):
            e.tensor_copy(d[:, o:o + 2048], ppb[:, 2 * 1024: 2 * 1024 + 2048]).then_inc(cv.h, 1)
        em("vector", _ev); cv.n += 1
        bank_ev[2] = (cv, cv.n)
        bank_ev[3] = (cv, cv.n)
        return cv.n

    qchunks = []
    for dt in range(16):
        for cc in range(2 * H // QW):
            dst = sqg[cc * 8:(cc + 1) * 8, :, dt, :].rearrange("h p w -> p h w")
            qchunks.append((wguT[dt * 128:(dt + 1) * 128, cc * QW:(cc + 1) * QW], dst))
    for jt in range(H // 128):
        qchunks.append((wpT[jt * 128:(jt + 1) * 128, :],
                        sqp[jt * 128:(jt + 1) * 128, :]))
    NQ = len(qchunks)

    qs = {"w_round": [0] * 4, "qs_round": [0] * 4,
          "aw_cv": [None] * 4, "mask_cv": [None] * 4, "aw_user": [None] * 4,
          "m2_cv": [None] * 4, "wq_cg": [None] * 4, "th0": True, "tmp_free": 0,
          "gs_ct": [None] * 4,
          "gs_user": {4: (cv, b7_user), 5: (cv, b7_user), 6: (cv, b7_user), 7: (cv, b7_user)},
          "sp_n": [0] * 4, "bc_n": [0] * 4,
          "gs_pair_ct": {}, "wqin_user": [None] * 6, "w_round4": [0] * 6}

    def q_load(j):
        if j < 0 or j >= NQ:
            return
        w4 = j % 6
        if qs["wqin_user"][w4] is not None:
            wait("gpsimd", cv, qs["wqin_user"][w4])
        if qs["w_round4"][w4] > 0:
            wait("gpsimd", s_w[w4], 16 * qs["w_round4"][w4])
        def _ld(e, w4=w4, src=qchunks[j][0]):
            e.dma_start(wqin[:, w4 * QW:(w4 + 1) * QW], src).then_inc(s_w[w4].h, 16)
        em("gpsimd", _ld)
        s_w[w4].n += 16
        qs["w_round4"][w4] += 1

    def q_aw(j):
        if j < 0 or j >= NQ:
            return
        sl = j % 4
        w4 = j % 6
        wait("vector", s_w[w4], 16 * qs["w_round4"][w4])
        if qs["gs_ct"][sl] is not None:
            wait("vector", ct, qs["gs_ct"][sl])
        if qs["aw_user"][sl] is not None:
            wait("vector", cv, qs["aw_user"][sl])
        def _aw(e, sl=sl, w4=w4):
            e.scalar_tensor_tensor(awb.ap().bitcast(I32)[:, sl * QW // 2:(sl + 1) * QW // 2],
                                   wqin.ap().bitcast(I32)[:, w4 * QW // 2:(w4 + 1) * QW // 2],
                                   absc[:, 0:1],
                                   zeroc[:, 0:1].broadcast_to([128, QW // 2]),
                                   ALU.bitwise_and, ALU.bitwise_or).then_inc(cv.h, 1)
        em("vector", _aw); cv.n += 1
        qs["aw_cv"][sl] = cv.n

    def q_gs(j, pairi):
        if j < 0 or j >= NQ:
            return
        sl = j % 4
        b0 = 4 + 2 * ((j + pairi) % 2)
        wait("tensor", cv, qs["aw_cv"][j % 4])
        for b in (b0, b0 + 1):
            u = qs["gs_user"][b]
            if u is not None:
                wait("tensor", u[0], u[1])
        def _gs(e, sl=sl, pairi=pairi, b0=b0):
            for i in (2 * pairi, 2 * pairi + 1):
                bnk = b0 + i % 2
                e.matmul(pp[0:2, bnk * 512:(bnk + 1) * 512], ind64s[:],
                         awb[:, sl * QW + i * 512: sl * QW + (i + 1) * 512],
                         start=True, stop=True).then_inc(ct.h, 1)
        em("tensor", _gs); ct.n += 2
        qs["gs_ct"][j % 4] = ct.n
        qs["gs_pair_ct"][(j, pairi)] = ct.n

    def q_th(j, pairi):
        if j < 0 or j >= NQ:
            return
        b0 = 4 + 2 * ((j + pairi) % 2)
        t2 = (j % 2) * 2 + pairi
        t2p = ((j + 1) % 2) * 2 + pairi
        if qs["th0"]:
            wait("scalar", cg, zsrc_cg)
            qs["th0"] = False
        wait("scalar", ct, qs["gs_pair_ct"][(j, pairi)])
        wait("scalar", s_tsp[t2p], 16 * qs["sp_n"][t2p])
        wait("scalar", cs, qs["tmp_free"])
        def _t1(e, b0=b0):
            e.activation(tmpth[:], pp[0:2, b0 * 512:(b0 + 2) * 512], AF.Relu,
                         bias=nege[0:2, 0:1], scale=thsc[0:2, 0:1]).then_inc(cs.h, 1)
        em("scalar", _t1); cs.n += 1
        wait("scalar", cs, cs.n)
        def _t2(e, pairi=pairi):
            e.activation(thr2[0:2, pairi * 1024:(pairi + 1) * 1024], tmpth[:], AF.Identity,
                         bias=pose[0:2, 0:1]).then_inc(cs.h, 1)
        em("scalar", _t2); cs.n += 1
        qs["tmp_free"] = cs.n
        qs["gs_user"][b0 + 0] = (cs, cs.n)
        qs["gs_user"][b0 + 1] = (cs, cs.n)
        wait("scalar", cs, cs.n)
        if qs["bc_n"][t2] > 0:
            wait("scalar", s_tbc[t2], 32 * qs["bc_n"][t2])
        def _sp(e, t2=t2, pairi=pairi):
            e.dma_start(thrh[0:2, t2 * 1024:(t2 + 1) * 1024],
                        thr2[0:2, pairi * 1024:(pairi + 1) * 1024]).then_inc(s_tsp[t2].h, 16)
        em("scalar", _sp)
        s_tsp[t2].n += 16
        qs["sp_n"][t2] += 1

    def q_bc(j, pairi):
        if j < 0 or j >= NQ:
            return
        sl = j % 2
        t2 = (j % 2) * 2 + pairi
        wait("sync", s_tsp[t2], 16 * qs["sp_n"][t2])
        if pairi == 0 and qs["wq_cg"][sl] is not None:
            wait("sync", cg, qs["wq_cg"][sl])
        def _bc(e, sl=sl, t2=t2, pairi=pairi):
            for g in range(2):
                e.dma_start(thrs[g * 64:(g + 1) * 64, sl * QW + pairi * 1024: sl * QW + (pairi + 1) * 1024],
                            thrh[g:g + 1, t2 * 1024:(t2 + 1) * 1024].broadcast_to([64, 1024])
                            ).then_inc(s_tbc[t2].h, 16)
        em("sync", _bc)
        s_tbc[t2].n += 32
        qs["bc_n"][t2] += 1

    def q_mask(j):
        if j < 0 or j >= NQ:
            return
        sl = j % 2
        mksl = j % 2
        for pi in (0, 1):
            wait("vector", s_tbc[sl * 2 + pi], 32 * qs["bc_n"][sl * 2 + pi])
        if qs["qs_round"][mksl] > 0:
            wait("vector", s_qs[mksl], 16 * qs["qs_round"][mksl])
        def _mk(e, sl=sl, awsl=j % 4, mksl=mksl):
            e.tensor_tensor(mkb[:, mksl * QW:(mksl + 1) * QW], awb[:, awsl * QW:(awsl + 1) * QW],
                            thrs[:, sl * QW:(sl + 1) * QW], ALU.is_gt).then_inc(cv.h, 1)
        em("vector", _mk); cv.n += 1
        qs["mask_cv"][sl] = cv.n
        qs["aw_user"][j % 4] = cv.n
        wait("vector", cv, cv.n)
        if qs["wq_cg"][sl] is not None:
            wait("vector", cg, qs["wq_cg"][sl])
        def _m2(e, sl=j % 2, w4=j % 6):
            e.scalar_tensor_tensor(m2b.ap().bitcast(I32)[:, sl * QW // 2:(sl + 1) * QW // 2],
                                   wqin.ap().bitcast(I32)[:, w4 * QW // 2:(w4 + 1) * QW // 2],
                                   sgnc[:, 0:1],
                                   mkb.ap().bitcast(I32)[:, sl * QW // 2:(sl + 1) * QW // 2],
                                   ALU.bitwise_and, ALU.bitwise_or).then_inc(cv.h, 1)
        em("vector", _m2); cv.n += 1
        qs["m2_cv"][j % 2] = cv.n
        qs["wqin_user"][j % 6] = cv.n

    def q_wq(j):
        if j < 0 or j >= NQ:
            return
        sl = j % 2
        wait("gpsimd", cv, qs["m2_cv"][j % 2])
        def _wq(e, sl=sl, m2sl=j % 2):
            e.tensor_mul(mkb[:, m2sl * QW:(m2sl + 1) * QW], m2b[:, m2sl * QW:(m2sl + 1) * QW],
                         thrs[:, sl * QW:(sl + 1) * QW]).then_inc(cg.h, 1)
        em("gpsimd", _wq); cg.n += 1
        qs["wq_cg"][sl] = cg.n
        is_gu = j < 128
        qeng = "scalar" if is_gu else "sync"
        wait(qeng, cg, qs["wq_cg"][sl])
        mksl = j % 2
        def _stq(e, mksl=mksl, dst=qchunks[j][1], is_gu=is_gu):
            if is_gu:
                e.dma_start(dst, mkb.ap().rearrange("p (s h w) -> p s h w", s=2, h=8)[:, mksl]
                            ).then_inc(s_qs[mksl].h, 16)
            else:
                e.dma_start(dst, mkb[:, mksl * QW:(mksl + 1) * QW]).then_inc(s_qs[mksl].h, 16)
        em(qeng, _stq)
        s_qs[mksl].n += 16
        qs["qs_round"][mksl] += 1

    def emit_quant(k):
        if k == 0:
            q_load(0)
            q_load(1)
        q_th(k - 2, 0)
        q_th(k - 2, 1)
        q_bc(k - 3, 0)
        q_bc(k - 3, 1)
        q_mask(k - 4)
        q_wq(k - 4)
        q_load(k + 2)
        q_aw(k)
        q_gs(k - 1, 0)
        q_gs(k - 1, 1)


    z_load(0, 0)
    z_load(1, 1)
    for _k in range(4):
        emit_quant(_k)
    ln_cs0 = ln_block(0, xln, 0)
    transpose_block(xln, 0 * C, xlnT, 0 * 2048, ln_cs0)
    ln_cs1 = ln_block(1, xln, 1)
    xT_cv = transpose_block(xln, 1 * C, xlnT, 1 * 2048, ln_cs1)
    xln_cs = cs.n

    wait("tensor", s_btw, s_btw.n)
    wait("tensor", cs, xln_cs)
    def mm_uw(widx, dst):
        for dt in range(2):
            for ncc in range(4):
                b = (dt * 4 + ncc) % 2
                bank_wait("tensor", b)
                def _mm(e, dt=dt, ncc=ncc, b=b, w=widx):
                    ii = None
                    for kt in range(2):
                        ii = e.matmul(bank(b), btile(w, kt, dt),
                                      xln[:, kt * C + ncc * 512: kt * C + (ncc + 1) * 512],
                                      start=(kt == 0), stop=(kt == 1))
                    ii.then_inc(ct.h, 1)
                em("tensor", _mm); ct.n += 1
                wait("scalar", ct, ct.n)
                def _ev(e, dt=dt, ncc=ncc, b=b, d=dst):
                    e.activation(d[:, dt * C + ncc * 512: dt * C + (ncc + 1) * 512],
                                 bank(b), AF.Identity).then_inc(cs.h, 1)
                em("scalar", _ev); cs.n += 1
                bank_ev[b] = (cs, cs.n)
    emit_quant(4)
    emit_quant(5)
    mm_uw(0, uu)
    mm_uw(1, ww)
    emit_quant(6)
    emit_quant(7)
    uw_cs = cs.n

    for nt in range(16):
        for mc in range(4):
            b = (nt * 4 + mc) % 4
            bank_wait("tensor", b)
            def _mm(e, nt=nt, mc=mc, b=b):
                ii = None
                for kt in range(2):
                    ii = e.matmul(bank(b), uu[:, kt * C + nt * 128: kt * C + (nt + 1) * 128],
                                  ww[:, kt * C + mc * 512: kt * C + (mc + 1) * 512],
                                  start=(kt == 0), stop=(kt == 1))
                ii.then_inc(ct.h, 1)
            em("tensor", _mm); ct.n += 1
            wait("scalar", ct, ct.n)
            def _ev(e, nt=nt, mc=mc, b=b):
                e.activation(att[:, nt * C + mc * 512: nt * C + (mc + 1) * 512],
                             bank(b), AF.Relu).then_inc(cs.h, 1)
            em("scalar", _ev); cs.n += 1
            bank_ev[b] = (cs, cs.n)
    att_cs = cs.n
    scores_ct = ct.n
    wait("gpsimd", cs, att_cs)
    wait("gpsimd", cg, zsrc_cg)
    def _dg(e):
        e.dma_start(att[127:128, 15 * C + 2047: 15 * C + 2048], zsrc[0:1, 0:1]).then_inc(s_dg.h, 16)
    em("gpsimd", _dg); s_dg.n += 16

    wait("tensor", cv, xT_cv)
    wait("tensor", s_dg, s_dg.n)
    xatt = uu
    for dt in range(2):
        for mc in range(4):
            b = (dt * 4 + mc) % 2
            bank_wait("tensor", b)
            def _mm(e, dt=dt, mc=mc, b=b):
                ii = None
                for nt in range(16):
                    ii = e.matmul(bank(b), xlnT[:, (dt * 16 + nt) * 128: (dt * 16 + nt + 1) * 128],
                                  att[:, nt * C + mc * 512: nt * C + (mc + 1) * 512],
                                  start=(nt == 0), stop=(nt == 15))
                ii.then_inc(ct.h, 1)
            em("tensor", _mm); ct.n += 1
            wait("scalar", ct, ct.n)
            wait("scalar", ct, scores_ct)
            def _ev(e, dt=dt, mc=mc, b=b):
                e.activation(xatt[:, dt * C + mc * 512: dt * C + (mc + 1) * 512],
                             bank(b), AF.Identity).then_inc(cs.h, 1)
            em("scalar", _ev); cs.n += 1
            bank_ev[b] = (cs, cs.n)
    xatt_cs = cs.n
    for _k in range(8, 12):
        emit_quant(_k)

    wait("gpsimd", cs, xln_cs)
    z_load(0, 0)
    z_load(1, 1)
    x1_round = [0, 0]
    wait("tensor", cs, xatt_cs)
    for it in range(2):
        for mc in range(4):
            b = mc
            bank_wait("tensor", b)
            def _mm(e, it=it, mc=mc, b=b):
                ii = None
                for kd in range(2):
                    ii = e.matmul(bank(b), btile(2, kd, it),
                                  xatt[:, kd * C + mc * 512: kd * C + (mc + 1) * 512],
                                  start=(kd == 0), stop=(kd == 1))
                ii.then_inc(ct.h, 1)
            em("tensor", _mm); ct.n += 1
            bank_ev[b] = None
        ct_x = ct.n
        wait("vector", s_z[it], 16 * zround[it])
        wait("vector", cv, cv.n)
        def _t1(e, it=it):
            e.tensor_add(t1b[:, 0:C], zring[:, it * C:(it + 1) * C],
                         xln[:, it * C:(it + 1) * C]).then_inc(cv.h, 1)
        em("vector", _t1); cv.n += 1
        wait("vector", cv, cv.n)
        wait("vector", ct, ct_x)
        def _x1(e, it=it):
            ii = None
            for mc in range(4):
                ii = e.scalar_tensor_tensor(
                    x1b[:, it * C + mc * 512: it * C + (mc + 1) * 512],
                    bank(mc), 1.0, t1b[:, mc * 512:(mc + 1) * 512],
                    ALU.bypass, ALU.add)
            ii.then_inc(cv.h, 1)
        em("vector", _x1); cv.n += 1
        for mc in range(4):
            bank_ev[mc] = (cv, cv.n)
        zfree[it] = (cv, cv.n)
        wait("sync", cv, cv.n)
        def _sp(e, it=it):
            e.dma_start(x1s[it * 128:(it + 1) * 128, :],
                        x1b[:, it * C:(it + 1) * C]).then_inc(s_x1[it].h, 16)
        em("sync", _sp)
        s_x1[it].n += 16
        x1_round[it] += 1

    lnslot = [None, None]

    def emit_yblock(yi):
        blk = 2 + yi
        zsl = blk % NZ
        sl = yi % 2
        z_load(zsl, blk)
        if lnslot[sl] is not None:
            wait("scalar", cv, lnslot[sl])
        yield
        ln_cs = ln_block(zsl, lnb, sl)
        yield
        lnT_cv = transpose_block(lnb, sl * C, lnT, sl * C, ln_cs)
        wait("tensor", cv, lnT_cv)
        yield
        wait("vector", s_z[zsl], 16 * zround[zsl])
        wait("vector", cs, ln_cs)
        if lnslot[sl] is not None:
            wait("vector", cv, lnslot[sl])
        def _t1(e, sl=sl, zsl=zsl):
            e.tensor_add(t1b[:, 0:C], zring[:, zsl * C:(zsl + 1) * C],
                         lnb[:, sl * C:(sl + 1) * C]).then_inc(cv.h, 1)
        em("vector", _t1); cv.n += 1
        t1cv = cv.n
        zfree[zsl] = (cv, cv.n)
        if x1_round[sl] > 0:
            wait("vector", s_x1[sl], 16 * x1_round[sl])
        for half in range(2):
            for q in range(2):
                bank_wait("tensor", q)
                def _mm(e, sl=sl, half=half, q=q):
                    ii = None
                    for nt in range(16):
                        mc = 2 * half + q
                        ii = e.matmul(bank(q), lnT[:, sl * C + nt * 128: sl * C + (nt + 1) * 128],
                                      att[:, nt * C + mc * 512: nt * C + (mc + 1) * 512],
                                      start=(nt == 0), stop=(nt == 15))
                    ii.then_inc(ct.h, 1)
                em("tensor", _mm); ct.n += 1
                ct_y = ct.n
                yield
                wait("vector", cv, t1cv)
                wait("vector", ct, ct_y)
                def _x1(e, sl=sl, half=half, q=q):
                    mc = 2 * half + q
                    e.scalar_tensor_tensor(
                        x1b[:, sl * C + mc * 512: sl * C + (mc + 1) * 512],
                        bank(q), rbc[:, 0:1], t1b[:, mc * 512:(mc + 1) * 512],
                        ALU.mult, ALU.add).then_inc(cv.h, 1)
                em("vector", _x1); cv.n += 1
                bank_ev[q] = (cv, cv.n)
        lnslot[sl] = cv.n
        wait("sync", cv, cv.n)
        def _sp(e, sl=sl, blk=blk):
            e.dma_start(x1s[blk * 128:(blk + 1) * 128, :],
                        x1b[:, sl * C:(sl + 1) * C]).then_inc(s_x1[sl].h, 16)
        em("sync", _sp)
        s_x1[sl].n += 16
        x1_round[sl] += 1

    NQI = NQ + 4
    qi = 12
    for yi in range(NYB):
        ql = min(NQI, 12 + (yi + 1) * (NQI - 12) // NYB)
        g = emit_yblock(yi)
        for _ in g:
            if qi < ql:
                emit_quant(qi)
                qi += 1
        while qi < ql:
            emit_quant(qi)
            qi += 1
    while qi < NQI:
        emit_quant(qi)
        qi += 1

    for eng in ENGS:
        def _b(e):
            e.drain().then_inc(bar.h, 1)
        em(eng, _b)
    bar.n += 5
    for eng in ENGS:
        wait(eng, bar, bar.n)
    wait("sync", s_x1[0], 16 * x1_round[0])
    wait("sync", s_x1[1], 16 * x1_round[1])
    for _qi in range(2):
        wait("sync", s_qs[_qi], 16 * qs["qs_round"][_qi])
    def _b2(e):
        e.drain().then_inc(bar.h, 1)
    em("sync", _b2); bar.n += 1
    for eng in ("tensor", "vector", "scalar", "gpsimd"):
        wait(eng, bar, bar.n)

    es.close()
    es2 = ExitStack()
    def sb2(name, p, f, dt=F32):
        return es2.enter_context(nc.sbuf_tensor(name, [p, f], dt))
    HRW = 256
    NHR = H // HRW
    hsil = sb2("hsil", 128, 64 * 512, BF16)
    ln2T = sb2("ln2T", 128, 16 * 512, BF16)
    x1c = sb2("x1c", 128, 2 * 4 * C, BF16)
    wgb = sb2("wgb", 128, 4 * 16 * HRW, BF16)
    wpb = sb2("wpb", 128, 4 * 1024, BF16)
    sgl = sb2("sgl", 128, 2 * 512)
    hsq2 = sb2("hsq2", 128, 2 * 512, BF16)
    xo = sb2("xo", 128, 2 * 1024)
    invv = sb2("invv", 1, 512)
    invc = sb2("invc", 128, 4)
    st2 = sb2("st2", 128, 24)
    ag2 = sb2("ag2", 128, 2)
    iv2 = sb2("iv2", 128, 4)
    ln2b = sb2("ln2b", 128, 4 * C, BF16)

    s_xc = [mk("s_xc0"), mk("s_xc1")]
    s_wg = [mk(f"s_wg{i}") for i in range(4)]
    s_wp = [mk(f"s_wp{i}") for i in range(4)]
    s_out = [mk(f"s_out{i}") for i in range(2)]

    fx = {"xc_cons": [None, None], "wg_cons": [None] * 4, "wp_cons": [None] * 4,
          "hs_cv": [None] * 64, "sg_free": [None, None], "hq_ct": [None, None],
          "hq_cv": [None, None], "xo_user": [None] * 8, "ss_ct": None}

    def xc_load(ch):
        sl = ch % 2
        if fx["xc_cons"][sl] is not None:
            wait("sync", cv, fx["xc_cons"][sl])
        def _f(e, sl=sl, ch=ch):
            e.dma_start(x1c.ap().rearrange("p (s b c) -> p s b c", s=2, b=4)[:, sl],
                        x1s.rearrange("(b p) c -> p b c", p=128)[:, ch * 4:(ch + 1) * 4]
                        ).then_inc(s_xc[sl].h, 16)
        em("sync", _f)
        s_xc[sl].n += 16

    def wg_load(hrg):
        for half, hr in ((0, hrg), (1, hrg + NHR)):
            sl = (hrg % 2) * 2 + half
            if fx["wg_cons"][sl] is not None:
                wait("sync", ct, fx["wg_cons"][sl])
            def _f(e, sl=sl, hr=hr):
                e.dma_start(wgb.ap().rearrange("p (s d h) -> p s d h", s=4, d=16)[:, sl],
                            sqg[hr]).then_inc(s_wg[sl].h, 16)
            em("sync", _f)
            s_wg[sl].n += 16

    def wp_load(jt, dchalf):
        sl = jt % 4
        if fx["wp_cons"][sl] is not None:
            wait("scalar", ct, fx["wp_cons"][sl])
        def _f(e, sl=sl, jt=jt, dchalf=dchalf):
            e.dma_start(wpb[:, sl * 1024:(sl + 1) * 1024],
                        sqp[jt * 128:(jt + 1) * 128, dchalf * 1024:(dchalf + 1) * 1024]
                        ).then_inc(s_wp[sl].h, 16)
        em("scalar", _f)
        s_wp[sl].n += 16

    ln_cs_of = [None] * 4
    tp_ct_of = [None] * 4

    def ln2_chain(ch, tb):
        xsl = ch % 2
        xv = x1c.ap().rearrange("p (s b c) -> p s b c", s=2, b=4)[:, xsl, tb]
        wait("vector", s_xc[xsl], 16 * (ch // 2 + 1))
        def _st(e, xv=xv):
            ii = None
            for i in range(4):
                ii = e.bn_stats(st2[:, i * 6:(i + 1) * 6], xv[:, i * 512:(i + 1) * 512])
            ii.then_inc(cv.h, 1)
        em("vector", _st); cv.n += 1
        wait("vector", cv, cv.n)
        def _ag(e):
            e.bn_aggr(ag2[:], st2[:]).then_inc(cv.h, 1)
        em("vector", _ag); cv.n += 1
        wait("vector", cv, cv.n)
        def _ve(e):
            e.tensor_scalar(iv2[:, 0:1], ag2[:, 1:2], LN_EPS, 0.25,
                            ALU.add, ALU.mult).then_inc(cv.h, 1)
        em("vector", _ve); cv.n += 1
        wait("scalar", cv, cv.n)
        def _sq(e):
            e.activation(iv2[:, 1:2], iv2[:, 0:1], AF.Sqrt).then_inc(cs.h, 1)
        em("scalar", _sq); cs.n += 1
        wait("vector", cs, cs.n)
        def _rc(e):
            e.reciprocal(iv2[:, 2:3], iv2[:, 1:2]).then_inc(cv.h, 1)
        em("vector", _rc); cv.n += 1
        wait("vector", cv, cv.n)
        def _nm(e):
            e.scalar_tensor_tensor(iv2[:, 3:4], ag2[:, 0:1], -1.0, iv2[:, 2:3],
                                   ALU.mult, ALU.mult).then_inc(cv.h, 1)
        em("vector", _nm); cv.n += 1
        wait("scalar", cv, cv.n)
        if tp_ct_of[tb] is not None:
            wait("scalar", ct, tp_ct_of[tb])
        def _ln(e, xv=xv, tb=tb):
            e.activation(ln2b[:, tb * C:(tb + 1) * C], xv, AF.Identity, bias=iv2[:, 3:4],
                         scale=iv2[:, 2:3]).then_inc(cs.h, 1)
        em("scalar", _ln); cs.n += 1
        ln_cs_of[tb] = cs.n

    def ln2_tp(tb):
        wait("tensor", cs, ln_cs_of[tb])
        if tb45[0] is not None:
            wait("tensor", tb45[0][0], tb45[0][1])
        def _tp(e, tb=tb):
            ii = None
            for dt in range(16):
                ii = e.transpose(ppb[:, 4 * 1024 + dt * 128: 4 * 1024 + (dt + 1) * 128],
                                 ln2b[:, tb * C + dt * 128: tb * C + (dt + 1) * 128], idbs[:])
            ii.then_inc(ct.h, 1)
        em("tensor", _tp); ct.n += 1
        tp_ct_of[tb] = ct.n
        wait("vector", ct, ct.n)
        def _ev(e, tb=tb):
            e.tensor_copy(ln2T.ap().rearrange("p (d t) -> p d t", d=16)[:, :, tb * 128:(tb + 1) * 128],
                          ppb[:, 4 * 1024: 4 * 1024 + 2048].rearrange("p (d t) -> p d t", d=16)
                          ).then_inc(cv.h, 1)
        em("vector", _ev); cv.n += 1
        tb45[0] = (cv, cv.n)

    xc_load(0)
    for tb in range(4):
        ln2_chain(0, tb)
    for ch in range(NCH):
        sl = ch % 2
        if ch + 1 < NCH:
            xc_load(ch + 1)
        for tb in range(4):
            ln2_tp(tb)
        ln2T_cv = cv.n

        pend_ss = [None]
        wait("tensor", cv, ln2T_cv)
        wg_load(0)
        for hrg in range(NHR):
            if hrg + 1 < NHR:
                wg_load(hrg + 1)
            if ch + 1 < NCH and hrg in (6, 12, 18, 24):
                ln2_chain(ch + 1, (hrg - 6) // 6)
            slg = (hrg % 2) * 2
            slu = slg + 1
            wait("tensor", s_wg[slg], s_wg[slg].n)
            wait("tensor", s_wg[slu], s_wg[slu].n)
            for hi in range(HRW // 128):
                jt = hrg * (HRW // 128) + hi
                gb = jt % 2
                if fx["sg_free"][gb] is not None:
                    wait("tensor", cs, fx["sg_free"][gb])
                if fx["hq_cv"][gb] is not None:
                    wait("tensor", cv, fx["hq_cv"][gb])
                def _mmg(e, slg=slg, hi=hi, gb=gb):
                    w = wgb.ap().rearrange("p (s d h) -> p s d h", s=4, d=16)[:, slg]
                    ii = None
                    for dt in range(16):
                        ii = e.matmul(bank(gb), w[:, dt, hi * 128:(hi + 1) * 128],
                                      ln2T[:, dt * 512:(dt + 1) * 512],
                                      start=(dt == 0), stop=(dt == 15))
                    ii.then_inc(ct.h, 1)
                em("tensor", _mmg); ct.n += 1
                def _mmu(e, slu=slu, hi=hi, gb=gb):
                    w = wgb.ap().rearrange("p (s d h) -> p s d h", s=4, d=16)[:, slu]
                    ii = None
                    for dt in range(16):
                        ii = e.matmul(bank(2 + gb), w[:, dt, hi * 128:(hi + 1) * 128],
                                      ln2T[:, dt * 512:(dt + 1) * 512],
                                      start=(dt == 0), stop=(dt == 15))
                    ii.then_inc(ct.h, 1)
                em("tensor", _mmu); ct.n += 1
                ct_pair = ct.n
                if hi == HRW // 128 - 1:
                    fx["wg_cons"][slg] = ct.n
                    fx["wg_cons"][slu] = ct.n
                wait("scalar", ct, ct_pair - 1)
                if fx["hq_cv"][gb] is not None:
                    wait("scalar", cv, fx["hq_cv"][gb])
                def _si(e, gb=gb):
                    e.activation(sgl[:, gb * 512:(gb + 1) * 512], bank(gb), AF.Silu
                                 ).then_inc(cs.h, 1)
                em("scalar", _si); cs.n += 1
                sg_cs = cs.n
                fx["sg_free"][gb] = cs.n
                wait("vector", cs, sg_cs)
                wait("vector", ct, ct_pair)
                def _hs(e, jt=jt, gb=gb):
                    e.tensor_mul(hsil[:, jt * 512:(jt + 1) * 512], sgl[:, gb * 512:(gb + 1) * 512],
                                 bank(2 + gb)).then_inc(cv.h, 1)
                em("vector", _hs); cv.n += 1
                hs_cv = cv.n
                fx["hs_cv"][jt] = hs_cv
                wait("vector", cv, hs_cv)
                if fx["hq_ct"][gb] is not None:
                    wait("vector", ct, fx["hq_ct"][gb])
                def _hq(e, jt=jt, gb=gb):
                    e.tensor_mul(hsq2[:, gb * 512:(gb + 1) * 512], hsil[:, jt * 512:(jt + 1) * 512],
                                 hsil[:, jt * 512:(jt + 1) * 512]).then_inc(cv.h, 1)
                em("vector", _hq); cv.n += 1
                fx["hq_cv"][gb] = cv.n
                def emit_ss(jt=jt, gb=gb, hqv=cv.n):
                    wait("tensor", cv, hqv)
                    def _ss(e, jt=jt, gb=gb):
                        e.matmul(pp[0:1, 6 * 512:7 * 512], ones1s[:], hsq2[:, gb * 512:(gb + 1) * 512],
                                 start=(jt == 0), stop=(jt == 63)).then_inc(ct.h, 1)
                    em("tensor", _ss); ct.n += 1
                    fx["hq_ct"][gb] = ct.n
                if pend_ss[0] is not None:
                    pend_ss[0]()
                pend_ss[0] = emit_ss
        if pend_ss[0] is not None:
            pend_ss[0]()
            pend_ss[0] = None
        gu_ct = ct.n

        wait("vector", ct, gu_ct)
        def _ms(e):
            e.tensor_scalar(invv[0:1, :], pp[0:1, 6 * 512:7 * 512], 0.25 / 8192.0,
                            2.9802322387695312e-08, ALU.mult, ALU.add).then_inc(cv.h, 1)
        em("vector", _ms); cv.n += 1
        wait("scalar", cv, cv.n)
        def _sq2(e):
            e.activation(invv[0:1, :], invv[0:1, :], AF.Sqrt).then_inc(cs.h, 1)
        em("scalar", _sq2); cs.n += 1
        wait("vector", cs, cs.n)
        def _rc2(e):
            e.reciprocal(invv[0:1, :], invv[0:1, :]).then_inc(cv.h, 1)
        em("vector", _rc2); cv.n += 1
        inv_cv = cv.n
        wait("tensor", cv, inv_cv)
        def _ti(e):
            ii = None
            for tb in range(4):
                ii = e.transpose(pp[:, 7 * 512 + tb: 7 * 512 + tb + 1],
                                 invv[0:1, tb * 128:(tb + 1) * 128], idfs[0:1, 0:1])
            ii.then_inc(ct.h, 1)
        em("tensor", _ti); ct.n += 1
        wait("vector", ct, ct.n)
        def _ic(e):
            e.tensor_copy(invc[:], pp[:, 7 * 512: 7 * 512 + 4]).then_inc(cv.h, 1)
        em("vector", _ic); cv.n += 1
        invc_cv = cv.n

        for dchalf in range(2):
            wait("tensor", cv, invc_cv)
            if dchalf == 1:
                for u in range(8):
                    if fx["xo_user"][u] is not None:
                        wait("tensor", cv, fx["xo_user"][u])
            wp_load(0, dchalf)
            wp_load(1, dchalf)
            for jt in range(64):
                if jt + 2 < 64:
                    wp_load(jt + 2, dchalf)
                wsl = jt % 4
                wait("tensor", s_wp[wsl], s_wp[wsl].n)
                wait("tensor", cv, fx["hs_cv"][jt])
                def _mm(e, jt=jt, wsl=wsl):
                    ii = None
                    for tb in range(4):
                        for dc in range(2):
                            ii = e.matmul(bank(tb * 2 + dc),
                                          hsil[:, jt * 512 + tb * 128: jt * 512 + (tb + 1) * 128],
                                          wpb[:, wsl * 1024 + dc * 512: wsl * 1024 + (dc + 1) * 512],
                                          start=(jt == 0), stop=(jt == 63))
                    ii.then_inc(ct.h, 1)
                em("tensor", _mm); ct.n += 1
                fx["wp_cons"][wsl] = ct.n
            mm2_ct = ct.n
            wait("vector", ct, mm2_ct)
            for tb in range(4):
                xsl = tb % 2
                if s_out[xsl].n > 0:
                    wait("vector", s_out[xsl], s_out[xsl].n)
                def _xo(e, tb=tb, sl=sl, dchalf=dchalf, xsl=xsl):
                    xv = x1c.ap().rearrange("p (s b c) -> p s b c", s=2, b=4)[:, sl, tb]
                    ii = None
                    for dc in range(2):
                        d0 = dchalf * 1024 + dc * 512
                        ii = e.scalar_tensor_tensor(
                            xo[:, xsl * 1024 + dc * 512: xsl * 1024 + (dc + 1) * 512],
                            bank(tb * 2 + dc), invc[:, tb:tb + 1], xv[:, d0:d0 + 512],
                            ALU.mult, ALU.add)
                    ii.then_inc(cv.h, 1)
                em("vector", _xo); cv.n += 1
                fx["xo_user"][dchalf * 4 + tb] = cv.n
                wait("gpsimd", cv, cv.n)
                def _ow(e, tb=tb, ch=ch, dchalf=dchalf, xsl=xsl):
                    e.dma_start(out[(ch * 4 + tb) * 128:(ch * 4 + tb + 1) * 128,
                                    dchalf * 1024:(dchalf + 1) * 1024],
                                xo[:, xsl * 1024:(xsl + 1) * 1024]).then_inc(s_out[xsl].h, 16)
                em("gpsimd", _ow)
                s_out[xsl].n += 16
        fx["xc_cons"][sl] = cv.n
        tb45[0] = (cv, cv.n)

    for q in range(2):
        wait("gpsimd", s_out[q], s_out[q].n)

    blk = es2.enter_context(nc.Block())
    for name, attr in (("sync", blk.sync), ("tensor", blk.tensor), ("vector", blk.vector),
                       ("scalar", blk.scalar), ("gpsimd", blk.gpsimd)):
        def _run(e, ops=streams[name]):
            for op in ops:
                op(e)
        attr(_run)
    es2.close()
    keep.close()
    sems.close()
    return nc


def kernel(L0_attn_Z_in, L0_attn_A_l, L0_attn_r_l, L0_attn_B_l, L0_attn_C_l,
           L0_ffn_w_gate_up, L0_ffn_w_proj):
    bf = ml_dtypes.bfloat16
    Z = np.asarray(L0_attn_Z_in, dtype=np.float32)
    A = np.ascontiguousarray(np.asarray(L0_attn_A_l, dtype=np.float32).T)
    B = np.ascontiguousarray(np.asarray(L0_attn_B_l, dtype=np.float32).T)
    Cm = np.ascontiguousarray(np.asarray(L0_attn_C_l, dtype=np.float32).T)
    r = np.asarray(L0_attn_r_l, dtype=np.float32).reshape(1)
    WguT = np.ascontiguousarray(np.asarray(L0_ffn_w_gate_up, dtype=np.float32).T)
    WpT = np.ascontiguousarray(np.asarray(L0_ffn_w_proj, dtype=np.float32).T)

    idf = np.eye(128, dtype=np.float32)
    idb = np.eye(128, dtype=np.float32).astype(bf)
    ind64 = np.zeros((128, 2), dtype=np.float32)
    ind64[0:64, 0] = 1.0
    ind64[64:128, 1] = 1.0
    ind2 = np.ascontiguousarray(ind64.T).astype(bf)
    ind64 = ind64.astype(bf)
    ones1 = np.ones((128, 1), dtype=np.float32).astype(bf)
    onerow = np.ones((1, 128), dtype=np.float32)

    nc = build_program(R=Z.shape[1])
    common = dict(bT=B, cT=Cm, aT=A, rl=r, wguT=WguT, wpT=WpT, idf=idf, idb=idb,
                  ind64=ind64, ind2=ind2, ones1=ones1, onerow=onerow)
    in_maps = [dict(common, z=np.ascontiguousarray(Z[b])) for b in range(Z.shape[0])]
    res = run_bass_kernel_spmd(nc, in_maps, core_ids=list(range(8)))
    return np.stack([np.asarray(res.results[b]["out"]) for b in range(Z.shape[0])], axis=0)


# version 40
# speedup vs baseline: 1.1154x; 1.0043x over previous
import numpy as np
import ml_dtypes
from contextlib import ExitStack

import concourse.bass as bass
import concourse.mybir as mybir
from concourse.bass_utils import run_bass_kernel_spmd

F32 = mybir.dt.float32
BF16 = mybir.dt.bfloat16
I16 = mybir.dt.int16
I32 = mybir.dt.int32
AF = mybir.ActivationFunctionType
ALU = mybir.AluOpType

C = 2048
D = 256
H = 8192
RMS_EPS = 1.1920928955078125e-07
LN_EPS = 1e-5
QW = 2048


class Ctr:
    def __init__(self, es, nc, name):
        self.h = es.enter_context(nc.semaphore(name))
        self.n = 0


def build_program(R=4096):
    NB = R // 128
    NYB = NB - 2
    NCH = R // 512
    nc = bass.Bass("TRN2", target_bir_lowering=False)

    z = nc.dram_tensor("z", [R, C], F32, kind="ExternalInput").ap()
    bT = nc.dram_tensor("bT", [D, D], F32, kind="ExternalInput").ap()
    cT = nc.dram_tensor("cT", [D, D], F32, kind="ExternalInput").ap()
    aT = nc.dram_tensor("aT", [D, D], F32, kind="ExternalInput").ap()
    rl = nc.dram_tensor("rl", [1], F32, kind="ExternalInput").ap()
    wguT = nc.dram_tensor("wguT", [C, 2 * H], F32, kind="ExternalInput").ap()
    wpT = nc.dram_tensor("wpT", [H, C], F32, kind="ExternalInput").ap()
    idf = nc.dram_tensor("idf", [128, 128], F32, kind="ExternalInput").ap()
    idb = nc.dram_tensor("idb", [128, 128], BF16, kind="ExternalInput").ap()
    ind64 = nc.dram_tensor("ind64", [128, 2], BF16, kind="ExternalInput").ap()
    ind2 = nc.dram_tensor("ind2", [2, 128], BF16, kind="ExternalInput").ap()
    ones1 = nc.dram_tensor("ones1", [128, 1], BF16, kind="ExternalInput").ap()
    onerow = nc.dram_tensor("onerow", [1, 128], F32, kind="ExternalInput").ap()
    out = nc.dram_tensor("out", [R, C], F32, kind="ExternalOutput").ap()
    sqg = nc.dram_tensor("sqg", [2 * H // 256, 128, 16, 256], BF16, kind="Internal").ap()
    sqp = nc.dram_tensor("sqp", [H, C], BF16, kind="Internal").ap()
    x1s = nc.dram_tensor("x1s", [R, C], BF16, kind="Internal").ap()
    thrh = nc.dram_tensor("thrh", [2, 4 * 1024], BF16, kind="Internal").ap()

    keep = ExitStack()
    def sbk(name, p, f, dt=F32):
        return keep.enter_context(nc.sbuf_tensor(name, [p, f], dt))

    pp = keep.enter_context(nc.psum_tensor("pp", [128, 4096], F32))
    ppb = pp.ap().bitcast(BF16)
    def bank(i):
        return pp[:, i * 512:(i + 1) * 512]

    idfs = sbk("idfs", 128, 128)
    idbs = sbk("idbs", 128, 128, BF16)
    ind64s = sbk("ind64s", 128, 2, BF16)
    ind2s = sbk("ind2s", 2, 128, BF16)
    ones1s = sbk("ones1s", 128, 1, BF16)
    onerow_s = sbk("onerow_s", 1, 128)
    rbc = sbk("rbc", 128, 1)
    zsrc = sbk("zsrc", 1, 8, BF16)
    sgnc = sbk("sgnc", 128, 1, I32)
    absc = sbk("absc", 128, 1, I32)
    qeps = sbk("qeps", 1, 1)
    thsc = sbk("thsc", 128, 1)
    nege = sbk("nege", 128, 1)
    pose = sbk("pose", 128, 1)

    es = ExitStack()
    def sb(name, p, f, dt=F32):
        return es.enter_context(nc.sbuf_tensor(name, [p, f], dt))

    att = sb("att", 128, 16 * C, BF16)
    xln = sb("xln", 128, 2 * C, BF16)
    xlnT = sb("xlnT", 128, 2 * 16 * 128, BF16)
    uu = sb("uu", 128, 2 * C, BF16)
    ww = sb("ww", 128, 2 * C, BF16)
    NZ = 2
    zring = sb("zring", 128, NZ * C, BF16)
    lnb = sb("lnb", 128, 2 * C, BF16)
    lnT = sb("lnT", 128, 2 * C, BF16)
    x1b = sb("x1b", 128, 2 * C, BF16)
    t1b = sb("t1b", 128, C, BF16)
    btw = sb("btw", 128, 3 * 2 * 2 * 128, BF16)
    stat = sb("stat", 128, 24)
    aggr = sb("aggr", 128, 2)
    ivar = sb("ivar", 128, 4)
    QSL = 4
    wqin = sb("wqin", 128, 6 * QW, BF16)
    awb = sb("awb", 128, 4 * QW, BF16)
    mkb = sb("mkb", 128, 2 * QW, BF16)
    m2b = sb("m2b", 128, 2 * QW, BF16)
    thrs = sb("thrs", 128, 2 * QW, BF16)
    thr2 = sb("thr2", 2, 2 * 1024, BF16)
    tmpth = sb("tmpth", 2, 1024, BF16)

    sems = ExitStack()
    def mk(name):
        return Ctr(sems, nc, name)
    s_z = [mk(f"s_z{i}") for i in range(NZ)]
    s_cst = mk("s_cst")
    s_btw = mk("s_btw")
    s_w = [mk(f"s_w{i}") for i in range(6)]
    s_qs = [mk(f"s_qs{i}") for i in range(2)]
    s_tsp = [mk(f"s_tsp{i}") for i in range(4)]
    s_tbc = [mk(f"s_tbc{i}") for i in range(4)]
    s_x1 = [mk(f"s_x1{i}") for i in range(2)]
    s_dg = mk("s_dg")
    cv = mk("cv"); cs = mk("cs"); ct = mk("ct"); cg = mk("cg")
    bar = mk("bar")

    ENGS = ("sync", "tensor", "vector", "scalar", "gpsimd")
    streams = {k: [] for k in ENGS}
    def em(eng, fn):
        streams[eng].append(fn)
    def wait(eng, ctr, val):
        if val is None or val <= 0:
            return
        streams[eng].append(lambda e, h=ctr.h, v=val: e.wait_ge(h, v))

    bank_ev = {}
    def bank_wait(eng, b):
        ev = bank_ev.get(b)
        if ev is not None:
            wait(eng, ev[0], ev[1])

    def _cst(e):
        e.dma_start(idfs[:], idf[:]).then_inc(s_cst.h, 16)
        e.dma_start(idbs[:], idb[:]).then_inc(s_cst.h, 16)
        e.dma_start(ind64s[:], ind64[:]).then_inc(s_cst.h, 16)
        e.dma_start(ind2s[:], ind2[:]).then_inc(s_cst.h, 16)
        e.dma_start(ones1s[:], ones1[:]).then_inc(s_cst.h, 16)
        e.dma_start(onerow_s[:], onerow[:]).then_inc(s_cst.h, 16)
        e.dma_start(rbc[0:1, 0:1], rl[0:1]).then_inc(s_cst.h, 16)
    em("sync", _cst); s_cst.n += 112

    def _btwload(e):
        for i, w in enumerate((bT, cT, aT)):
            e.dma_start(
                btw.ap().rearrange("p (w a b q) -> p w a b q", w=3, a=2, b=2)[:, i],
                w.rearrange("(a p) (b q) -> p a b q", p=128, q=128),
            ).then_inc(s_btw.h, 16)
    em("gpsimd", _btwload); s_btw.n += 48
    def _zs(e):
        e.memset(zsrc[:], 0.0)
        e.memset(sgnc[:], -2147450880)
        e.memset(absc[:], 2147450879)
        e.memset(thsc[:], 0.5 / 64.0)
        e.memset(nege[:], -0.5e-8)
        e.memset(pose[:], 0.5e-8)
        e.memset(qeps[:], 2.9802322387695312e-08).then_inc(cg.h, 1)
    em("gpsimd", _zs); cg.n += 1
    zsrc_cg = cg.n
    def btile(w, kt, mt):
        o = (w * 4 + kt * 2 + mt) * 128
        return btw[:, o:o + 128]

    wait("tensor", s_cst, s_cst.n)
    def _rbc(e):
        e.matmul(pp[:, 7 * 512:7 * 512 + 1], onerow_s[:], rbc[0:1, 0:1],
                 start=True, stop=True).then_inc(ct.h, 1)
    em("tensor", _rbc); ct.n += 1
    wait("vector", ct, ct.n)
    def _rbcev(e):
        e.tensor_copy(rbc[:], pp[:, 7 * 512:7 * 512 + 1]).then_inc(cv.h, 1)
    em("vector", _rbcev); cv.n += 1
    rbc_cv = cv.n
    b7_user = cv.n

    zround = [0] * NZ
    zfree = [None] * NZ

    def z_load(slot, blk_row):
        if zfree[slot] is not None:
            wait("gpsimd", zfree[slot][0], zfree[slot][1])
        def _f(e, sl=slot, r0=blk_row * 128):
            e.dma_start(zring[:, sl * C:(sl + 1) * C], z[r0:r0 + 128, :]).then_inc(s_z[sl].h, 16)
        em("gpsimd", _f)
        s_z[slot].n += 16
        zround[slot] += 1

    def ln_block(zslot, dst, dsttile):
        wait("vector", s_z[zslot], 16 * zround[zslot])
        zv = zring[:, zslot * C:(zslot + 1) * C]
        def _st(e, zv=zv):
            ii = None
            for i in range(4):
                ii = e.bn_stats(stat[:, i * 6:(i + 1) * 6], zv[:, i * 512:(i + 1) * 512])
            ii.then_inc(cv.h, 1)
        em("vector", _st); cv.n += 1
        wait("vector", cv, cv.n)
        def _ag(e):
            e.bn_aggr(aggr[:], stat[:]).then_inc(cv.h, 1)
        em("vector", _ag); cv.n += 1
        wait("vector", cv, cv.n)
        def _ve(e):
            e.tensor_scalar_add(ivar[:, 0:1], aggr[:, 1:2], LN_EPS).then_inc(cv.h, 1)
        em("vector", _ve); cv.n += 1
        wait("scalar", cv, cv.n)
        def _sq(e):
            e.activation(ivar[:, 1:2], ivar[:, 0:1], AF.Sqrt).then_inc(cs.h, 1)
        em("scalar", _sq); cs.n += 1
        wait("vector", cs, cs.n)
        def _rc(e):
            e.reciprocal(ivar[:, 2:3], ivar[:, 1:2]).then_inc(cv.h, 1)
        em("vector", _rc); cv.n += 1
        wait("vector", cv, cv.n)
        def _nm(e):
            e.scalar_tensor_tensor(ivar[:, 3:4], aggr[:, 0:1], -1.0, ivar[:, 2:3],
                                   ALU.mult, ALU.mult).then_inc(cv.h, 1)
        em("vector", _nm); cv.n += 1
        wait("scalar", cv, cv.n)
        def _ln(e, d=dst, i=dsttile, zv=zv):
            e.activation(d[:, i * C:(i + 1) * C], zv, AF.Identity,
                         bias=ivar[:, 3:4], scale=ivar[:, 2:3]).then_inc(cs.h, 1)
        em("scalar", _ln); cs.n += 1
        return cs.n

    tb45 = [None]

    def transpose_block(src, srcoff, dst, dstoff, ln_cs):
        wait("tensor", cs, ln_cs)
        bank_wait("tensor", 2)
        bank_wait("tensor", 3)
        def _tp(e, s=src, o=srcoff):
            ii = None
            for nt in range(16):
                ii = e.transpose(ppb[:, 2 * 1024 + nt * 128: 2 * 1024 + (nt + 1) * 128],
                                 s[:, o + nt * 128: o + (nt + 1) * 128], idbs[:])
            ii.then_inc(ct.h, 1)
        em("tensor", _tp); ct.n += 1
        wait("vector", ct, ct.n)
        def _ev(e, d=dst, o=dstoff):
            e.tensor_copy(d[:, o:o + 2048], ppb[:, 2 * 1024: 2 * 1024 + 2048]).then_inc(cv.h, 1)
        em("vector", _ev); cv.n += 1
        bank_ev[2] = (cv, cv.n)
        bank_ev[3] = (cv, cv.n)
        return cv.n

    qchunks = []
    for dt in range(16):
        for cc in range(2 * H // QW):
            dst = sqg[cc * 8:(cc + 1) * 8, :, dt, :].rearrange("h p w -> p h w")
            qchunks.append((wguT[dt * 128:(dt + 1) * 128, cc * QW:(cc + 1) * QW], dst))
    for jt in range(H // 128):
        qchunks.append((wpT[jt * 128:(jt + 1) * 128, :],
                        sqp[jt * 128:(jt + 1) * 128, :]))
    NQ = len(qchunks)

    qs = {"w_round": [0] * 4, "qs_round": [0] * 4,
          "aw_cv": [None] * 4, "mask_cv": [None] * 4, "aw_user": [None] * 4,
          "m2_cv": [None] * 4, "wq_cg": [None] * 4, "th0": True, "tmp_free": 0,
          "gs_ct": [None] * 4,
          "gs_user": {4: (cv, b7_user), 5: (cv, b7_user), 6: (cv, b7_user), 7: (cv, b7_user)},
          "sp_n": [0] * 4, "bc_n": [0] * 4,
          "gs_pair_ct": {}, "wqin_user": [None] * 6, "w_round4": [0] * 6}

    def q_load(j):
        if j < 0 or j >= NQ:
            return
        w4 = j % 6
        if qs["wqin_user"][w4] is not None:
            wait("gpsimd", cv, qs["wqin_user"][w4])
        if qs["w_round4"][w4] > 0:
            wait("gpsimd", s_w[w4], 16 * qs["w_round4"][w4])
        def _ld(e, w4=w4, src=qchunks[j][0]):
            e.dma_start(wqin[:, w4 * QW:(w4 + 1) * QW], src).then_inc(s_w[w4].h, 16)
        em("gpsimd", _ld)
        s_w[w4].n += 16
        qs["w_round4"][w4] += 1

    def q_aw(j):
        if j < 0 or j >= NQ:
            return
        sl = j % 4
        w4 = j % 6
        wait("vector", s_w[w4], 16 * qs["w_round4"][w4])
        if qs["gs_ct"][sl] is not None:
            wait("vector", ct, qs["gs_ct"][sl])
        if qs["aw_user"][sl] is not None:
            wait("vector", cv, qs["aw_user"][sl])
        def _aw(e, sl=sl, w4=w4):
            e.tensor_scalar(awb.ap().bitcast(I32)[:, sl * QW // 2:(sl + 1) * QW // 2],
                            wqin.ap().bitcast(I32)[:, w4 * QW // 2:(w4 + 1) * QW // 2],
                            absc[:, 0:1], None, ALU.bitwise_and).then_inc(cv.h, 1)
        em("vector", _aw); cv.n += 1
        qs["aw_cv"][sl] = cv.n

    def q_gs(j, pairi):
        if j < 0 or j >= NQ:
            return
        sl = j % 4
        b0 = 4 + 2 * ((j + pairi) % 2)
        wait("tensor", cv, qs["aw_cv"][j % 4])
        for b in (b0, b0 + 1):
            u = qs["gs_user"][b]
            if u is not None:
                wait("tensor", u[0], u[1])
        def _gs(e, sl=sl, pairi=pairi, b0=b0):
            for i in (2 * pairi, 2 * pairi + 1):
                bnk = b0 + i % 2
                e.matmul(pp[0:2, bnk * 512:(bnk + 1) * 512], ind64s[:],
                         awb[:, sl * QW + i * 512: sl * QW + (i + 1) * 512],
                         start=True, stop=True).then_inc(ct.h, 1)
        em("tensor", _gs); ct.n += 2
        qs["gs_ct"][j % 4] = ct.n
        qs["gs_pair_ct"][(j, pairi)] = ct.n

    def q_th(j, pairi):
        if j < 0 or j >= NQ:
            return
        b0 = 4 + 2 * ((j + pairi) % 2)
        t2 = (j % 2) * 2 + pairi
        t2p = ((j + 1) % 2) * 2 + pairi
        if qs["th0"]:
            wait("scalar", cg, zsrc_cg)
            qs["th0"] = False
        wait("scalar", ct, qs["gs_pair_ct"][(j, pairi)])
        wait("scalar", s_tsp[t2p], 16 * qs["sp_n"][t2p])
        wait("scalar", cs, qs["tmp_free"])
        def _t1(e, b0=b0):
            e.activation(tmpth[:], pp[0:2, b0 * 512:(b0 + 2) * 512], AF.Relu,
                         bias=nege[0:2, 0:1], scale=thsc[0:2, 0:1]).then_inc(cs.h, 1)
        em("scalar", _t1); cs.n += 1
        wait("scalar", cs, cs.n)
        def _t2(e, pairi=pairi):
            e.activation(thr2[0:2, pairi * 1024:(pairi + 1) * 1024], tmpth[:], AF.Identity,
                         bias=pose[0:2, 0:1]).then_inc(cs.h, 1)
        em("scalar", _t2); cs.n += 1
        qs["tmp_free"] = cs.n
        qs["gs_user"][b0 + 0] = (cs, cs.n)
        qs["gs_user"][b0 + 1] = (cs, cs.n)
        wait("scalar", cs, cs.n)
        if qs["bc_n"][t2] > 0:
            wait("scalar", s_tbc[t2], 32 * qs["bc_n"][t2])
        def _sp(e, t2=t2, pairi=pairi):
            e.dma_start(thrh[0:2, t2 * 1024:(t2 + 1) * 1024],
                        thr2[0:2, pairi * 1024:(pairi + 1) * 1024]).then_inc(s_tsp[t2].h, 16)
        em("scalar", _sp)
        s_tsp[t2].n += 16
        qs["sp_n"][t2] += 1

    def q_bc(j, pairi):
        if j < 0 or j >= NQ:
            return
        sl = j % 2
        t2 = (j % 2) * 2 + pairi
        wait("sync", s_tsp[t2], 16 * qs["sp_n"][t2])
        if pairi == 0 and qs["wq_cg"][sl] is not None:
            wait("sync", cg, qs["wq_cg"][sl])
        def _bc(e, sl=sl, t2=t2, pairi=pairi):
            for g in range(2):
                e.dma_start(thrs[g * 64:(g + 1) * 64, sl * QW + pairi * 1024: sl * QW + (pairi + 1) * 1024],
                            thrh[g:g + 1, t2 * 1024:(t2 + 1) * 1024].broadcast_to([64, 1024])
                            ).then_inc(s_tbc[t2].h, 16)
        em("sync", _bc)
        s_tbc[t2].n += 32
        qs["bc_n"][t2] += 1

    def q_mask(j):
        if j < 0 or j >= NQ:
            return
        sl = j % 2
        mksl = j % 2
        for pi in (0, 1):
            wait("vector", s_tbc[sl * 2 + pi], 32 * qs["bc_n"][sl * 2 + pi])
        if qs["qs_round"][mksl] > 0:
            wait("vector", s_qs[mksl], 16 * qs["qs_round"][mksl])
        def _mk(e, sl=sl, awsl=j % 4, mksl=mksl):
            e.tensor_tensor(mkb[:, mksl * QW:(mksl + 1) * QW], awb[:, awsl * QW:(awsl + 1) * QW],
                            thrs[:, sl * QW:(sl + 1) * QW], ALU.is_gt).then_inc(cv.h, 1)
        em("vector", _mk); cv.n += 1
        qs["mask_cv"][sl] = cv.n
        qs["aw_user"][j % 4] = cv.n
        wait("vector", cv, cv.n)
        if qs["wq_cg"][sl] is not None:
            wait("vector", cg, qs["wq_cg"][sl])
        def _m2(e, sl=j % 2, w4=j % 6):
            e.scalar_tensor_tensor(m2b.ap().bitcast(I32)[:, sl * QW // 2:(sl + 1) * QW // 2],
                                   wqin.ap().bitcast(I32)[:, w4 * QW // 2:(w4 + 1) * QW // 2],
                                   sgnc[:, 0:1],
                                   mkb.ap().bitcast(I32)[:, sl * QW // 2:(sl + 1) * QW // 2],
                                   ALU.bitwise_and, ALU.bitwise_or).then_inc(cv.h, 1)
        em("vector", _m2); cv.n += 1
        qs["m2_cv"][j % 2] = cv.n
        qs["wqin_user"][j % 6] = cv.n

    def q_wq(j):
        if j < 0 or j >= NQ:
            return
        sl = j % 2
        wait("gpsimd", cv, qs["m2_cv"][j % 2])
        def _wq(e, sl=sl, m2sl=j % 2):
            e.tensor_mul(mkb[:, m2sl * QW:(m2sl + 1) * QW], m2b[:, m2sl * QW:(m2sl + 1) * QW],
                         thrs[:, sl * QW:(sl + 1) * QW]).then_inc(cg.h, 1)
        em("gpsimd", _wq); cg.n += 1
        qs["wq_cg"][sl] = cg.n
        is_gu = j < 128
        qeng = "scalar"
        wait(qeng, cg, qs["wq_cg"][sl])
        mksl = j % 2
        def _stq(e, mksl=mksl, dst=qchunks[j][1], is_gu=is_gu):
            if is_gu:
                e.dma_start(dst, mkb.ap().rearrange("p (s h w) -> p s h w", s=2, h=8)[:, mksl]
                            ).then_inc(s_qs[mksl].h, 16)
            else:
                e.dma_start(dst, mkb[:, mksl * QW:(mksl + 1) * QW]).then_inc(s_qs[mksl].h, 16)
        em(qeng, _stq)
        s_qs[mksl].n += 16
        qs["qs_round"][mksl] += 1

    def emit_quant(k):
        if k == 0:
            q_load(0)
            q_load(1)
        q_bc(k - 3, 0)
        q_bc(k - 3, 1)
        q_th(k - 2, 0)
        q_th(k - 2, 1)
        q_mask(k - 4)
        q_wq(k - 4)
        q_load(k + 2)
        q_aw(k)
        q_gs(k - 1, 0)
        q_gs(k - 1, 1)


    z_load(0, 0)
    z_load(1, 1)
    for _k in range(4):
        emit_quant(_k)
    ln_cs0 = ln_block(0, xln, 0)
    transpose_block(xln, 0 * C, xlnT, 0 * 2048, ln_cs0)
    ln_cs1 = ln_block(1, xln, 1)
    xT_cv = transpose_block(xln, 1 * C, xlnT, 1 * 2048, ln_cs1)
    xln_cs = cs.n

    wait("tensor", s_btw, s_btw.n)
    wait("tensor", cs, xln_cs)
    def mm_uw(widx, dst):
        for dt in range(2):
            for ncc in range(4):
                b = (dt * 4 + ncc) % 2
                bank_wait("tensor", b)
                def _mm(e, dt=dt, ncc=ncc, b=b, w=widx):
                    ii = None
                    for kt in range(2):
                        ii = e.matmul(bank(b), btile(w, kt, dt),
                                      xln[:, kt * C + ncc * 512: kt * C + (ncc + 1) * 512],
                                      start=(kt == 0), stop=(kt == 1))
                    ii.then_inc(ct.h, 1)
                em("tensor", _mm); ct.n += 1
                wait("scalar", ct, ct.n)
                def _ev(e, dt=dt, ncc=ncc, b=b, d=dst):
                    e.activation(d[:, dt * C + ncc * 512: dt * C + (ncc + 1) * 512],
                                 bank(b), AF.Identity).then_inc(cs.h, 1)
                em("scalar", _ev); cs.n += 1
                bank_ev[b] = (cs, cs.n)
    emit_quant(4)
    emit_quant(5)
    mm_uw(0, uu)
    mm_uw(1, ww)
    emit_quant(6)
    emit_quant(7)
    uw_cs = cs.n

    for nt in range(16):
        for mc in range(4):
            b = (nt * 4 + mc) % 4
            bank_wait("tensor", b)
            def _mm(e, nt=nt, mc=mc, b=b):
                ii = None
                for kt in range(2):
                    ii = e.matmul(bank(b), uu[:, kt * C + nt * 128: kt * C + (nt + 1) * 128],
                                  ww[:, kt * C + mc * 512: kt * C + (mc + 1) * 512],
                                  start=(kt == 0), stop=(kt == 1))
                ii.then_inc(ct.h, 1)
            em("tensor", _mm); ct.n += 1
            wait("scalar", ct, ct.n)
            def _ev(e, nt=nt, mc=mc, b=b):
                e.activation(att[:, nt * C + mc * 512: nt * C + (mc + 1) * 512],
                             bank(b), AF.Relu).then_inc(cs.h, 1)
            em("scalar", _ev); cs.n += 1
            bank_ev[b] = (cs, cs.n)
    att_cs = cs.n
    scores_ct = ct.n
    wait("gpsimd", cs, att_cs)
    wait("gpsimd", cg, zsrc_cg)
    def _dg(e):
        e.dma_start(att[127:128, 15 * C + 2047: 15 * C + 2048], zsrc[0:1, 0:1]).then_inc(s_dg.h, 16)
    em("gpsimd", _dg); s_dg.n += 16

    wait("tensor", cv, xT_cv)
    wait("tensor", s_dg, s_dg.n)
    xatt = uu
    for dt in range(2):
        for mc in range(4):
            b = (dt * 4 + mc) % 2
            bank_wait("tensor", b)
            def _mm(e, dt=dt, mc=mc, b=b):
                ii = None
                for nt in range(16):
                    ii = e.matmul(bank(b), xlnT[:, (dt * 16 + nt) * 128: (dt * 16 + nt + 1) * 128],
                                  att[:, nt * C + mc * 512: nt * C + (mc + 1) * 512],
                                  start=(nt == 0), stop=(nt == 15))
                ii.then_inc(ct.h, 1)
            em("tensor", _mm); ct.n += 1
            wait("scalar", ct, ct.n)
            wait("scalar", ct, scores_ct)
            def _ev(e, dt=dt, mc=mc, b=b):
                e.activation(xatt[:, dt * C + mc * 512: dt * C + (mc + 1) * 512],
                             bank(b), AF.Identity).then_inc(cs.h, 1)
            em("scalar", _ev); cs.n += 1
            bank_ev[b] = (cs, cs.n)
    xatt_cs = cs.n
    for _k in range(8, 12):
        emit_quant(_k)

    wait("gpsimd", cs, xln_cs)
    z_load(0, 0)
    z_load(1, 1)
    x1_round = [0, 0]
    wait("tensor", cs, xatt_cs)
    for it in range(2):
        for mc in range(4):
            b = mc
            bank_wait("tensor", b)
            def _mm(e, it=it, mc=mc, b=b):
                ii = None
                for kd in range(2):
                    ii = e.matmul(bank(b), btile(2, kd, it),
                                  xatt[:, kd * C + mc * 512: kd * C + (mc + 1) * 512],
                                  start=(kd == 0), stop=(kd == 1))
                ii.then_inc(ct.h, 1)
            em("tensor", _mm); ct.n += 1
            bank_ev[b] = None
        ct_x = ct.n
        wait("vector", s_z[it], 16 * zround[it])
        wait("vector", cv, cv.n)
        def _t1(e, it=it):
            e.tensor_add(t1b[:, 0:C], zring[:, it * C:(it + 1) * C],
                         xln[:, it * C:(it + 1) * C]).then_inc(cv.h, 1)
        em("vector", _t1); cv.n += 1
        wait("vector", cv, cv.n)
        wait("vector", ct, ct_x)
        def _x1(e, it=it):
            ii = None
            for mc in range(4):
                ii = e.scalar_tensor_tensor(
                    x1b[:, it * C + mc * 512: it * C + (mc + 1) * 512],
                    bank(mc), 1.0, t1b[:, mc * 512:(mc + 1) * 512],
                    ALU.bypass, ALU.add)
            ii.then_inc(cv.h, 1)
        em("vector", _x1); cv.n += 1
        for mc in range(4):
            bank_ev[mc] = (cv, cv.n)
        zfree[it] = (cv, cv.n)
        wait("scalar", cv, cv.n)
        def _sp(e, it=it):
            e.dma_start(x1s[it * 128:(it + 1) * 128, :],
                        x1b[:, it * C:(it + 1) * C]).then_inc(s_x1[it].h, 16)
        em("scalar", _sp)
        s_x1[it].n += 16
        x1_round[it] += 1

    lnslot = [None, None]

    def emit_yblock(yi):
        blk = 2 + yi
        zsl = blk % NZ
        sl = yi % 2
        z_load(zsl, blk)
        if lnslot[sl] is not None:
            wait("scalar", cv, lnslot[sl])
        yield
        ln_cs = ln_block(zsl, lnb, sl)
        yield
        lnT_cv = transpose_block(lnb, sl * C, lnT, sl * C, ln_cs)
        wait("tensor", cv, lnT_cv)
        yield
        wait("vector", s_z[zsl], 16 * zround[zsl])
        wait("vector", cs, ln_cs)
        if lnslot[sl] is not None:
            wait("vector", cv, lnslot[sl])
        def _t1(e, sl=sl, zsl=zsl):
            e.tensor_add(t1b[:, 0:C], zring[:, zsl * C:(zsl + 1) * C],
                         lnb[:, sl * C:(sl + 1) * C]).then_inc(cv.h, 1)
        em("vector", _t1); cv.n += 1
        t1cv = cv.n
        zfree[zsl] = (cv, cv.n)
        if x1_round[sl] > 0:
            wait("vector", s_x1[sl], 16 * x1_round[sl])
        for half in range(2):
            for q in range(2):
                bank_wait("tensor", q)
                def _mm(e, sl=sl, half=half, q=q):
                    ii = None
                    for nt in range(16):
                        mc = 2 * half + q
                        ii = e.matmul(bank(q), lnT[:, sl * C + nt * 128: sl * C + (nt + 1) * 128],
                                      att[:, nt * C + mc * 512: nt * C + (mc + 1) * 512],
                                      start=(nt == 0), stop=(nt == 15))
                    ii.then_inc(ct.h, 1)
                em("tensor", _mm); ct.n += 1
                ct_y = ct.n
                yield
                wait("vector", cv, t1cv)
                wait("vector", ct, ct_y)
                def _x1(e, sl=sl, half=half, q=q):
                    mc = 2 * half + q
                    e.scalar_tensor_tensor(
                        x1b[:, sl * C + mc * 512: sl * C + (mc + 1) * 512],
                        bank(q), rbc[:, 0:1], t1b[:, mc * 512:(mc + 1) * 512],
                        ALU.mult, ALU.add).then_inc(cv.h, 1)
                em("vector", _x1); cv.n += 1
                bank_ev[q] = (cv, cv.n)
        lnslot[sl] = cv.n
        wait("scalar", cv, cv.n)
        def _sp(e, sl=sl, blk=blk):
            e.dma_start(x1s[blk * 128:(blk + 1) * 128, :],
                        x1b[:, sl * C:(sl + 1) * C]).then_inc(s_x1[sl].h, 16)
        em("scalar", _sp)
        s_x1[sl].n += 16
        x1_round[sl] += 1

    NQI = NQ + 4
    qi = 12
    for yi in range(NYB):
        ql = min(NQI, 12 + (yi + 1) * (NQI - 12) // NYB)
        g = emit_yblock(yi)
        for _ in g:
            if qi < ql:
                emit_quant(qi)
                qi += 1
        while qi < ql:
            emit_quant(qi)
            qi += 1
    while qi < NQI:
        emit_quant(qi)
        qi += 1

    for eng in ENGS:
        def _b(e):
            e.drain().then_inc(bar.h, 1)
        em(eng, _b)
    bar.n += 5
    for eng in ENGS:
        wait(eng, bar, bar.n)
    wait("sync", s_x1[0], 16 * x1_round[0])
    wait("sync", s_x1[1], 16 * x1_round[1])
    for _qi in range(2):
        wait("sync", s_qs[_qi], 16 * qs["qs_round"][_qi])
    def _b2(e):
        e.drain().then_inc(bar.h, 1)
    em("sync", _b2); bar.n += 1
    for eng in ("tensor", "vector", "scalar", "gpsimd"):
        wait(eng, bar, bar.n)

    es.close()
    es2 = ExitStack()
    def sb2(name, p, f, dt=F32):
        return es2.enter_context(nc.sbuf_tensor(name, [p, f], dt))
    HRW = 256
    NHR = H // HRW
    hsil = sb2("hsil", 128, 64 * 512, BF16)
    ln2T = sb2("ln2T", 128, 16 * 512, BF16)
    x1c = sb2("x1c", 128, 2 * 4 * C, BF16)
    wgb = sb2("wgb", 128, 4 * 16 * HRW, BF16)
    wpb = sb2("wpb", 128, 4 * 1024, BF16)
    sgl = sb2("sgl", 128, 2 * 512)
    hsq2 = sb2("hsq2", 128, 2 * 512, BF16)
    xo = sb2("xo", 128, 2 * 1024)
    invv = sb2("invv", 1, 512)
    invc = sb2("invc", 128, 4)
    st2 = sb2("st2", 128, 24)
    ag2 = sb2("ag2", 128, 2)
    iv2 = sb2("iv2", 128, 4)
    ln2b = sb2("ln2b", 128, 4 * C, BF16)

    s_xc = [mk("s_xc0"), mk("s_xc1")]
    s_wg = [mk(f"s_wg{i}") for i in range(4)]
    s_wp = [mk(f"s_wp{i}") for i in range(4)]
    s_out = [mk(f"s_out{i}") for i in range(2)]

    fx = {"xc_cons": [None, None], "wg_cons": [None] * 4, "wp_cons": [None] * 4,
          "hs_cv": [None] * 64, "sg_free": [None, None], "hq_ct": [None, None],
          "hq_cv": [None, None], "xo_user": [None] * 8, "ss_ct": None}

    def xc_load(ch):
        sl = ch % 2
        if fx["xc_cons"][sl] is not None:
            wait("sync", cv, fx["xc_cons"][sl])
        def _f(e, sl=sl, ch=ch):
            e.dma_start(x1c.ap().rearrange("p (s b c) -> p s b c", s=2, b=4)[:, sl],
                        x1s.rearrange("(b p) c -> p b c", p=128)[:, ch * 4:(ch + 1) * 4]
                        ).then_inc(s_xc[sl].h, 16)
        em("sync", _f)
        s_xc[sl].n += 16

    def wg_load(hrg):
        for half, hr in ((0, hrg), (1, hrg + NHR)):
            sl = (hrg % 2) * 2 + half
            if fx["wg_cons"][sl] is not None:
                wait("sync", ct, fx["wg_cons"][sl])
            def _f(e, sl=sl, hr=hr):
                e.dma_start(wgb.ap().rearrange("p (s d h) -> p s d h", s=4, d=16)[:, sl],
                            sqg[hr]).then_inc(s_wg[sl].h, 16)
            em("sync", _f)
            s_wg[sl].n += 16

    def wp_load(jt, dchalf):
        sl = jt % 4
        if fx["wp_cons"][sl] is not None:
            wait("scalar", ct, fx["wp_cons"][sl])
        def _f(e, sl=sl, jt=jt, dchalf=dchalf):
            e.dma_start(wpb[:, sl * 1024:(sl + 1) * 1024],
                        sqp[jt * 128:(jt + 1) * 128, dchalf * 1024:(dchalf + 1) * 1024]
                        ).then_inc(s_wp[sl].h, 16)
        em("scalar", _f)
        s_wp[sl].n += 16

    ln_cs_of = [None] * 4
    tp_ct_of = [None] * 4

    def ln2_chain(ch, tb):
        xsl = ch % 2
        xv = x1c.ap().rearrange("p (s b c) -> p s b c", s=2, b=4)[:, xsl, tb]
        wait("vector", s_xc[xsl], 16 * (ch // 2 + 1))
        def _st(e, xv=xv):
            ii = None
            for i in range(4):
                ii = e.bn_stats(st2[:, i * 6:(i + 1) * 6], xv[:, i * 512:(i + 1) * 512])
            ii.then_inc(cv.h, 1)
        em("vector", _st); cv.n += 1
        wait("vector", cv, cv.n)
        def _ag(e):
            e.bn_aggr(ag2[:], st2[:]).then_inc(cv.h, 1)
        em("vector", _ag); cv.n += 1
        wait("vector", cv, cv.n)
        def _ve(e):
            e.tensor_scalar(iv2[:, 0:1], ag2[:, 1:2], LN_EPS, 0.25,
                            ALU.add, ALU.mult).then_inc(cv.h, 1)
        em("vector", _ve); cv.n += 1
        wait("scalar", cv, cv.n)
        def _sq(e):
            e.activation(iv2[:, 1:2], iv2[:, 0:1], AF.Sqrt).then_inc(cs.h, 1)
        em("scalar", _sq); cs.n += 1
        wait("vector", cs, cs.n)
        def _rc(e):
            e.reciprocal(iv2[:, 2:3], iv2[:, 1:2]).then_inc(cv.h, 1)
        em("vector", _rc); cv.n += 1
        wait("vector", cv, cv.n)
        def _nm(e):
            e.scalar_tensor_tensor(iv2[:, 3:4], ag2[:, 0:1], -1.0, iv2[:, 2:3],
                                   ALU.mult, ALU.mult).then_inc(cv.h, 1)
        em("vector", _nm); cv.n += 1
        wait("scalar", cv, cv.n)
        if tp_ct_of[tb] is not None:
            wait("scalar", ct, tp_ct_of[tb])
        def _ln(e, xv=xv, tb=tb):
            e.activation(ln2b[:, tb * C:(tb + 1) * C], xv, AF.Identity, bias=iv2[:, 3:4],
                         scale=iv2[:, 2:3]).then_inc(cs.h, 1)
        em("scalar", _ln); cs.n += 1
        ln_cs_of[tb] = cs.n

    def ln2_tp(tb):
        wait("tensor", cs, ln_cs_of[tb])
        if tb45[0] is not None:
            wait("tensor", tb45[0][0], tb45[0][1])
        def _tp(e, tb=tb):
            ii = None
            for dt in range(16):
                ii = e.transpose(ppb[:, 4 * 1024 + dt * 128: 4 * 1024 + (dt + 1) * 128],
                                 ln2b[:, tb * C + dt * 128: tb * C + (dt + 1) * 128], idbs[:])
            ii.then_inc(ct.h, 1)
        em("tensor", _tp); ct.n += 1
        tp_ct_of[tb] = ct.n
        wait("vector", ct, ct.n)
        def _ev(e, tb=tb):
            e.tensor_copy(ln2T.ap().rearrange("p (d t) -> p d t", d=16)[:, :, tb * 128:(tb + 1) * 128],
                          ppb[:, 4 * 1024: 4 * 1024 + 2048].rearrange("p (d t) -> p d t", d=16)
                          ).then_inc(cv.h, 1)
        em("vector", _ev); cv.n += 1
        tb45[0] = (cv, cv.n)

    xc_load(0)
    for tb in range(4):
        ln2_chain(0, tb)
    for ch in range(NCH):
        sl = ch % 2
        if ch + 1 < NCH:
            xc_load(ch + 1)
        for tb in range(4):
            ln2_tp(tb)
        ln2T_cv = cv.n

        pend_ss = [None]
        wait("tensor", cv, ln2T_cv)
        wg_load(0)
        for hrg in range(NHR):
            if hrg + 1 < NHR:
                wg_load(hrg + 1)
            if ch + 1 < NCH and hrg in (6, 12, 18, 24):
                ln2_chain(ch + 1, (hrg - 6) // 6)
            slg = (hrg % 2) * 2
            slu = slg + 1
            wait("tensor", s_wg[slg], s_wg[slg].n)
            wait("tensor", s_wg[slu], s_wg[slu].n)
            for hi in range(HRW // 128):
                jt = hrg * (HRW // 128) + hi
                gb = jt % 2
                if fx["sg_free"][gb] is not None:
                    wait("tensor", cs, fx["sg_free"][gb])
                if fx["hq_cv"][gb] is not None:
                    wait("tensor", cv, fx["hq_cv"][gb])
                def _mmg(e, slg=slg, hi=hi, gb=gb):
                    w = wgb.ap().rearrange("p (s d h) -> p s d h", s=4, d=16)[:, slg]
                    ii = None
                    for dt in range(16):
                        ii = e.matmul(bank(gb), w[:, dt, hi * 128:(hi + 1) * 128],
                                      ln2T[:, dt * 512:(dt + 1) * 512],
                                      start=(dt == 0), stop=(dt == 15))
                    ii.then_inc(ct.h, 1)
                em("tensor", _mmg); ct.n += 1
                def _mmu(e, slu=slu, hi=hi, gb=gb):
                    w = wgb.ap().rearrange("p (s d h) -> p s d h", s=4, d=16)[:, slu]
                    ii = None
                    for dt in range(16):
                        ii = e.matmul(bank(2 + gb), w[:, dt, hi * 128:(hi + 1) * 128],
                                      ln2T[:, dt * 512:(dt + 1) * 512],
                                      start=(dt == 0), stop=(dt == 15))
                    ii.then_inc(ct.h, 1)
                em("tensor", _mmu); ct.n += 1
                ct_pair = ct.n
                if hi == HRW // 128 - 1:
                    fx["wg_cons"][slg] = ct.n
                    fx["wg_cons"][slu] = ct.n
                wait("scalar", ct, ct_pair - 1)
                if fx["hq_cv"][gb] is not None:
                    wait("scalar", cv, fx["hq_cv"][gb])
                def _si(e, gb=gb):
                    e.activation(sgl[:, gb * 512:(gb + 1) * 512], bank(gb), AF.Silu
                                 ).then_inc(cs.h, 1)
                em("scalar", _si); cs.n += 1
                sg_cs = cs.n
                fx["sg_free"][gb] = cs.n
                wait("vector", cs, sg_cs)
                wait("vector", ct, ct_pair)
                def _hs(e, jt=jt, gb=gb):
                    e.tensor_mul(hsil[:, jt * 512:(jt + 1) * 512], sgl[:, gb * 512:(gb + 1) * 512],
                                 bank(2 + gb)).then_inc(cv.h, 1)
                em("vector", _hs); cv.n += 1
                hs_cv = cv.n
                fx["hs_cv"][jt] = hs_cv
                wait("vector", cv, hs_cv)
                if fx["hq_ct"][gb] is not None:
                    wait("vector", ct, fx["hq_ct"][gb])
                def _hq(e, jt=jt, gb=gb):
                    e.tensor_mul(hsq2[:, gb * 512:(gb + 1) * 512], hsil[:, jt * 512:(jt + 1) * 512],
                                 hsil[:, jt * 512:(jt + 1) * 512]).then_inc(cv.h, 1)
                em("vector", _hq); cv.n += 1
                fx["hq_cv"][gb] = cv.n
                def emit_ss(jt=jt, gb=gb, hqv=cv.n):
                    wait("tensor", cv, hqv)
                    def _ss(e, jt=jt, gb=gb):
                        e.matmul(pp[0:1, 6 * 512:7 * 512], ones1s[:], hsq2[:, gb * 512:(gb + 1) * 512],
                                 start=(jt == 0), stop=(jt == 63)).then_inc(ct.h, 1)
                    em("tensor", _ss); ct.n += 1
                    fx["hq_ct"][gb] = ct.n
                if pend_ss[0] is not None:
                    pend_ss[0]()
                pend_ss[0] = emit_ss
        if pend_ss[0] is not None:
            pend_ss[0]()
            pend_ss[0] = None
        gu_ct = ct.n

        wait("vector", ct, gu_ct)
        def _ms(e):
            e.tensor_scalar(invv[0:1, :], pp[0:1, 6 * 512:7 * 512], 0.25 / 8192.0,
                            2.9802322387695312e-08, ALU.mult, ALU.add).then_inc(cv.h, 1)
        em("vector", _ms); cv.n += 1
        wait("scalar", cv, cv.n)
        def _sq2(e):
            e.activation(invv[0:1, :], invv[0:1, :], AF.Sqrt).then_inc(cs.h, 1)
        em("scalar", _sq2); cs.n += 1
        wait("vector", cs, cs.n)
        def _rc2(e):
            e.reciprocal(invv[0:1, :], invv[0:1, :]).then_inc(cv.h, 1)
        em("vector", _rc2); cv.n += 1
        inv_cv = cv.n
        wait("tensor", cv, inv_cv)
        def _ti(e):
            ii = None
            for tb in range(4):
                ii = e.transpose(pp[:, 7 * 512 + tb: 7 * 512 + tb + 1],
                                 invv[0:1, tb * 128:(tb + 1) * 128], idfs[0:1, 0:1])
            ii.then_inc(ct.h, 1)
        em("tensor", _ti); ct.n += 1
        wait("vector", ct, ct.n)
        def _ic(e):
            e.tensor_copy(invc[:], pp[:, 7 * 512: 7 * 512 + 4]).then_inc(cv.h, 1)
        em("vector", _ic); cv.n += 1
        invc_cv = cv.n

        for dchalf in range(2):
            wait("tensor", cv, invc_cv)
            if dchalf == 1:
                for u in range(8):
                    if fx["xo_user"][u] is not None:
                        wait("tensor", cv, fx["xo_user"][u])
            wp_load(0, dchalf)
            wp_load(1, dchalf)
            for jt in range(64):
                if jt + 2 < 64:
                    wp_load(jt + 2, dchalf)
                wsl = jt % 4
                wait("tensor", s_wp[wsl], s_wp[wsl].n)
                wait("tensor", cv, fx["hs_cv"][jt])
                def _mm(e, jt=jt, wsl=wsl):
                    ii = None
                    for tb in range(4):
                        for dc in range(2):
                            ii = e.matmul(bank(tb * 2 + dc),
                                          hsil[:, jt * 512 + tb * 128: jt * 512 + (tb + 1) * 128],
                                          wpb[:, wsl * 1024 + dc * 512: wsl * 1024 + (dc + 1) * 512],
                                          start=(jt == 0), stop=(jt == 63))
                    ii.then_inc(ct.h, 1)
                em("tensor", _mm); ct.n += 1
                fx["wp_cons"][wsl] = ct.n
            mm2_ct = ct.n
            wait("vector", ct, mm2_ct)
            for tb in range(4):
                xsl = tb % 2
                if s_out[xsl].n > 0:
                    wait("vector", s_out[xsl], s_out[xsl].n)
                def _xo(e, tb=tb, sl=sl, dchalf=dchalf, xsl=xsl):
                    xv = x1c.ap().rearrange("p (s b c) -> p s b c", s=2, b=4)[:, sl, tb]
                    ii = None
                    for dc in range(2):
                        d0 = dchalf * 1024 + dc * 512
                        ii = e.scalar_tensor_tensor(
                            xo[:, xsl * 1024 + dc * 512: xsl * 1024 + (dc + 1) * 512],
                            bank(tb * 2 + dc), invc[:, tb:tb + 1], xv[:, d0:d0 + 512],
                            ALU.mult, ALU.add)
                    ii.then_inc(cv.h, 1)
                em("vector", _xo); cv.n += 1
                fx["xo_user"][dchalf * 4 + tb] = cv.n
                wait("gpsimd", cv, cv.n)
                def _ow(e, tb=tb, ch=ch, dchalf=dchalf, xsl=xsl):
                    e.dma_start(out[(ch * 4 + tb) * 128:(ch * 4 + tb + 1) * 128,
                                    dchalf * 1024:(dchalf + 1) * 1024],
                                xo[:, xsl * 1024:(xsl + 1) * 1024]).then_inc(s_out[xsl].h, 16)
                em("gpsimd", _ow)
                s_out[xsl].n += 16
        fx["xc_cons"][sl] = cv.n
        tb45[0] = (cv, cv.n)

    for q in range(2):
        wait("gpsimd", s_out[q], s_out[q].n)

    blk = es2.enter_context(nc.Block())
    for name, attr in (("sync", blk.sync), ("tensor", blk.tensor), ("vector", blk.vector),
                       ("scalar", blk.scalar), ("gpsimd", blk.gpsimd)):
        def _run(e, ops=streams[name]):
            for op in ops:
                op(e)
        attr(_run)
    es2.close()
    keep.close()
    sems.close()
    return nc


def kernel(L0_attn_Z_in, L0_attn_A_l, L0_attn_r_l, L0_attn_B_l, L0_attn_C_l,
           L0_ffn_w_gate_up, L0_ffn_w_proj):
    bf = ml_dtypes.bfloat16
    Z = np.asarray(L0_attn_Z_in, dtype=np.float32)
    A = np.ascontiguousarray(np.asarray(L0_attn_A_l, dtype=np.float32).T)
    B = np.ascontiguousarray(np.asarray(L0_attn_B_l, dtype=np.float32).T)
    Cm = np.ascontiguousarray(np.asarray(L0_attn_C_l, dtype=np.float32).T)
    r = np.asarray(L0_attn_r_l, dtype=np.float32).reshape(1)
    WguT = np.ascontiguousarray(np.asarray(L0_ffn_w_gate_up, dtype=np.float32).T)
    WpT = np.ascontiguousarray(np.asarray(L0_ffn_w_proj, dtype=np.float32).T)

    idf = np.eye(128, dtype=np.float32)
    idb = np.eye(128, dtype=np.float32).astype(bf)
    ind64 = np.zeros((128, 2), dtype=np.float32)
    ind64[0:64, 0] = 1.0
    ind64[64:128, 1] = 1.0
    ind2 = np.ascontiguousarray(ind64.T).astype(bf)
    ind64 = ind64.astype(bf)
    ones1 = np.ones((128, 1), dtype=np.float32).astype(bf)
    onerow = np.ones((1, 128), dtype=np.float32)

    nc = build_program(R=Z.shape[1])
    common = dict(bT=B, cT=Cm, aT=A, rl=r, wguT=WguT, wpT=WpT, idf=idf, idb=idb,
                  ind64=ind64, ind2=ind2, ones1=ones1, onerow=onerow)
    in_maps = [dict(common, z=np.ascontiguousarray(Z[b])) for b in range(Z.shape[0])]
    res = run_bass_kernel_spmd(nc, in_maps, core_ids=list(range(8)))
    return np.stack([np.asarray(res.results[b]["out"]) for b in range(Z.shape[0])], axis=0)


# version 41 (final)
# speedup vs baseline: 1.1211x; 1.0051x over previous
import numpy as np
import ml_dtypes
from contextlib import ExitStack

import concourse.bass as bass
import concourse.mybir as mybir
from concourse.bass_utils import run_bass_kernel_spmd

F32 = mybir.dt.float32
BF16 = mybir.dt.bfloat16
I16 = mybir.dt.int16
I32 = mybir.dt.int32
AF = mybir.ActivationFunctionType
ALU = mybir.AluOpType

C = 2048
D = 256
H = 8192
RMS_EPS = 1.1920928955078125e-07
LN_EPS = 1e-5
QW = 2048


class Ctr:
    def __init__(self, es, nc, name):
        self.h = es.enter_context(nc.semaphore(name))
        self.n = 0


def build_program(R=4096):
    NB = R // 128
    NYB = NB - 2
    NCH = R // 512
    nc = bass.Bass("TRN2", target_bir_lowering=False)

    z = nc.dram_tensor("z", [R, C], F32, kind="ExternalInput").ap()
    bT = nc.dram_tensor("bT", [D, D], F32, kind="ExternalInput").ap()
    cT = nc.dram_tensor("cT", [D, D], F32, kind="ExternalInput").ap()
    aT = nc.dram_tensor("aT", [D, D], F32, kind="ExternalInput").ap()
    rl = nc.dram_tensor("rl", [1], F32, kind="ExternalInput").ap()
    wguT = nc.dram_tensor("wguT", [C, 2 * H], F32, kind="ExternalInput").ap()
    wpT = nc.dram_tensor("wpT", [H, C], F32, kind="ExternalInput").ap()
    idf = nc.dram_tensor("idf", [128, 128], F32, kind="ExternalInput").ap()
    idb = nc.dram_tensor("idb", [128, 128], BF16, kind="ExternalInput").ap()
    ind64 = nc.dram_tensor("ind64", [128, 2], BF16, kind="ExternalInput").ap()
    ind2 = nc.dram_tensor("ind2", [2, 128], BF16, kind="ExternalInput").ap()
    ones1 = nc.dram_tensor("ones1", [128, 1], BF16, kind="ExternalInput").ap()
    onerow = nc.dram_tensor("onerow", [1, 128], F32, kind="ExternalInput").ap()
    out = nc.dram_tensor("out", [R, C], F32, kind="ExternalOutput").ap()
    sqg = nc.dram_tensor("sqg", [2 * H // 256, 128, 16, 256], BF16, kind="Internal").ap()
    sqp = nc.dram_tensor("sqp", [H, C], BF16, kind="Internal").ap()
    x1s = nc.dram_tensor("x1s", [R, C], BF16, kind="Internal").ap()
    thrh = nc.dram_tensor("thrh", [2, 4 * 1024], BF16, kind="Internal").ap()

    keep = ExitStack()
    def sbk(name, p, f, dt=F32):
        return keep.enter_context(nc.sbuf_tensor(name, [p, f], dt))

    pp = keep.enter_context(nc.psum_tensor("pp", [128, 4096], F32))
    ppb = pp.ap().bitcast(BF16)
    def bank(i):
        return pp[:, i * 512:(i + 1) * 512]

    idfs = sbk("idfs", 128, 128)
    idbs = sbk("idbs", 128, 128, BF16)
    ind64s = sbk("ind64s", 128, 2, BF16)
    ind2s = sbk("ind2s", 2, 128, BF16)
    ones1s = sbk("ones1s", 128, 1, BF16)
    onerow_s = sbk("onerow_s", 1, 128)
    rbc = sbk("rbc", 128, 1)
    zsrc = sbk("zsrc", 1, 8, BF16)
    sgnc = sbk("sgnc", 128, 1, I32)
    absc = sbk("absc", 128, 1, I32)
    qeps = sbk("qeps", 1, 1)
    thsc = sbk("thsc", 128, 1)
    nege = sbk("nege", 128, 1)
    pose = sbk("pose", 128, 1)

    es = ExitStack()
    def sb(name, p, f, dt=F32):
        return es.enter_context(nc.sbuf_tensor(name, [p, f], dt))

    att = sb("att", 128, 16 * C, BF16)
    xln = sb("xln", 128, 2 * C, BF16)
    xlnT = sb("xlnT", 128, 2 * 16 * 128, BF16)
    uu = sb("uu", 128, 2 * C, BF16)
    ww = sb("ww", 128, 2 * C, BF16)
    NZ = 2
    zring = sb("zring", 128, NZ * C, BF16)
    lnb = sb("lnb", 128, 2 * C, BF16)
    lnT = sb("lnT", 128, 2 * C, BF16)
    x1b = sb("x1b", 128, 2 * C, BF16)
    t1b = sb("t1b", 128, C, BF16)
    btw = sb("btw", 128, 3 * 2 * 2 * 128, BF16)
    stat = sb("stat", 128, 24)
    aggr = sb("aggr", 128, 2)
    ivar = sb("ivar", 128, 4)
    QSL = 4
    wqin = sb("wqin", 128, 6 * QW, BF16)
    awb = sb("awb", 128, 4 * QW, BF16)
    mkb = sb("mkb", 128, 2 * QW, BF16)
    m2b = sb("m2b", 128, 2 * QW, BF16)
    thrs = sb("thrs", 128, 2 * QW, BF16)
    thr2 = sb("thr2", 2, 2 * 1024, BF16)
    tmpth = sb("tmpth", 2, 1024, BF16)

    sems = ExitStack()
    def mk(name):
        return Ctr(sems, nc, name)
    s_z = [mk(f"s_z{i}") for i in range(NZ)]
    s_cst = mk("s_cst")
    s_btw = mk("s_btw")
    s_w = [mk(f"s_w{i}") for i in range(6)]
    s_qs = [mk(f"s_qs{i}") for i in range(2)]
    s_tsp = [mk(f"s_tsp{i}") for i in range(4)]
    s_tbc = [mk(f"s_tbc{i}") for i in range(4)]
    s_x1 = [mk(f"s_x1{i}") for i in range(2)]
    s_dg = mk("s_dg")
    cv = mk("cv"); cs = mk("cs"); ct = mk("ct"); cg = mk("cg")
    bar = mk("bar")

    ENGS = ("sync", "tensor", "vector", "scalar", "gpsimd")
    streams = {k: [] for k in ENGS}
    def em(eng, fn):
        streams[eng].append(fn)
    def wait(eng, ctr, val):
        if val is None or val <= 0:
            return
        streams[eng].append(lambda e, h=ctr.h, v=val: e.wait_ge(h, v))

    bank_ev = {}
    def bank_wait(eng, b):
        ev = bank_ev.get(b)
        if ev is not None:
            wait(eng, ev[0], ev[1])

    def _cst(e):
        e.dma_start(idfs[:], idf[:]).then_inc(s_cst.h, 16)
        e.dma_start(idbs[:], idb[:]).then_inc(s_cst.h, 16)
        e.dma_start(ind64s[:], ind64[:]).then_inc(s_cst.h, 16)
        e.dma_start(ind2s[:], ind2[:]).then_inc(s_cst.h, 16)
        e.dma_start(ones1s[:], ones1[:]).then_inc(s_cst.h, 16)
        e.dma_start(onerow_s[:], onerow[:]).then_inc(s_cst.h, 16)
        e.dma_start(rbc[0:1, 0:1], rl[0:1]).then_inc(s_cst.h, 16)
    em("sync", _cst); s_cst.n += 112

    def _btwload(e):
        for i, w in enumerate((bT, cT, aT)):
            e.dma_start(
                btw.ap().rearrange("p (w a b q) -> p w a b q", w=3, a=2, b=2)[:, i],
                w.rearrange("(a p) (b q) -> p a b q", p=128, q=128),
            ).then_inc(s_btw.h, 16)
    em("gpsimd", _btwload); s_btw.n += 48
    def _zs(e):
        e.memset(zsrc[:], 0.0)
        e.memset(sgnc[:], -2147450880)
        e.memset(absc[:], 2147450879)
        e.memset(thsc[:], 0.5 / 64.0)
        e.memset(nege[:], -0.5e-8)
        e.memset(pose[:], 0.5e-8)
        e.memset(qeps[:], 2.9802322387695312e-08).then_inc(cg.h, 1)
    em("gpsimd", _zs); cg.n += 1
    zsrc_cg = cg.n
    def btile(w, kt, mt):
        o = (w * 4 + kt * 2 + mt) * 128
        return btw[:, o:o + 128]

    wait("tensor", s_cst, s_cst.n)
    def _rbc(e):
        e.matmul(pp[:, 7 * 512:7 * 512 + 1], onerow_s[:], rbc[0:1, 0:1],
                 start=True, stop=True).then_inc(ct.h, 1)
    em("tensor", _rbc); ct.n += 1
    wait("vector", ct, ct.n)
    def _rbcev(e):
        e.tensor_copy(rbc[:], pp[:, 7 * 512:7 * 512 + 1]).then_inc(cv.h, 1)
    em("vector", _rbcev); cv.n += 1
    rbc_cv = cv.n
    b7_user = cv.n

    zround = [0] * NZ
    zfree = [None] * NZ

    def z_load(slot, blk_row):
        if zfree[slot] is not None:
            wait("gpsimd", zfree[slot][0], zfree[slot][1])
        def _f(e, sl=slot, r0=blk_row * 128):
            e.dma_start(zring[:, sl * C:(sl + 1) * C], z[r0:r0 + 128, :]).then_inc(s_z[sl].h, 16)
        em("gpsimd", _f)
        s_z[slot].n += 16
        zround[slot] += 1

    def ln_block(zslot, dst, dsttile):
        wait("vector", s_z[zslot], 16 * zround[zslot])
        zv = zring[:, zslot * C:(zslot + 1) * C]
        def _st(e, zv=zv):
            ii = None
            for i in range(4):
                ii = e.bn_stats(stat[:, i * 6:(i + 1) * 6], zv[:, i * 512:(i + 1) * 512])
            ii.then_inc(cv.h, 1)
        em("vector", _st); cv.n += 1
        wait("vector", cv, cv.n)
        def _ag(e):
            e.bn_aggr(aggr[:], stat[:]).then_inc(cv.h, 1)
        em("vector", _ag); cv.n += 1
        wait("vector", cv, cv.n)
        def _ve(e):
            e.tensor_scalar_add(ivar[:, 0:1], aggr[:, 1:2], LN_EPS).then_inc(cv.h, 1)
        em("vector", _ve); cv.n += 1
        wait("scalar", cv, cv.n)
        def _sq(e):
            e.activation(ivar[:, 1:2], ivar[:, 0:1], AF.Sqrt).then_inc(cs.h, 1)
        em("scalar", _sq); cs.n += 1
        wait("vector", cs, cs.n)
        def _rc(e):
            e.reciprocal(ivar[:, 2:3], ivar[:, 1:2]).then_inc(cv.h, 1)
        em("vector", _rc); cv.n += 1
        wait("vector", cv, cv.n)
        def _nm(e):
            e.scalar_tensor_tensor(ivar[:, 3:4], aggr[:, 0:1], -1.0, ivar[:, 2:3],
                                   ALU.mult, ALU.mult).then_inc(cv.h, 1)
        em("vector", _nm); cv.n += 1
        wait("scalar", cv, cv.n)
        def _ln(e, d=dst, i=dsttile, zv=zv):
            e.activation(d[:, i * C:(i + 1) * C], zv, AF.Identity,
                         bias=ivar[:, 3:4], scale=ivar[:, 2:3]).then_inc(cs.h, 1)
        em("scalar", _ln); cs.n += 1
        return cs.n

    tb45 = [None]

    def transpose_block(src, srcoff, dst, dstoff, ln_cs):
        wait("tensor", cs, ln_cs)
        bank_wait("tensor", 2)
        bank_wait("tensor", 3)
        def _tp(e, s=src, o=srcoff):
            ii = None
            for nt in range(16):
                ii = e.transpose(ppb[:, 2 * 1024 + nt * 128: 2 * 1024 + (nt + 1) * 128],
                                 s[:, o + nt * 128: o + (nt + 1) * 128], idbs[:])
            ii.then_inc(ct.h, 1)
        em("tensor", _tp); ct.n += 1
        wait("vector", ct, ct.n)
        def _ev(e, d=dst, o=dstoff):
            e.tensor_copy(d[:, o:o + 2048], ppb[:, 2 * 1024: 2 * 1024 + 2048]).then_inc(cv.h, 1)
        em("vector", _ev); cv.n += 1
        bank_ev[2] = (cv, cv.n)
        bank_ev[3] = (cv, cv.n)
        return cv.n

    qchunks = []
    for dt in range(16):
        for cc in range(2 * H // QW):
            dst = sqg[cc * 8:(cc + 1) * 8, :, dt, :].rearrange("h p w -> p h w")
            qchunks.append((wguT[dt * 128:(dt + 1) * 128, cc * QW:(cc + 1) * QW], dst))
    for jt in range(H // 128):
        qchunks.append((wpT[jt * 128:(jt + 1) * 128, :],
                        sqp[jt * 128:(jt + 1) * 128, :]))
    NQ = len(qchunks)

    qs = {"w_round": [0] * 4, "qs_round": [0] * 4,
          "aw_cv": [None] * 4, "mask_cv": [None] * 4, "aw_user": [None] * 4,
          "m2_cv": [None] * 4, "wq_cg": [None] * 4, "th0": True, "tmp_free": 0,
          "gs_ct": [None] * 4,
          "gs_user": {4: (cv, b7_user), 5: (cv, b7_user), 6: (cv, b7_user), 7: (cv, b7_user)},
          "sp_n": [0] * 4, "bc_n": [0] * 4,
          "gs_pair_ct": {}, "wqin_user": [None] * 6, "w_round4": [0] * 6}

    def q_load(j):
        if j < 0 or j >= NQ:
            return
        w4 = j % 6
        if qs["wqin_user"][w4] is not None:
            wait("gpsimd", cv, qs["wqin_user"][w4])
        if qs["w_round4"][w4] > 0:
            wait("gpsimd", s_w[w4], 16 * qs["w_round4"][w4])
        def _ld(e, w4=w4, src=qchunks[j][0]):
            e.dma_start(wqin[:, w4 * QW:(w4 + 1) * QW], src).then_inc(s_w[w4].h, 16)
        em("gpsimd", _ld)
        s_w[w4].n += 16
        qs["w_round4"][w4] += 1

    def q_aw(j):
        if j < 0 or j >= NQ:
            return
        sl = j % 4
        w4 = j % 6
        wait("vector", s_w[w4], 16 * qs["w_round4"][w4])
        if qs["gs_ct"][sl] is not None:
            wait("vector", ct, qs["gs_ct"][sl])
        if qs["aw_user"][sl] is not None:
            wait("vector", cv, qs["aw_user"][sl])
        def _aw(e, sl=sl, w4=w4):
            e.tensor_scalar(awb.ap().bitcast(I32)[:, sl * QW // 2:(sl + 1) * QW // 2],
                            wqin.ap().bitcast(I32)[:, w4 * QW // 2:(w4 + 1) * QW // 2],
                            absc[:, 0:1], None, ALU.bitwise_and).then_inc(cv.h, 1)
        em("vector", _aw); cv.n += 1
        qs["aw_cv"][sl] = cv.n

    def q_gs(j, pairi):
        if j < 0 or j >= NQ:
            return
        sl = j % 4
        b0 = 4 + 2 * ((j + pairi) % 2)
        wait("tensor", cv, qs["aw_cv"][j % 4])
        for b in (b0, b0 + 1):
            u = qs["gs_user"][b]
            if u is not None:
                wait("tensor", u[0], u[1])
        def _gs(e, sl=sl, pairi=pairi, b0=b0):
            for i in (2 * pairi, 2 * pairi + 1):
                bnk = b0 + i % 2
                e.matmul(pp[0:2, bnk * 512:(bnk + 1) * 512], ind64s[:],
                         awb[:, sl * QW + i * 512: sl * QW + (i + 1) * 512],
                         start=True, stop=True).then_inc(ct.h, 1)
        em("tensor", _gs); ct.n += 2
        qs["gs_ct"][j % 4] = ct.n
        qs["gs_pair_ct"][(j, pairi)] = ct.n

    def q_th(j, pairi):
        if j < 0 or j >= NQ:
            return
        b0 = 4 + 2 * ((j + pairi) % 2)
        t2 = (j % 2) * 2 + pairi
        t2p = ((j + 1) % 2) * 2 + pairi
        if qs["th0"]:
            wait("scalar", cg, zsrc_cg)
            qs["th0"] = False
        wait("scalar", ct, qs["gs_pair_ct"][(j, pairi)])
        wait("scalar", s_tsp[t2p], 16 * qs["sp_n"][t2p])
        wait("scalar", cs, qs["tmp_free"])
        def _t1(e, b0=b0):
            e.activation(tmpth[:], pp[0:2, b0 * 512:(b0 + 2) * 512], AF.Relu,
                         bias=nege[0:2, 0:1], scale=thsc[0:2, 0:1]).then_inc(cs.h, 1)
        em("scalar", _t1); cs.n += 1
        wait("scalar", cs, cs.n)
        def _t2(e, pairi=pairi):
            e.activation(thr2[0:2, pairi * 1024:(pairi + 1) * 1024], tmpth[:], AF.Identity,
                         bias=pose[0:2, 0:1]).then_inc(cs.h, 1)
        em("scalar", _t2); cs.n += 1
        qs["tmp_free"] = cs.n
        qs["gs_user"][b0 + 0] = (cs, cs.n)
        qs["gs_user"][b0 + 1] = (cs, cs.n)
        wait("scalar", cs, cs.n)
        if qs["bc_n"][t2] > 0:
            wait("scalar", s_tbc[t2], 32 * qs["bc_n"][t2])
        def _sp(e, t2=t2, pairi=pairi):
            e.dma_start(thrh[0:2, t2 * 1024:(t2 + 1) * 1024],
                        thr2[0:2, pairi * 1024:(pairi + 1) * 1024]).then_inc(s_tsp[t2].h, 16)
        em("scalar", _sp)
        s_tsp[t2].n += 16
        qs["sp_n"][t2] += 1

    def q_bc(j, pairi):
        if j < 0 or j >= NQ:
            return
        sl = j % 2
        t2 = (j % 2) * 2 + pairi
        wait("gpsimd", s_tsp[t2], 16 * qs["sp_n"][t2])
        if pairi == 0 and qs["wq_cg"][sl] is not None:
            wait("gpsimd", cg, qs["wq_cg"][sl])
        def _bc(e, sl=sl, t2=t2, pairi=pairi):
            for g in range(2):
                e.dma_start(thrs[g * 64:(g + 1) * 64, sl * QW + pairi * 1024: sl * QW + (pairi + 1) * 1024],
                            thrh[g:g + 1, t2 * 1024:(t2 + 1) * 1024].broadcast_to([64, 1024])
                            ).then_inc(s_tbc[t2].h, 16)
        em("gpsimd", _bc)
        s_tbc[t2].n += 32
        qs["bc_n"][t2] += 1

    def q_mask(j):
        if j < 0 or j >= NQ:
            return
        sl = j % 2
        mksl = j % 2
        for pi in (0, 1):
            wait("vector", s_tbc[sl * 2 + pi], 32 * qs["bc_n"][sl * 2 + pi])
        if qs["qs_round"][mksl] > 0:
            wait("vector", s_qs[mksl], 16 * qs["qs_round"][mksl])
        def _mk(e, sl=sl, awsl=j % 4, mksl=mksl):
            e.tensor_tensor(mkb[:, mksl * QW:(mksl + 1) * QW], awb[:, awsl * QW:(awsl + 1) * QW],
                            thrs[:, sl * QW:(sl + 1) * QW], ALU.is_gt).then_inc(cv.h, 1)
        em("vector", _mk); cv.n += 1
        qs["mask_cv"][sl] = cv.n
        qs["aw_user"][j % 4] = cv.n
        wait("vector", cv, cv.n)
        if qs["wq_cg"][sl] is not None:
            wait("vector", cg, qs["wq_cg"][sl])
        def _m2(e, sl=j % 2, w4=j % 6):
            e.scalar_tensor_tensor(m2b.ap().bitcast(I32)[:, sl * QW // 2:(sl + 1) * QW // 2],
                                   wqin.ap().bitcast(I32)[:, w4 * QW // 2:(w4 + 1) * QW // 2],
                                   sgnc[:, 0:1],
                                   mkb.ap().bitcast(I32)[:, sl * QW // 2:(sl + 1) * QW // 2],
                                   ALU.bitwise_and, ALU.bitwise_or).then_inc(cv.h, 1)
        em("vector", _m2); cv.n += 1
        qs["m2_cv"][j % 2] = cv.n
        qs["wqin_user"][j % 6] = cv.n

    def q_wq(j):
        if j < 0 or j >= NQ:
            return
        sl = j % 2
        wait("gpsimd", cv, qs["m2_cv"][j % 2])
        def _wq(e, sl=sl, m2sl=j % 2):
            e.tensor_mul(mkb[:, m2sl * QW:(m2sl + 1) * QW], m2b[:, m2sl * QW:(m2sl + 1) * QW],
                         thrs[:, sl * QW:(sl + 1) * QW]).then_inc(cg.h, 1)
        em("gpsimd", _wq); cg.n += 1
        qs["wq_cg"][sl] = cg.n
        is_gu = j < 128
        qeng = "scalar" if is_gu else "sync"
        wait(qeng, cg, qs["wq_cg"][sl])
        mksl = j % 2
        def _stq(e, mksl=mksl, dst=qchunks[j][1], is_gu=is_gu):
            if is_gu:
                e.dma_start(dst, mkb.ap().rearrange("p (s h w) -> p s h w", s=2, h=8)[:, mksl]
                            ).then_inc(s_qs[mksl].h, 16)
            else:
                e.dma_start(dst, mkb[:, mksl * QW:(mksl + 1) * QW]).then_inc(s_qs[mksl].h, 16)
        em(qeng, _stq)
        s_qs[mksl].n += 16
        qs["qs_round"][mksl] += 1

    def emit_quant(k):
        if k == 0:
            q_load(0)
            q_load(1)
        q_bc(k - 3, 0)
        q_bc(k - 3, 1)
        q_th(k - 2, 0)
        q_th(k - 2, 1)
        q_mask(k - 4)
        q_wq(k - 4)
        q_load(k + 2)
        q_aw(k)
        q_gs(k - 1, 0)
        q_gs(k - 1, 1)


    z_load(0, 0)
    z_load(1, 1)
    for _k in range(4):
        emit_quant(_k)
    ln_cs0 = ln_block(0, xln, 0)
    transpose_block(xln, 0 * C, xlnT, 0 * 2048, ln_cs0)
    ln_cs1 = ln_block(1, xln, 1)
    xT_cv = transpose_block(xln, 1 * C, xlnT, 1 * 2048, ln_cs1)
    xln_cs = cs.n

    wait("tensor", s_btw, s_btw.n)
    wait("tensor", cs, xln_cs)
    def mm_uw(widx, dst):
        for dt in range(2):
            for ncc in range(4):
                b = (dt * 4 + ncc) % 2
                bank_wait("tensor", b)
                def _mm(e, dt=dt, ncc=ncc, b=b, w=widx):
                    ii = None
                    for kt in range(2):
                        ii = e.matmul(bank(b), btile(w, kt, dt),
                                      xln[:, kt * C + ncc * 512: kt * C + (ncc + 1) * 512],
                                      start=(kt == 0), stop=(kt == 1))
                    ii.then_inc(ct.h, 1)
                em("tensor", _mm); ct.n += 1
                wait("scalar", ct, ct.n)
                def _ev(e, dt=dt, ncc=ncc, b=b, d=dst):
                    e.activation(d[:, dt * C + ncc * 512: dt * C + (ncc + 1) * 512],
                                 bank(b), AF.Identity).then_inc(cs.h, 1)
                em("scalar", _ev); cs.n += 1
                bank_ev[b] = (cs, cs.n)
    emit_quant(4)
    emit_quant(5)
    mm_uw(0, uu)
    mm_uw(1, ww)
    emit_quant(6)
    emit_quant(7)
    uw_cs = cs.n

    for nt in range(16):
        for mc in range(4):
            b = (nt * 4 + mc) % 4
            bank_wait("tensor", b)
            def _mm(e, nt=nt, mc=mc, b=b):
                ii = None
                for kt in range(2):
                    ii = e.matmul(bank(b), uu[:, kt * C + nt * 128: kt * C + (nt + 1) * 128],
                                  ww[:, kt * C + mc * 512: kt * C + (mc + 1) * 512],
                                  start=(kt == 0), stop=(kt == 1))
                ii.then_inc(ct.h, 1)
            em("tensor", _mm); ct.n += 1
            wait("scalar", ct, ct.n)
            def _ev(e, nt=nt, mc=mc, b=b):
                e.activation(att[:, nt * C + mc * 512: nt * C + (mc + 1) * 512],
                             bank(b), AF.Relu).then_inc(cs.h, 1)
            em("scalar", _ev); cs.n += 1
            bank_ev[b] = (cs, cs.n)
    att_cs = cs.n
    scores_ct = ct.n
    wait("gpsimd", cs, att_cs)
    wait("gpsimd", cg, zsrc_cg)
    def _dg(e):
        e.dma_start(att[127:128, 15 * C + 2047: 15 * C + 2048], zsrc[0:1, 0:1]).then_inc(s_dg.h, 16)
    em("gpsimd", _dg); s_dg.n += 16

    wait("tensor", cv, xT_cv)
    wait("tensor", s_dg, s_dg.n)
    xatt = uu
    for dt in range(2):
        for mc in range(4):
            b = (dt * 4 + mc) % 2
            bank_wait("tensor", b)
            def _mm(e, dt=dt, mc=mc, b=b):
                ii = None
                for nt in range(16):
                    ii = e.matmul(bank(b), xlnT[:, (dt * 16 + nt) * 128: (dt * 16 + nt + 1) * 128],
                                  att[:, nt * C + mc * 512: nt * C + (mc + 1) * 512],
                                  start=(nt == 0), stop=(nt == 15))
                ii.then_inc(ct.h, 1)
            em("tensor", _mm); ct.n += 1
            wait("scalar", ct, ct.n)
            wait("scalar", ct, scores_ct)
            def _ev(e, dt=dt, mc=mc, b=b):
                e.activation(xatt[:, dt * C + mc * 512: dt * C + (mc + 1) * 512],
                             bank(b), AF.Identity).then_inc(cs.h, 1)
            em("scalar", _ev); cs.n += 1
            bank_ev[b] = (cs, cs.n)
    xatt_cs = cs.n
    for _k in range(8, 12):
        emit_quant(_k)

    wait("gpsimd", cs, xln_cs)
    z_load(0, 0)
    z_load(1, 1)
    x1_round = [0, 0]
    wait("tensor", cs, xatt_cs)
    for it in range(2):
        for mc in range(4):
            b = mc
            bank_wait("tensor", b)
            def _mm(e, it=it, mc=mc, b=b):
                ii = None
                for kd in range(2):
                    ii = e.matmul(bank(b), btile(2, kd, it),
                                  xatt[:, kd * C + mc * 512: kd * C + (mc + 1) * 512],
                                  start=(kd == 0), stop=(kd == 1))
                ii.then_inc(ct.h, 1)
            em("tensor", _mm); ct.n += 1
            bank_ev[b] = None
        ct_x = ct.n
        wait("vector", s_z[it], 16 * zround[it])
        wait("vector", cv, cv.n)
        def _t1(e, it=it):
            e.tensor_add(t1b[:, 0:C], zring[:, it * C:(it + 1) * C],
                         xln[:, it * C:(it + 1) * C]).then_inc(cv.h, 1)
        em("vector", _t1); cv.n += 1
        wait("vector", cv, cv.n)
        wait("vector", ct, ct_x)
        def _x1(e, it=it):
            ii = None
            for mc in range(4):
                ii = e.scalar_tensor_tensor(
                    x1b[:, it * C + mc * 512: it * C + (mc + 1) * 512],
                    bank(mc), 1.0, t1b[:, mc * 512:(mc + 1) * 512],
                    ALU.bypass, ALU.add)
            ii.then_inc(cv.h, 1)
        em("vector", _x1); cv.n += 1
        for mc in range(4):
            bank_ev[mc] = (cv, cv.n)
        zfree[it] = (cv, cv.n)
        wait("sync", cv, cv.n)
        def _sp(e, it=it):
            e.dma_start(x1s[it * 128:(it + 1) * 128, :],
                        x1b[:, it * C:(it + 1) * C]).then_inc(s_x1[it].h, 16)
        em("sync", _sp)
        s_x1[it].n += 16
        x1_round[it] += 1

    lnslot = [None, None]

    def emit_yblock(yi):
        blk = 2 + yi
        zsl = blk % NZ
        sl = yi % 2
        z_load(zsl, blk)
        if lnslot[sl] is not None:
            wait("scalar", cv, lnslot[sl])
        yield
        ln_cs = ln_block(zsl, lnb, sl)
        yield
        lnT_cv = transpose_block(lnb, sl * C, lnT, sl * C, ln_cs)
        wait("tensor", cv, lnT_cv)
        yield
        wait("vector", s_z[zsl], 16 * zround[zsl])
        wait("vector", cs, ln_cs)
        if lnslot[sl] is not None:
            wait("vector", cv, lnslot[sl])
        def _t1(e, sl=sl, zsl=zsl):
            e.tensor_add(t1b[:, 0:C], zring[:, zsl * C:(zsl + 1) * C],
                         lnb[:, sl * C:(sl + 1) * C]).then_inc(cv.h, 1)
        em("vector", _t1); cv.n += 1
        t1cv = cv.n
        zfree[zsl] = (cv, cv.n)
        if x1_round[sl] > 0:
            wait("vector", s_x1[sl], 16 * x1_round[sl])
        for half in range(2):
            for q in range(2):
                bank_wait("tensor", q)
                def _mm(e, sl=sl, half=half, q=q):
                    ii = None
                    for nt in range(16):
                        mc = 2 * half + q
                        ii = e.matmul(bank(q), lnT[:, sl * C + nt * 128: sl * C + (nt + 1) * 128],
                                      att[:, nt * C + mc * 512: nt * C + (mc + 1) * 512],
                                      start=(nt == 0), stop=(nt == 15))
                    ii.then_inc(ct.h, 1)
                em("tensor", _mm); ct.n += 1
                ct_y = ct.n
                yield
                wait("vector", cv, t1cv)
                wait("vector", ct, ct_y)
                def _x1(e, sl=sl, half=half, q=q):
                    mc = 2 * half + q
                    e.scalar_tensor_tensor(
                        x1b[:, sl * C + mc * 512: sl * C + (mc + 1) * 512],
                        bank(q), rbc[:, 0:1], t1b[:, mc * 512:(mc + 1) * 512],
                        ALU.mult, ALU.add).then_inc(cv.h, 1)
                em("vector", _x1); cv.n += 1
                bank_ev[q] = (cv, cv.n)
        lnslot[sl] = cv.n
        wait("sync", cv, cv.n)
        def _sp(e, sl=sl, blk=blk):
            e.dma_start(x1s[blk * 128:(blk + 1) * 128, :],
                        x1b[:, sl * C:(sl + 1) * C]).then_inc(s_x1[sl].h, 16)
        em("sync", _sp)
        s_x1[sl].n += 16
        x1_round[sl] += 1

    NQI = NQ + 4
    qi = 12
    for yi in range(NYB):
        ql = min(NQI, 12 + (yi + 1) * (NQI - 12) // NYB)
        g = emit_yblock(yi)
        for _ in g:
            if qi < ql:
                emit_quant(qi)
                qi += 1
        while qi < ql:
            emit_quant(qi)
            qi += 1
    while qi < NQI:
        emit_quant(qi)
        qi += 1

    for eng in ENGS:
        def _b(e):
            e.drain().then_inc(bar.h, 1)
        em(eng, _b)
    bar.n += 5
    for eng in ENGS:
        wait(eng, bar, bar.n)
    wait("sync", s_x1[0], 16 * x1_round[0])
    wait("sync", s_x1[1], 16 * x1_round[1])
    for _qi in range(2):
        wait("sync", s_qs[_qi], 16 * qs["qs_round"][_qi])
    def _b2(e):
        e.drain().then_inc(bar.h, 1)
    em("sync", _b2); bar.n += 1
    for eng in ("tensor", "vector", "scalar", "gpsimd"):
        wait(eng, bar, bar.n)

    es.close()
    es2 = ExitStack()
    def sb2(name, p, f, dt=F32):
        return es2.enter_context(nc.sbuf_tensor(name, [p, f], dt))
    HRW = 256
    NHR = H // HRW
    hsil = sb2("hsil", 128, 64 * 512, BF16)
    ln2T = sb2("ln2T", 128, 16 * 512, BF16)
    x1c = sb2("x1c", 128, 2 * 4 * C, BF16)
    wgb = sb2("wgb", 128, 4 * 16 * HRW, BF16)
    wpb = sb2("wpb", 128, 4 * 1024, BF16)
    sgl = sb2("sgl", 128, 2 * 512)
    hsq2 = sb2("hsq2", 128, 2 * 512, BF16)
    xo = sb2("xo", 128, 2 * 1024)
    invv = sb2("invv", 1, 512)
    invc = sb2("invc", 128, 4)
    st2 = sb2("st2", 128, 24)
    ag2 = sb2("ag2", 128, 2)
    iv2 = sb2("iv2", 128, 4)
    ln2b = sb2("ln2b", 128, 4 * C, BF16)

    s_xc = [mk("s_xc0"), mk("s_xc1")]
    s_wg = [mk(f"s_wg{i}") for i in range(4)]
    s_wp = [mk(f"s_wp{i}") for i in range(4)]
    s_out = [mk(f"s_out{i}") for i in range(2)]

    fx = {"xc_cons": [None, None], "wg_cons": [None] * 4, "wp_cons": [None] * 4,
          "hs_cv": [None] * 64, "sg_free": [None, None], "hq_ct": [None, None],
          "hq_cv": [None, None], "xo_user": [None] * 8, "ss_ct": None}

    def xc_load(ch):
        sl = ch % 2
        if fx["xc_cons"][sl] is not None:
            wait("sync", cv, fx["xc_cons"][sl])
        def _f(e, sl=sl, ch=ch):
            e.dma_start(x1c.ap().rearrange("p (s b c) -> p s b c", s=2, b=4)[:, sl],
                        x1s.rearrange("(b p) c -> p b c", p=128)[:, ch * 4:(ch + 1) * 4]
                        ).then_inc(s_xc[sl].h, 16)
        em("sync", _f)
        s_xc[sl].n += 16

    def wg_load(hrg):
        for half, hr in ((0, hrg), (1, hrg + NHR)):
            sl = (hrg % 2) * 2 + half
            if fx["wg_cons"][sl] is not None:
                wait("sync", ct, fx["wg_cons"][sl])
            def _f(e, sl=sl, hr=hr):
                e.dma_start(wgb.ap().rearrange("p (s d h) -> p s d h", s=4, d=16)[:, sl],
                            sqg[hr]).then_inc(s_wg[sl].h, 16)
            em("sync", _f)
            s_wg[sl].n += 16

    def wp_load(jt, dchalf):
        sl = jt % 4
        if fx["wp_cons"][sl] is not None:
            wait("scalar", ct, fx["wp_cons"][sl])
        def _f(e, sl=sl, jt=jt, dchalf=dchalf):
            e.dma_start(wpb[:, sl * 1024:(sl + 1) * 1024],
                        sqp[jt * 128:(jt + 1) * 128, dchalf * 1024:(dchalf + 1) * 1024]
                        ).then_inc(s_wp[sl].h, 16)
        em("scalar", _f)
        s_wp[sl].n += 16

    ln_cs_of = [None] * 4
    tp_ct_of = [None] * 4

    def ln2_chain(ch, tb):
        xsl = ch % 2
        xv = x1c.ap().rearrange("p (s b c) -> p s b c", s=2, b=4)[:, xsl, tb]
        wait("vector", s_xc[xsl], 16 * (ch // 2 + 1))
        def _st(e, xv=xv):
            ii = None
            for i in range(4):
                ii = e.bn_stats(st2[:, i * 6:(i + 1) * 6], xv[:, i * 512:(i + 1) * 512])
            ii.then_inc(cv.h, 1)
        em("vector", _st); cv.n += 1
        wait("vector", cv, cv.n)
        def _ag(e):
            e.bn_aggr(ag2[:], st2[:]).then_inc(cv.h, 1)
        em("vector", _ag); cv.n += 1
        wait("vector", cv, cv.n)
        def _ve(e):
            e.tensor_scalar(iv2[:, 0:1], ag2[:, 1:2], LN_EPS, 0.25,
                            ALU.add, ALU.mult).then_inc(cv.h, 1)
        em("vector", _ve); cv.n += 1
        wait("scalar", cv, cv.n)
        def _sq(e):
            e.activation(iv2[:, 1:2], iv2[:, 0:1], AF.Sqrt).then_inc(cs.h, 1)
        em("scalar", _sq); cs.n += 1
        wait("vector", cs, cs.n)
        def _rc(e):
            e.reciprocal(iv2[:, 2:3], iv2[:, 1:2]).then_inc(cv.h, 1)
        em("vector", _rc); cv.n += 1
        wait("vector", cv, cv.n)
        def _nm(e):
            e.scalar_tensor_tensor(iv2[:, 3:4], ag2[:, 0:1], -1.0, iv2[:, 2:3],
                                   ALU.mult, ALU.mult).then_inc(cv.h, 1)
        em("vector", _nm); cv.n += 1
        wait("scalar", cv, cv.n)
        if tp_ct_of[tb] is not None:
            wait("scalar", ct, tp_ct_of[tb])
        def _ln(e, xv=xv, tb=tb):
            e.activation(ln2b[:, tb * C:(tb + 1) * C], xv, AF.Identity, bias=iv2[:, 3:4],
                         scale=iv2[:, 2:3]).then_inc(cs.h, 1)
        em("scalar", _ln); cs.n += 1
        ln_cs_of[tb] = cs.n

    def ln2_tp(tb):
        wait("tensor", cs, ln_cs_of[tb])
        if tb45[0] is not None:
            wait("tensor", tb45[0][0], tb45[0][1])
        def _tp(e, tb=tb):
            ii = None
            for dt in range(16):
                ii = e.transpose(ppb[:, 4 * 1024 + dt * 128: 4 * 1024 + (dt + 1) * 128],
                                 ln2b[:, tb * C + dt * 128: tb * C + (dt + 1) * 128], idbs[:])
            ii.then_inc(ct.h, 1)
        em("tensor", _tp); ct.n += 1
        tp_ct_of[tb] = ct.n
        wait("vector", ct, ct.n)
        def _ev(e, tb=tb):
            e.tensor_copy(ln2T.ap().rearrange("p (d t) -> p d t", d=16)[:, :, tb * 128:(tb + 1) * 128],
                          ppb[:, 4 * 1024: 4 * 1024 + 2048].rearrange("p (d t) -> p d t", d=16)
                          ).then_inc(cv.h, 1)
        em("vector", _ev); cv.n += 1
        tb45[0] = (cv, cv.n)

    xc_load(0)
    for tb in range(4):
        ln2_chain(0, tb)
    for ch in range(NCH):
        sl = ch % 2
        if ch + 1 < NCH:
            xc_load(ch + 1)
        for tb in range(4):
            ln2_tp(tb)
        ln2T_cv = cv.n

        pend_ss = [None]
        wait("tensor", cv, ln2T_cv)
        wg_load(0)
        for hrg in range(NHR):
            if hrg + 1 < NHR:
                wg_load(hrg + 1)
            if ch + 1 < NCH and hrg in (6, 12, 18, 24):
                ln2_chain(ch + 1, (hrg - 6) // 6)
            slg = (hrg % 2) * 2
            slu = slg + 1
            wait("tensor", s_wg[slg], s_wg[slg].n)
            wait("tensor", s_wg[slu], s_wg[slu].n)
            for hi in range(HRW // 128):
                jt = hrg * (HRW // 128) + hi
                gb = jt % 2
                if fx["sg_free"][gb] is not None:
                    wait("tensor", cs, fx["sg_free"][gb])
                if fx["hq_cv"][gb] is not None:
                    wait("tensor", cv, fx["hq_cv"][gb])
                def _mmg(e, slg=slg, hi=hi, gb=gb):
                    w = wgb.ap().rearrange("p (s d h) -> p s d h", s=4, d=16)[:, slg]
                    ii = None
                    for dt in range(16):
                        ii = e.matmul(bank(gb), w[:, dt, hi * 128:(hi + 1) * 128],
                                      ln2T[:, dt * 512:(dt + 1) * 512],
                                      start=(dt == 0), stop=(dt == 15))
                    ii.then_inc(ct.h, 1)
                em("tensor", _mmg); ct.n += 1
                def _mmu(e, slu=slu, hi=hi, gb=gb):
                    w = wgb.ap().rearrange("p (s d h) -> p s d h", s=4, d=16)[:, slu]
                    ii = None
                    for dt in range(16):
                        ii = e.matmul(bank(2 + gb), w[:, dt, hi * 128:(hi + 1) * 128],
                                      ln2T[:, dt * 512:(dt + 1) * 512],
                                      start=(dt == 0), stop=(dt == 15))
                    ii.then_inc(ct.h, 1)
                em("tensor", _mmu); ct.n += 1
                ct_pair = ct.n
                if hi == HRW // 128 - 1:
                    fx["wg_cons"][slg] = ct.n
                    fx["wg_cons"][slu] = ct.n
                wait("scalar", ct, ct_pair - 1)
                if fx["hq_cv"][gb] is not None:
                    wait("scalar", cv, fx["hq_cv"][gb])
                def _si(e, gb=gb):
                    e.activation(sgl[:, gb * 512:(gb + 1) * 512], bank(gb), AF.Silu
                                 ).then_inc(cs.h, 1)
                em("scalar", _si); cs.n += 1
                sg_cs = cs.n
                fx["sg_free"][gb] = cs.n
                wait("vector", cs, sg_cs)
                wait("vector", ct, ct_pair)
                def _hs(e, jt=jt, gb=gb):
                    e.tensor_mul(hsil[:, jt * 512:(jt + 1) * 512], sgl[:, gb * 512:(gb + 1) * 512],
                                 bank(2 + gb)).then_inc(cv.h, 1)
                em("vector", _hs); cv.n += 1
                hs_cv = cv.n
                fx["hs_cv"][jt] = hs_cv
                wait("vector", cv, hs_cv)
                if fx["hq_ct"][gb] is not None:
                    wait("vector", ct, fx["hq_ct"][gb])
                def _hq(e, jt=jt, gb=gb):
                    e.tensor_mul(hsq2[:, gb * 512:(gb + 1) * 512], hsil[:, jt * 512:(jt + 1) * 512],
                                 hsil[:, jt * 512:(jt + 1) * 512]).then_inc(cv.h, 1)
                em("vector", _hq); cv.n += 1
                fx["hq_cv"][gb] = cv.n
                def emit_ss(jt=jt, gb=gb, hqv=cv.n):
                    wait("tensor", cv, hqv)
                    def _ss(e, jt=jt, gb=gb):
                        e.matmul(pp[0:1, 6 * 512:7 * 512], ones1s[:], hsq2[:, gb * 512:(gb + 1) * 512],
                                 start=(jt == 0), stop=(jt == 63)).then_inc(ct.h, 1)
                    em("tensor", _ss); ct.n += 1
                    fx["hq_ct"][gb] = ct.n
                if pend_ss[0] is not None:
                    pend_ss[0]()
                pend_ss[0] = emit_ss
        if pend_ss[0] is not None:
            pend_ss[0]()
            pend_ss[0] = None
        gu_ct = ct.n

        wait("vector", ct, gu_ct)
        def _ms(e):
            e.tensor_scalar(invv[0:1, :], pp[0:1, 6 * 512:7 * 512], 0.25 / 8192.0,
                            2.9802322387695312e-08, ALU.mult, ALU.add).then_inc(cv.h, 1)
        em("vector", _ms); cv.n += 1
        wait("scalar", cv, cv.n)
        def _sq2(e):
            e.activation(invv[0:1, :], invv[0:1, :], AF.Sqrt).then_inc(cs.h, 1)
        em("scalar", _sq2); cs.n += 1
        wait("vector", cs, cs.n)
        def _rc2(e):
            e.reciprocal(invv[0:1, :], invv[0:1, :]).then_inc(cv.h, 1)
        em("vector", _rc2); cv.n += 1
        inv_cv = cv.n
        wait("tensor", cv, inv_cv)
        def _ti(e):
            ii = None
            for tb in range(4):
                ii = e.transpose(pp[:, 7 * 512 + tb: 7 * 512 + tb + 1],
                                 invv[0:1, tb * 128:(tb + 1) * 128], idfs[0:1, 0:1])
            ii.then_inc(ct.h, 1)
        em("tensor", _ti); ct.n += 1
        wait("vector", ct, ct.n)
        def _ic(e):
            e.tensor_copy(invc[:], pp[:, 7 * 512: 7 * 512 + 4]).then_inc(cv.h, 1)
        em("vector", _ic); cv.n += 1
        invc_cv = cv.n

        for dchalf in range(2):
            wait("tensor", cv, invc_cv)
            if dchalf == 1:
                for u in range(8):
                    if fx["xo_user"][u] is not None:
                        wait("tensor", cv, fx["xo_user"][u])
            wp_load(0, dchalf)
            wp_load(1, dchalf)
            for jt in range(64):
                if jt + 2 < 64:
                    wp_load(jt + 2, dchalf)
                wsl = jt % 4
                wait("tensor", s_wp[wsl], s_wp[wsl].n)
                wait("tensor", cv, fx["hs_cv"][jt])
                def _mm(e, jt=jt, wsl=wsl):
                    ii = None
                    for tb in range(4):
                        for dc in range(2):
                            ii = e.matmul(bank(tb * 2 + dc),
                                          hsil[:, jt * 512 + tb * 128: jt * 512 + (tb + 1) * 128],
                                          wpb[:, wsl * 1024 + dc * 512: wsl * 1024 + (dc + 1) * 512],
                                          start=(jt == 0), stop=(jt == 63))
                    ii.then_inc(ct.h, 1)
                em("tensor", _mm); ct.n += 1
                fx["wp_cons"][wsl] = ct.n
            mm2_ct = ct.n
            wait("vector", ct, mm2_ct)
            for tb in range(4):
                xsl = tb % 2
                if s_out[xsl].n > 0:
                    wait("vector", s_out[xsl], s_out[xsl].n)
                def _xo(e, tb=tb, sl=sl, dchalf=dchalf, xsl=xsl):
                    xv = x1c.ap().rearrange("p (s b c) -> p s b c", s=2, b=4)[:, sl, tb]
                    ii = None
                    for dc in range(2):
                        d0 = dchalf * 1024 + dc * 512
                        ii = e.scalar_tensor_tensor(
                            xo[:, xsl * 1024 + dc * 512: xsl * 1024 + (dc + 1) * 512],
                            bank(tb * 2 + dc), invc[:, tb:tb + 1], xv[:, d0:d0 + 512],
                            ALU.mult, ALU.add)
                    ii.then_inc(cv.h, 1)
                em("vector", _xo); cv.n += 1
                fx["xo_user"][dchalf * 4 + tb] = cv.n
                wait("gpsimd", cv, cv.n)
                def _ow(e, tb=tb, ch=ch, dchalf=dchalf, xsl=xsl):
                    e.dma_start(out[(ch * 4 + tb) * 128:(ch * 4 + tb + 1) * 128,
                                    dchalf * 1024:(dchalf + 1) * 1024],
                                xo[:, xsl * 1024:(xsl + 1) * 1024]).then_inc(s_out[xsl].h, 16)
                em("gpsimd", _ow)
                s_out[xsl].n += 16
        fx["xc_cons"][sl] = cv.n
        tb45[0] = (cv, cv.n)

    for q in range(2):
        wait("gpsimd", s_out[q], s_out[q].n)

    blk = es2.enter_context(nc.Block())
    for name, attr in (("sync", blk.sync), ("tensor", blk.tensor), ("vector", blk.vector),
                       ("scalar", blk.scalar), ("gpsimd", blk.gpsimd)):
        def _run(e, ops=streams[name]):
            for op in ops:
                op(e)
        attr(_run)
    es2.close()
    keep.close()
    sems.close()
    return nc


def kernel(L0_attn_Z_in, L0_attn_A_l, L0_attn_r_l, L0_attn_B_l, L0_attn_C_l,
           L0_ffn_w_gate_up, L0_ffn_w_proj):
    bf = ml_dtypes.bfloat16
    Z = np.asarray(L0_attn_Z_in, dtype=np.float32)
    A = np.ascontiguousarray(np.asarray(L0_attn_A_l, dtype=np.float32).T)
    B = np.ascontiguousarray(np.asarray(L0_attn_B_l, dtype=np.float32).T)
    Cm = np.ascontiguousarray(np.asarray(L0_attn_C_l, dtype=np.float32).T)
    r = np.asarray(L0_attn_r_l, dtype=np.float32).reshape(1)
    WguT = np.ascontiguousarray(np.asarray(L0_ffn_w_gate_up, dtype=np.float32).T)
    WpT = np.ascontiguousarray(np.asarray(L0_ffn_w_proj, dtype=np.float32).T)

    idf = np.eye(128, dtype=np.float32)
    idb = np.eye(128, dtype=np.float32).astype(bf)
    ind64 = np.zeros((128, 2), dtype=np.float32)
    ind64[0:64, 0] = 1.0
    ind64[64:128, 1] = 1.0
    ind2 = np.ascontiguousarray(ind64.T).astype(bf)
    ind64 = ind64.astype(bf)
    ones1 = np.ones((128, 1), dtype=np.float32).astype(bf)
    onerow = np.ones((1, 128), dtype=np.float32)

    nc = build_program(R=Z.shape[1])
    common = dict(bT=B, cT=Cm, aT=A, rl=r, wguT=WguT, wpT=WpT, idf=idf, idb=idb,
                  ind64=ind64, ind2=ind2, ones1=ones1, onerow=onerow)
    in_maps = [dict(common, z=np.ascontiguousarray(Z[b])) for b in range(Z.shape[0])]
    res = run_bass_kernel_spmd(nc, in_maps, core_ids=list(range(8)))
    return np.stack([np.asarray(res.results[b]["out"]) for b in range(Z.shape[0])], axis=0)


# version 42
# speedup vs baseline: 1.1251x; 1.0035x over previous
import numpy as np
import ml_dtypes
from contextlib import ExitStack

import concourse.bass as bass
import concourse.mybir as mybir
from concourse.bass_utils import run_bass_kernel_spmd

F32 = mybir.dt.float32
BF16 = mybir.dt.bfloat16
I16 = mybir.dt.int16
I32 = mybir.dt.int32
AF = mybir.ActivationFunctionType
ALU = mybir.AluOpType

C = 2048
D = 256
H = 8192
RMS_EPS = 1.1920928955078125e-07
LN_EPS = 1e-5
QW = 2048


class Ctr:
    def __init__(self, es, nc, name):
        self.h = es.enter_context(nc.semaphore(name))
        self.n = 0


def build_program(R=4096):
    NB = R // 128
    NYB = NB - 2
    NCH = R // 512
    nc = bass.Bass("TRN2", target_bir_lowering=False)

    z = nc.dram_tensor("z", [R, C], F32, kind="ExternalInput").ap()
    bT = nc.dram_tensor("bT", [D, D], F32, kind="ExternalInput").ap()
    cT = nc.dram_tensor("cT", [D, D], F32, kind="ExternalInput").ap()
    aT = nc.dram_tensor("aT", [D, D], F32, kind="ExternalInput").ap()
    rl = nc.dram_tensor("rl", [1], F32, kind="ExternalInput").ap()
    wguT = nc.dram_tensor("wguT", [C, 2 * H], F32, kind="ExternalInput").ap()
    wpT = nc.dram_tensor("wpT", [H, C], F32, kind="ExternalInput").ap()
    idf = nc.dram_tensor("idf", [128, 128], F32, kind="ExternalInput").ap()
    idb = nc.dram_tensor("idb", [128, 128], BF16, kind="ExternalInput").ap()
    ind64 = nc.dram_tensor("ind64", [128, 2], BF16, kind="ExternalInput").ap()
    ind2 = nc.dram_tensor("ind2", [2, 128], BF16, kind="ExternalInput").ap()
    ones1 = nc.dram_tensor("ones1", [128, 1], BF16, kind="ExternalInput").ap()
    onerow = nc.dram_tensor("onerow", [1, 128], F32, kind="ExternalInput").ap()
    out = nc.dram_tensor("out", [R, C], F32, kind="ExternalOutput").ap()
    sqg = nc.dram_tensor("sqg", [2 * H // 256, 128, 16, 256], BF16, kind="Internal").ap()
    sqp = nc.dram_tensor("sqp", [H, C], BF16, kind="Internal").ap()
    x1s = nc.dram_tensor("x1s", [R, C], BF16, kind="Internal").ap()
    thrh = nc.dram_tensor("thrh", [2, 4 * 1024], BF16, kind="Internal").ap()

    keep = ExitStack()
    def sbk(name, p, f, dt=F32):
        return keep.enter_context(nc.sbuf_tensor(name, [p, f], dt))

    pp = keep.enter_context(nc.psum_tensor("pp", [128, 4096], F32))
    ppb = pp.ap().bitcast(BF16)
    def bank(i):
        return pp[:, i * 512:(i + 1) * 512]

    idfs = sbk("idfs", 128, 128)
    idbs = sbk("idbs", 128, 128, BF16)
    ind64s = sbk("ind64s", 128, 2, BF16)
    ind2s = sbk("ind2s", 2, 128, BF16)
    ones1s = sbk("ones1s", 128, 1, BF16)
    onerow_s = sbk("onerow_s", 1, 128)
    rbc = sbk("rbc", 128, 1)
    zsrc = sbk("zsrc", 1, 8, BF16)
    sgnc = sbk("sgnc", 128, 1, I32)
    absc = sbk("absc", 128, 1, I32)
    qeps = sbk("qeps", 1, 1)
    thsc = sbk("thsc", 128, 1)
    nege = sbk("nege", 128, 1)
    pose = sbk("pose", 128, 1)

    es = ExitStack()
    def sb(name, p, f, dt=F32):
        return es.enter_context(nc.sbuf_tensor(name, [p, f], dt))

    att = sb("att", 128, 16 * C, BF16)
    xln = sb("xln", 128, 2 * C, BF16)
    xlnT = sb("xlnT", 128, 2 * 16 * 128, BF16)
    uu = sb("uu", 128, 2 * C, BF16)
    ww = sb("ww", 128, 2 * C, BF16)
    NZ = 2
    zring = sb("zring", 128, NZ * C, BF16)
    lnb = sb("lnb", 128, 2 * C, BF16)
    lnT = sb("lnT", 128, 2 * C, BF16)
    x1b = sb("x1b", 128, 2 * C, BF16)
    t1b = sb("t1b", 128, C, BF16)
    btw = sb("btw", 128, 3 * 2 * 2 * 128, BF16)
    stat = sb("stat", 128, 24)
    aggr = sb("aggr", 128, 2)
    ivar = sb("ivar", 128, 4)
    QSL = 4
    wqin = sb("wqin", 128, 6 * QW, BF16)
    awb = sb("awb", 128, 4 * QW, BF16)
    mkb = sb("mkb", 128, 2 * QW, BF16)
    m2b = sb("m2b", 128, 2 * QW, BF16)
    thrs = sb("thrs", 128, 2 * QW, BF16)
    thr2 = sb("thr2", 2, 2 * 1024, BF16)
    tmpth = sb("tmpth", 2, 1024, BF16)

    sems = ExitStack()
    def mk(name):
        return Ctr(sems, nc, name)
    s_z = [mk(f"s_z{i}") for i in range(NZ)]
    s_cst = mk("s_cst")
    s_btw = mk("s_btw")
    s_w = [mk(f"s_w{i}") for i in range(6)]
    s_qs = [mk(f"s_qs{i}") for i in range(2)]
    s_tsp = [mk(f"s_tsp{i}") for i in range(4)]
    s_tbc = [mk(f"s_tbc{i}") for i in range(4)]
    s_x1 = [mk(f"s_x1{i}") for i in range(2)]
    s_dg = mk("s_dg")
    cv = mk("cv"); cs = mk("cs"); ct = mk("ct"); cg = mk("cg")
    bar = mk("bar")

    ENGS = ("sync", "tensor", "vector", "scalar", "gpsimd")
    streams = {k: [] for k in ENGS}
    def em(eng, fn):
        streams[eng].append(fn)
    def wait(eng, ctr, val):
        if val is None or val <= 0:
            return
        streams[eng].append(lambda e, h=ctr.h, v=val: e.wait_ge(h, v))

    bank_ev = {}
    def bank_wait(eng, b):
        ev = bank_ev.get(b)
        if ev is not None:
            wait(eng, ev[0], ev[1])

    def _cst(e):
        e.dma_start(idfs[:], idf[:]).then_inc(s_cst.h, 16)
        e.dma_start(idbs[:], idb[:]).then_inc(s_cst.h, 16)
        e.dma_start(ind64s[:], ind64[:]).then_inc(s_cst.h, 16)
        e.dma_start(ind2s[:], ind2[:]).then_inc(s_cst.h, 16)
        e.dma_start(ones1s[:], ones1[:]).then_inc(s_cst.h, 16)
        e.dma_start(onerow_s[:], onerow[:]).then_inc(s_cst.h, 16)
        e.dma_start(rbc[0:1, 0:1], rl[0:1]).then_inc(s_cst.h, 16)
    em("sync", _cst); s_cst.n += 112

    def _btwload(e):
        for i, w in enumerate((bT, cT, aT)):
            e.dma_start(
                btw.ap().rearrange("p (w a b q) -> p w a b q", w=3, a=2, b=2)[:, i],
                w.rearrange("(a p) (b q) -> p a b q", p=128, q=128),
            ).then_inc(s_btw.h, 16)
    em("gpsimd", _btwload); s_btw.n += 48
    def _zs(e):
        e.memset(zsrc[:], 0.0)
        e.memset(sgnc[:], -2147450880)
        e.memset(absc[:], 2147450879)
        e.memset(thsc[:], 0.5 / 64.0)
        e.memset(nege[:], -0.5e-8)
        e.memset(pose[:], 0.5e-8)
        e.memset(qeps[:], 2.9802322387695312e-08).then_inc(cg.h, 1)
    em("gpsimd", _zs); cg.n += 1
    zsrc_cg = cg.n
    def btile(w, kt, mt):
        o = (w * 4 + kt * 2 + mt) * 128
        return btw[:, o:o + 128]

    wait("tensor", s_cst, s_cst.n)
    def _rbc(e):
        e.matmul(pp[:, 7 * 512:7 * 512 + 1], onerow_s[:], rbc[0:1, 0:1],
                 start=True, stop=True).then_inc(ct.h, 1)
    em("tensor", _rbc); ct.n += 1
    wait("vector", ct, ct.n)
    def _rbcev(e):
        e.tensor_copy(rbc[:], pp[:, 7 * 512:7 * 512 + 1]).then_inc(cv.h, 1)
    em("vector", _rbcev); cv.n += 1
    rbc_cv = cv.n
    b7_user = cv.n

    zround = [0] * NZ
    zfree = [None] * NZ

    def z_load(slot, blk_row):
        if zfree[slot] is not None:
            wait("gpsimd", zfree[slot][0], zfree[slot][1])
        def _f(e, sl=slot, r0=blk_row * 128):
            e.dma_start(zring[:, sl * C:(sl + 1) * C], z[r0:r0 + 128, :]).then_inc(s_z[sl].h, 16)
        em("gpsimd", _f)
        s_z[slot].n += 16
        zround[slot] += 1

    def ln_block(zslot, dst, dsttile):
        wait("vector", s_z[zslot], 16 * zround[zslot])
        zv = zring[:, zslot * C:(zslot + 1) * C]
        def _st(e, zv=zv):
            ii = None
            for i in range(4):
                ii = e.bn_stats(stat[:, i * 6:(i + 1) * 6], zv[:, i * 512:(i + 1) * 512])
            ii.then_inc(cv.h, 1)
        em("vector", _st); cv.n += 1
        wait("vector", cv, cv.n)
        def _ag(e):
            e.bn_aggr(aggr[:], stat[:]).then_inc(cv.h, 1)
        em("vector", _ag); cv.n += 1
        wait("vector", cv, cv.n)
        def _ve(e):
            e.tensor_scalar_add(ivar[:, 0:1], aggr[:, 1:2], LN_EPS).then_inc(cv.h, 1)
        em("vector", _ve); cv.n += 1
        wait("scalar", cv, cv.n)
        def _sq(e):
            e.activation(ivar[:, 1:2], ivar[:, 0:1], AF.Sqrt).then_inc(cs.h, 1)
        em("scalar", _sq); cs.n += 1
        wait("vector", cs, cs.n)
        def _rc(e):
            e.reciprocal(ivar[:, 2:3], ivar[:, 1:2]).then_inc(cv.h, 1)
        em("vector", _rc); cv.n += 1
        wait("vector", cv, cv.n)
        def _nm(e):
            e.scalar_tensor_tensor(ivar[:, 3:4], aggr[:, 0:1], -1.0, ivar[:, 2:3],
                                   ALU.mult, ALU.mult).then_inc(cv.h, 1)
        em("vector", _nm); cv.n += 1
        wait("scalar", cv, cv.n)
        def _ln(e, d=dst, i=dsttile, zv=zv):
            e.activation(d[:, i * C:(i + 1) * C], zv, AF.Identity,
                         bias=ivar[:, 3:4], scale=ivar[:, 2:3]).then_inc(cs.h, 1)
        em("scalar", _ln); cs.n += 1
        return cs.n

    tb45 = [None]

    def transpose_block(src, srcoff, dst, dstoff, ln_cs):
        wait("tensor", cs, ln_cs)
        bank_wait("tensor", 2)
        bank_wait("tensor", 3)
        def _tp(e, s=src, o=srcoff):
            ii = None
            for nt in range(16):
                ii = e.transpose(ppb[:, 2 * 1024 + nt * 128: 2 * 1024 + (nt + 1) * 128],
                                 s[:, o + nt * 128: o + (nt + 1) * 128], idbs[:])
            ii.then_inc(ct.h, 1)
        em("tensor", _tp); ct.n += 1
        wait("vector", ct, ct.n)
        def _ev(e, d=dst, o=dstoff):
            e.tensor_copy(d[:, o:o + 2048], ppb[:, 2 * 1024: 2 * 1024 + 2048]).then_inc(cv.h, 1)
        em("vector", _ev); cv.n += 1
        bank_ev[2] = (cv, cv.n)
        bank_ev[3] = (cv, cv.n)
        return cv.n

    qchunks = []
    for dt in range(16):
        for cc in range(2 * H // QW):
            dst = sqg[cc * 8:(cc + 1) * 8, :, dt, :].rearrange("h p w -> p h w")
            qchunks.append((wguT[dt * 128:(dt + 1) * 128, cc * QW:(cc + 1) * QW], dst))
    for jt in range(H // 128):
        qchunks.append((wpT[jt * 128:(jt + 1) * 128, :],
                        sqp[jt * 128:(jt + 1) * 128, :]))
    NQ = len(qchunks)

    qs = {"w_round": [0] * 4, "qs_round": [0] * 4,
          "aw_cv": [None] * 4, "mask_cv": [None] * 4, "aw_user": [None] * 4,
          "m2_cv": [None] * 4, "wq_cg": [None] * 4, "th0": True, "tmp_free": 0,
          "gs_ct": [None] * 4,
          "gs_user": {4: (cv, b7_user), 5: (cv, b7_user), 6: (cv, b7_user), 7: (cv, b7_user)},
          "sp_n": [0] * 4, "bc_n": [0] * 4,
          "gs_pair_ct": {}, "wqin_user": [None] * 6, "w_round4": [0] * 6}

    def q_load(j):
        if j < 0 or j >= NQ:
            return
        w4 = j % 6
        if qs["wqin_user"][w4] is not None:
            wait("gpsimd", cv, qs["wqin_user"][w4])
        if qs["w_round4"][w4] > 0:
            wait("gpsimd", s_w[w4], 16 * qs["w_round4"][w4])
        def _ld(e, w4=w4, src=qchunks[j][0]):
            e.dma_start(wqin[:, w4 * QW:(w4 + 1) * QW], src).then_inc(s_w[w4].h, 16)
        em("gpsimd", _ld)
        s_w[w4].n += 16
        qs["w_round4"][w4] += 1

    def q_aw(j):
        if j < 0 or j >= NQ:
            return
        sl = j % 4
        w4 = j % 6
        wait("vector", s_w[w4], 16 * qs["w_round4"][w4])
        if qs["gs_ct"][sl] is not None:
            wait("vector", ct, qs["gs_ct"][sl])
        if qs["aw_user"][sl] is not None:
            wait("vector", cv, qs["aw_user"][sl])
        def _aw(e, sl=sl, w4=w4):
            e.tensor_scalar(awb.ap().bitcast(I32)[:, sl * QW // 2:(sl + 1) * QW // 2],
                            wqin.ap().bitcast(I32)[:, w4 * QW // 2:(w4 + 1) * QW // 2],
                            absc[:, 0:1], None, ALU.bitwise_and).then_inc(cv.h, 1)
        em("vector", _aw); cv.n += 1
        qs["aw_cv"][sl] = cv.n

    def q_gs(j, pairi):
        if j < 0 or j >= NQ:
            return
        sl = j % 4
        b0 = 4 + 2 * ((j + pairi) % 2)
        wait("tensor", cv, qs["aw_cv"][j % 4])
        for b in (b0, b0 + 1):
            u = qs["gs_user"][b]
            if u is not None:
                wait("tensor", u[0], u[1])
        def _gs(e, sl=sl, pairi=pairi, b0=b0):
            for i in (2 * pairi, 2 * pairi + 1):
                bnk = b0 + i % 2
                e.matmul(pp[0:2, bnk * 512:(bnk + 1) * 512], ind64s[:],
                         awb[:, sl * QW + i * 512: sl * QW + (i + 1) * 512],
                         start=True, stop=True).then_inc(ct.h, 1)
        em("tensor", _gs); ct.n += 2
        qs["gs_ct"][j % 4] = ct.n
        qs["gs_pair_ct"][(j, pairi)] = ct.n

    def q_th(j, pairi):
        if j < 0 or j >= NQ:
            return
        b0 = 4 + 2 * ((j + pairi) % 2)
        t2 = (j % 2) * 2 + pairi
        t2p = ((j + 1) % 2) * 2 + pairi
        if qs["th0"]:
            wait("scalar", cg, zsrc_cg)
            qs["th0"] = False
        wait("scalar", ct, qs["gs_pair_ct"][(j, pairi)])
        wait("scalar", s_tsp[t2p], 16 * qs["sp_n"][t2p])
        wait("scalar", cs, qs["tmp_free"])
        def _t1(e, b0=b0):
            e.activation(tmpth[:], pp[0:2, b0 * 512:(b0 + 2) * 512], AF.Relu,
                         bias=nege[0:2, 0:1], scale=thsc[0:2, 0:1]).then_inc(cs.h, 1)
        em("scalar", _t1); cs.n += 1
        wait("scalar", cs, cs.n)
        def _t2(e, pairi=pairi):
            e.activation(thr2[0:2, pairi * 1024:(pairi + 1) * 1024], tmpth[:], AF.Identity,
                         bias=pose[0:2, 0:1]).then_inc(cs.h, 1)
        em("scalar", _t2); cs.n += 1
        qs["tmp_free"] = cs.n
        qs["gs_user"][b0 + 0] = (cs, cs.n)
        qs["gs_user"][b0 + 1] = (cs, cs.n)
        wait("scalar", cs, cs.n)
        if qs["bc_n"][t2] > 0:
            wait("scalar", s_tbc[t2], 32 * qs["bc_n"][t2])
        def _sp(e, t2=t2, pairi=pairi):
            e.dma_start(thrh[0:2, t2 * 1024:(t2 + 1) * 1024],
                        thr2[0:2, pairi * 1024:(pairi + 1) * 1024]).then_inc(s_tsp[t2].h, 16)
        em("scalar", _sp)
        s_tsp[t2].n += 16
        qs["sp_n"][t2] += 1

    def q_bc(j, pairi):
        if j < 0 or j >= NQ:
            return
        sl = j % 2
        t2 = (j % 2) * 2 + pairi
        wait("gpsimd", s_tsp[t2], 16 * qs["sp_n"][t2])
        if pairi == 0 and qs["wq_cg"][sl] is not None:
            wait("gpsimd", cg, qs["wq_cg"][sl])
        def _bc(e, sl=sl, t2=t2, pairi=pairi):
            for g in range(2):
                e.dma_start(thrs[g * 64:(g + 1) * 64, sl * QW + pairi * 1024: sl * QW + (pairi + 1) * 1024],
                            thrh[g:g + 1, t2 * 1024:(t2 + 1) * 1024].broadcast_to([64, 1024])
                            ).then_inc(s_tbc[t2].h, 16)
        em("gpsimd", _bc)
        s_tbc[t2].n += 32
        qs["bc_n"][t2] += 1

    def q_mask(j):
        if j < 0 or j >= NQ:
            return
        sl = j % 2
        mksl = j % 2
        for pi in (0, 1):
            wait("vector", s_tbc[sl * 2 + pi], 32 * qs["bc_n"][sl * 2 + pi])
        if qs["qs_round"][mksl] > 0:
            wait("vector", s_qs[mksl], 16 * qs["qs_round"][mksl])
        def _mk(e, sl=sl, awsl=j % 4, mksl=mksl):
            e.tensor_tensor(mkb[:, mksl * QW:(mksl + 1) * QW], awb[:, awsl * QW:(awsl + 1) * QW],
                            thrs[:, sl * QW:(sl + 1) * QW], ALU.is_gt).then_inc(cv.h, 1)
        em("vector", _mk); cv.n += 1
        qs["mask_cv"][sl] = cv.n
        qs["aw_user"][j % 4] = cv.n
        wait("vector", cv, cv.n)
        if qs["wq_cg"][sl] is not None:
            wait("vector", cg, qs["wq_cg"][sl])
        def _m2(e, sl=j % 2, w4=j % 6):
            e.scalar_tensor_tensor(m2b.ap().bitcast(I32)[:, sl * QW // 2:(sl + 1) * QW // 2],
                                   wqin.ap().bitcast(I32)[:, w4 * QW // 2:(w4 + 1) * QW // 2],
                                   sgnc[:, 0:1],
                                   mkb.ap().bitcast(I32)[:, sl * QW // 2:(sl + 1) * QW // 2],
                                   ALU.bitwise_and, ALU.bitwise_or).then_inc(cv.h, 1)
        em("vector", _m2); cv.n += 1
        qs["m2_cv"][j % 2] = cv.n
        qs["wqin_user"][j % 6] = cv.n

    def q_wq(j):
        if j < 0 or j >= NQ:
            return
        sl = j % 2
        wait("gpsimd", cv, qs["m2_cv"][j % 2])
        def _wq(e, sl=sl, m2sl=j % 2):
            e.tensor_mul(mkb[:, m2sl * QW:(m2sl + 1) * QW], m2b[:, m2sl * QW:(m2sl + 1) * QW],
                         thrs[:, sl * QW:(sl + 1) * QW]).then_inc(cg.h, 1)
        em("gpsimd", _wq); cg.n += 1
        qs["wq_cg"][sl] = cg.n
        is_gu = j < 128
        qeng = "scalar" if is_gu else "sync"
        wait(qeng, cg, qs["wq_cg"][sl])
        mksl = j % 2
        def _stq(e, mksl=mksl, dst=qchunks[j][1], is_gu=is_gu):
            if is_gu:
                e.dma_start(dst, mkb.ap().rearrange("p (s h w) -> p s h w", s=2, h=8)[:, mksl]
                            ).then_inc(s_qs[mksl].h, 16)
            else:
                e.dma_start(dst, mkb[:, mksl * QW:(mksl + 1) * QW]).then_inc(s_qs[mksl].h, 16)
        em(qeng, _stq)
        s_qs[mksl].n += 16
        qs["qs_round"][mksl] += 1

    def emit_quant(k):
        if k == 0:
            q_load(0)
            q_load(1)
        q_bc(k - 3, 0)
        q_bc(k - 3, 1)
        q_th(k - 2, 0)
        q_th(k - 2, 1)
        q_mask(k - 4)
        q_wq(k - 4)
        q_load(k + 2)
        q_aw(k)
        q_gs(k - 1, 0)
        q_gs(k - 1, 1)


    z_load(0, 0)
    z_load(1, 1)
    for _k in range(4):
        emit_quant(_k)
    ln_cs0 = ln_block(0, xln, 0)
    transpose_block(xln, 0 * C, xlnT, 0 * 2048, ln_cs0)
    ln_cs1 = ln_block(1, xln, 1)
    xT_cv = transpose_block(xln, 1 * C, xlnT, 1 * 2048, ln_cs1)
    xln_cs = cs.n

    wait("tensor", s_btw, s_btw.n)
    wait("tensor", cs, xln_cs)
    def mm_uw(widx, dst):
        for dt in range(2):
            for ncc in range(4):
                b = (dt * 4 + ncc) % 2
                bank_wait("tensor", b)
                def _mm(e, dt=dt, ncc=ncc, b=b, w=widx):
                    ii = None
                    for kt in range(2):
                        ii = e.matmul(bank(b), btile(w, kt, dt),
                                      xln[:, kt * C + ncc * 512: kt * C + (ncc + 1) * 512],
                                      start=(kt == 0), stop=(kt == 1))
                    ii.then_inc(ct.h, 1)
                em("tensor", _mm); ct.n += 1
                wait("scalar", ct, ct.n)
                def _ev(e, dt=dt, ncc=ncc, b=b, d=dst):
                    e.activation(d[:, dt * C + ncc * 512: dt * C + (ncc + 1) * 512],
                                 bank(b), AF.Identity).then_inc(cs.h, 1)
                em("scalar", _ev); cs.n += 1
                bank_ev[b] = (cs, cs.n)
    emit_quant(4)
    emit_quant(5)
    mm_uw(0, uu)
    mm_uw(1, ww)
    emit_quant(6)
    emit_quant(7)
    uw_cs = cs.n

    for nt in range(16):
        for mc in range(4):
            b = (nt * 4 + mc) % 4
            bank_wait("tensor", b)
            def _mm(e, nt=nt, mc=mc, b=b):
                ii = None
                for kt in range(2):
                    ii = e.matmul(bank(b), uu[:, kt * C + nt * 128: kt * C + (nt + 1) * 128],
                                  ww[:, kt * C + mc * 512: kt * C + (mc + 1) * 512],
                                  start=(kt == 0), stop=(kt == 1))
                ii.then_inc(ct.h, 1)
            em("tensor", _mm); ct.n += 1
            wait("scalar", ct, ct.n)
            def _ev(e, nt=nt, mc=mc, b=b):
                e.activation(att[:, nt * C + mc * 512: nt * C + (mc + 1) * 512],
                             bank(b), AF.Relu).then_inc(cs.h, 1)
            em("scalar", _ev); cs.n += 1
            bank_ev[b] = (cs, cs.n)
    att_cs = cs.n
    scores_ct = ct.n
    wait("gpsimd", cs, att_cs)
    wait("gpsimd", cg, zsrc_cg)
    def _dg(e):
        e.dma_start(att[127:128, 15 * C + 2047: 15 * C + 2048], zsrc[0:1, 0:1]).then_inc(s_dg.h, 16)
    em("gpsimd", _dg); s_dg.n += 16

    wait("tensor", cv, xT_cv)
    wait("tensor", s_dg, s_dg.n)
    xatt = uu
    for dt in range(2):
        for mc in range(4):
            b = (dt * 4 + mc) % 2
            bank_wait("tensor", b)
            def _mm(e, dt=dt, mc=mc, b=b):
                ii = None
                for nt in range(16):
                    ii = e.matmul(bank(b), xlnT[:, (dt * 16 + nt) * 128: (dt * 16 + nt + 1) * 128],
                                  att[:, nt * C + mc * 512: nt * C + (mc + 1) * 512],
                                  start=(nt == 0), stop=(nt == 15))
                ii.then_inc(ct.h, 1)
            em("tensor", _mm); ct.n += 1
            wait("scalar", ct, ct.n)
            wait("scalar", ct, scores_ct)
            def _ev(e, dt=dt, mc=mc, b=b):
                e.activation(xatt[:, dt * C + mc * 512: dt * C + (mc + 1) * 512],
                             bank(b), AF.Identity).then_inc(cs.h, 1)
            em("scalar", _ev); cs.n += 1
            bank_ev[b] = (cs, cs.n)
    xatt_cs = cs.n
    for _k in range(8, 12):
        emit_quant(_k)

    wait("gpsimd", cs, xln_cs)
    z_load(0, 0)
    z_load(1, 1)
    x1_round = [0, 0]
    wait("tensor", cs, xatt_cs)
    for it in range(2):
        for mc in range(4):
            b = mc
            bank_wait("tensor", b)
            def _mm(e, it=it, mc=mc, b=b):
                ii = None
                for kd in range(2):
                    ii = e.matmul(bank(b), btile(2, kd, it),
                                  xatt[:, kd * C + mc * 512: kd * C + (mc + 1) * 512],
                                  start=(kd == 0), stop=(kd == 1))
                ii.then_inc(ct.h, 1)
            em("tensor", _mm); ct.n += 1
            bank_ev[b] = None
        ct_x = ct.n
        wait("vector", s_z[it], 16 * zround[it])
        wait("vector", cv, cv.n)
        def _t1(e, it=it):
            e.tensor_add(t1b[:, 0:C], zring[:, it * C:(it + 1) * C],
                         xln[:, it * C:(it + 1) * C]).then_inc(cv.h, 1)
        em("vector", _t1); cv.n += 1
        wait("vector", cv, cv.n)
        wait("vector", ct, ct_x)
        def _x1(e, it=it):
            ii = None
            for mc in range(4):
                ii = e.scalar_tensor_tensor(
                    x1b[:, it * C + mc * 512: it * C + (mc + 1) * 512],
                    bank(mc), 1.0, t1b[:, mc * 512:(mc + 1) * 512],
                    ALU.bypass, ALU.add)
            ii.then_inc(cv.h, 1)
        em("vector", _x1); cv.n += 1
        for mc in range(4):
            bank_ev[mc] = (cv, cv.n)
        zfree[it] = (cv, cv.n)
        wait("sync", cv, cv.n)
        def _sp(e, it=it):
            e.dma_start(x1s[it * 128:(it + 1) * 128, :],
                        x1b[:, it * C:(it + 1) * C]).then_inc(s_x1[it].h, 16)
        em("sync", _sp)
        s_x1[it].n += 16
        x1_round[it] += 1

    lnslot = [None, None]
    lncsv = [None, None]
    lntcv = [None, None]

    def prep_block(yj):
        blk = 2 + yj
        zsl = blk % NZ
        sl = yj % 2
        z_load(zsl, blk)
        if lnslot[sl] is not None:
            wait("scalar", cv, lnslot[sl])
        yield
        lncsv[sl] = ln_block(zsl, lnb, sl)
        yield
        lntcv[sl] = transpose_block(lnb, sl * C, lnT, sl * C, lncsv[sl])

    def emit_yblock(yi):
        blk = 2 + yi
        zsl = blk % NZ
        sl = yi % 2
        ln_cs = lncsv[sl]
        wait("tensor", cv, lntcv[sl])
        wait("vector", s_z[zsl], 16 * zround[zsl])
        wait("vector", cs, ln_cs)
        if lnslot[sl] is not None:
            wait("vector", cv, lnslot[sl])
        def _t1(e, sl=sl, zsl=zsl):
            e.tensor_add(t1b[:, 0:C], zring[:, zsl * C:(zsl + 1) * C],
                         lnb[:, sl * C:(sl + 1) * C]).then_inc(cv.h, 1)
        em("vector", _t1); cv.n += 1
        t1cv = cv.n
        zfree[zsl] = (cv, cv.n)
        if x1_round[sl] > 0:
            wait("vector", s_x1[sl], 16 * x1_round[sl])
        for half in range(2):
            for q in range(2):
                bank_wait("tensor", q)
                def _mm(e, sl=sl, half=half, q=q):
                    ii = None
                    for nt in range(16):
                        mc = 2 * half + q
                        ii = e.matmul(bank(q), lnT[:, sl * C + nt * 128: sl * C + (nt + 1) * 128],
                                      att[:, nt * C + mc * 512: nt * C + (mc + 1) * 512],
                                      start=(nt == 0), stop=(nt == 15))
                    ii.then_inc(ct.h, 1)
                em("tensor", _mm); ct.n += 1
                ct_y = ct.n
                yield
                wait("vector", cv, t1cv)
                wait("vector", ct, ct_y)
                def _x1(e, sl=sl, half=half, q=q):
                    mc = 2 * half + q
                    e.scalar_tensor_tensor(
                        x1b[:, sl * C + mc * 512: sl * C + (mc + 1) * 512],
                        bank(q), rbc[:, 0:1], t1b[:, mc * 512:(mc + 1) * 512],
                        ALU.mult, ALU.add).then_inc(cv.h, 1)
                em("vector", _x1); cv.n += 1
                bank_ev[q] = (cv, cv.n)
        lnslot[sl] = cv.n
        wait("sync", cv, cv.n)
        def _sp(e, sl=sl, blk=blk):
            e.dma_start(x1s[blk * 128:(blk + 1) * 128, :],
                        x1b[:, sl * C:(sl + 1) * C]).then_inc(s_x1[sl].h, 16)
        em("sync", _sp)
        s_x1[sl].n += 16
        x1_round[sl] += 1

    NQI = NQ + 4
    qi = 12
    for _ in prep_block(0):
        pass
    for yi in range(NYB):
        ql = min(NQI, 12 + (yi + 1) * (NQI - 12) // NYB)
        gens = [emit_yblock(yi)]
        if yi + 1 < NYB:
            gens.append(prep_block(yi + 1))
        while gens:
            for g in list(gens):
                try:
                    next(g)
                except StopIteration:
                    gens.remove(g)
                if qi < ql:
                    emit_quant(qi)
                    qi += 1
        while qi < ql:
            emit_quant(qi)
            qi += 1
    while qi < NQI:
        emit_quant(qi)
        qi += 1

    for eng in ENGS:
        def _b(e):
            e.drain().then_inc(bar.h, 1)
        em(eng, _b)
    bar.n += 5
    for eng in ENGS:
        wait(eng, bar, bar.n)
    wait("sync", s_x1[0], 16 * x1_round[0])
    wait("sync", s_x1[1], 16 * x1_round[1])
    for _qi in range(2):
        wait("sync", s_qs[_qi], 16 * qs["qs_round"][_qi])
    def _b2(e):
        e.drain().then_inc(bar.h, 1)
    em("sync", _b2); bar.n += 1
    for eng in ("tensor", "vector", "scalar", "gpsimd"):
        wait(eng, bar, bar.n)

    es.close()
    es2 = ExitStack()
    def sb2(name, p, f, dt=F32):
        return es2.enter_context(nc.sbuf_tensor(name, [p, f], dt))
    HRW = 256
    NHR = H // HRW
    hsil = sb2("hsil", 128, 64 * 512, BF16)
    ln2T = sb2("ln2T", 128, 16 * 512, BF16)
    x1c = sb2("x1c", 128, 2 * 4 * C, BF16)
    wgb = sb2("wgb", 128, 4 * 16 * HRW, BF16)
    wpb = sb2("wpb", 128, 4 * 1024, BF16)
    sgl = sb2("sgl", 128, 2 * 512)
    hsq2 = sb2("hsq2", 128, 2 * 512, BF16)
    xo = sb2("xo", 128, 2 * 1024)
    invv = sb2("invv", 1, 512)
    invc = sb2("invc", 128, 4)
    st2 = sb2("st2", 128, 24)
    ag2 = sb2("ag2", 128, 2)
    iv2 = sb2("iv2", 128, 4)
    ln2b = sb2("ln2b", 128, 4 * C, BF16)

    s_xc = [mk("s_xc0"), mk("s_xc1")]
    s_wg = [mk(f"s_wg{i}") for i in range(4)]
    s_wp = [mk(f"s_wp{i}") for i in range(4)]
    s_out = [mk(f"s_out{i}") for i in range(2)]

    fx = {"xc_cons": [None, None], "wg_cons": [None] * 4, "wp_cons": [None] * 4,
          "hs_cv": [None] * 64, "sg_free": [None, None], "hq_ct": [None, None],
          "hq_cv": [None, None], "xo_user": [None] * 8, "ss_ct": None}

    def xc_load(ch):
        sl = ch % 2
        if fx["xc_cons"][sl] is not None:
            wait("sync", cv, fx["xc_cons"][sl])
        def _f(e, sl=sl, ch=ch):
            e.dma_start(x1c.ap().rearrange("p (s b c) -> p s b c", s=2, b=4)[:, sl],
                        x1s.rearrange("(b p) c -> p b c", p=128)[:, ch * 4:(ch + 1) * 4]
                        ).then_inc(s_xc[sl].h, 16)
        em("sync", _f)
        s_xc[sl].n += 16

    def wg_load(hrg):
        for half, hr in ((0, hrg), (1, hrg + NHR)):
            sl = (hrg % 2) * 2 + half
            if fx["wg_cons"][sl] is not None:
                wait("sync", ct, fx["wg_cons"][sl])
            def _f(e, sl=sl, hr=hr):
                e.dma_start(wgb.ap().rearrange("p (s d h) -> p s d h", s=4, d=16)[:, sl],
                            sqg[hr]).then_inc(s_wg[sl].h, 16)
            em("sync", _f)
            s_wg[sl].n += 16

    def wp_load(jt, dchalf):
        sl = jt % 4
        if fx["wp_cons"][sl] is not None:
            wait("scalar", ct, fx["wp_cons"][sl])
        def _f(e, sl=sl, jt=jt, dchalf=dchalf):
            e.dma_start(wpb[:, sl * 1024:(sl + 1) * 1024],
                        sqp[jt * 128:(jt + 1) * 128, dchalf * 1024:(dchalf + 1) * 1024]
                        ).then_inc(s_wp[sl].h, 16)
        em("scalar", _f)
        s_wp[sl].n += 16

    ln_cs_of = [None] * 4
    tp_ct_of = [None] * 4

    def ln2_chain(ch, tb):
        xsl = ch % 2
        xv = x1c.ap().rearrange("p (s b c) -> p s b c", s=2, b=4)[:, xsl, tb]
        wait("vector", s_xc[xsl], 16 * (ch // 2 + 1))
        def _st(e, xv=xv):
            ii = None
            for i in range(4):
                ii = e.bn_stats(st2[:, i * 6:(i + 1) * 6], xv[:, i * 512:(i + 1) * 512])
            ii.then_inc(cv.h, 1)
        em("vector", _st); cv.n += 1
        wait("vector", cv, cv.n)
        def _ag(e):
            e.bn_aggr(ag2[:], st2[:]).then_inc(cv.h, 1)
        em("vector", _ag); cv.n += 1
        wait("vector", cv, cv.n)
        def _ve(e):
            e.tensor_scalar(iv2[:, 0:1], ag2[:, 1:2], LN_EPS, 0.25,
                            ALU.add, ALU.mult).then_inc(cv.h, 1)
        em("vector", _ve); cv.n += 1
        wait("scalar", cv, cv.n)
        def _sq(e):
            e.activation(iv2[:, 1:2], iv2[:, 0:1], AF.Sqrt).then_inc(cs.h, 1)
        em("scalar", _sq); cs.n += 1
        wait("vector", cs, cs.n)
        def _rc(e):
            e.reciprocal(iv2[:, 2:3], iv2[:, 1:2]).then_inc(cv.h, 1)
        em("vector", _rc); cv.n += 1
        wait("vector", cv, cv.n)
        def _nm(e):
            e.scalar_tensor_tensor(iv2[:, 3:4], ag2[:, 0:1], -1.0, iv2[:, 2:3],
                                   ALU.mult, ALU.mult).then_inc(cv.h, 1)
        em("vector", _nm); cv.n += 1
        wait("scalar", cv, cv.n)
        if tp_ct_of[tb] is not None:
            wait("scalar", ct, tp_ct_of[tb])
        def _ln(e, xv=xv, tb=tb):
            e.activation(ln2b[:, tb * C:(tb + 1) * C], xv, AF.Identity, bias=iv2[:, 3:4],
                         scale=iv2[:, 2:3]).then_inc(cs.h, 1)
        em("scalar", _ln); cs.n += 1
        ln_cs_of[tb] = cs.n

    def ln2_tp(tb):
        wait("tensor", cs, ln_cs_of[tb])
        if tb45[0] is not None:
            wait("tensor", tb45[0][0], tb45[0][1])
        def _tp(e, tb=tb):
            ii = None
            for dt in range(16):
                ii = e.transpose(ppb[:, 4 * 1024 + dt * 128: 4 * 1024 + (dt + 1) * 128],
                                 ln2b[:, tb * C + dt * 128: tb * C + (dt + 1) * 128], idbs[:])
            ii.then_inc(ct.h, 1)
        em("tensor", _tp); ct.n += 1
        tp_ct_of[tb] = ct.n
        wait("vector", ct, ct.n)
        def _ev(e, tb=tb):
            e.tensor_copy(ln2T.ap().rearrange("p (d t) -> p d t", d=16)[:, :, tb * 128:(tb + 1) * 128],
                          ppb[:, 4 * 1024: 4 * 1024 + 2048].rearrange("p (d t) -> p d t", d=16)
                          ).then_inc(cv.h, 1)
        em("vector", _ev); cv.n += 1
        tb45[0] = (cv, cv.n)

    xc_load(0)
    for tb in range(4):
        ln2_chain(0, tb)
    for ch in range(NCH):
        sl = ch % 2
        if ch + 1 < NCH:
            xc_load(ch + 1)
        for tb in range(4):
            ln2_tp(tb)
        ln2T_cv = cv.n

        pend_ss = [None]
        wait("tensor", cv, ln2T_cv)
        wg_load(0)
        for hrg in range(NHR):
            if hrg + 1 < NHR:
                wg_load(hrg + 1)
            if ch + 1 < NCH and hrg in (6, 12, 18, 24):
                ln2_chain(ch + 1, (hrg - 6) // 6)
            slg = (hrg % 2) * 2
            slu = slg + 1
            wait("tensor", s_wg[slg], s_wg[slg].n)
            wait("tensor", s_wg[slu], s_wg[slu].n)
            for hi in range(HRW // 128):
                jt = hrg * (HRW // 128) + hi
                gb = jt % 2
                if fx["sg_free"][gb] is not None:
                    wait("tensor", cs, fx["sg_free"][gb])
                if fx["hq_cv"][gb] is not None:
                    wait("tensor", cv, fx["hq_cv"][gb])
                def _mmg(e, slg=slg, hi=hi, gb=gb):
                    w = wgb.ap().rearrange("p (s d h) -> p s d h", s=4, d=16)[:, slg]
                    ii = None
                    for dt in range(16):
                        ii = e.matmul(bank(gb), w[:, dt, hi * 128:(hi + 1) * 128],
                                      ln2T[:, dt * 512:(dt + 1) * 512],
                                      start=(dt == 0), stop=(dt == 15))
                    ii.then_inc(ct.h, 1)
                em("tensor", _mmg); ct.n += 1
                def _mmu(e, slu=slu, hi=hi, gb=gb):
                    w = wgb.ap().rearrange("p (s d h) -> p s d h", s=4, d=16)[:, slu]
                    ii = None
                    for dt in range(16):
                        ii = e.matmul(bank(2 + gb), w[:, dt, hi * 128:(hi + 1) * 128],
                                      ln2T[:, dt * 512:(dt + 1) * 512],
                                      start=(dt == 0), stop=(dt == 15))
                    ii.then_inc(ct.h, 1)
                em("tensor", _mmu); ct.n += 1
                ct_pair = ct.n
                if hi == HRW // 128 - 1:
                    fx["wg_cons"][slg] = ct.n
                    fx["wg_cons"][slu] = ct.n
                wait("scalar", ct, ct_pair - 1)
                if fx["hq_cv"][gb] is not None:
                    wait("scalar", cv, fx["hq_cv"][gb])
                def _si(e, gb=gb):
                    e.activation(sgl[:, gb * 512:(gb + 1) * 512], bank(gb), AF.Silu
                                 ).then_inc(cs.h, 1)
                em("scalar", _si); cs.n += 1
                sg_cs = cs.n
                fx["sg_free"][gb] = cs.n
                wait("vector", cs, sg_cs)
                wait("vector", ct, ct_pair)
                def _hs(e, jt=jt, gb=gb):
                    e.tensor_mul(hsil[:, jt * 512:(jt + 1) * 512], sgl[:, gb * 512:(gb + 1) * 512],
                                 bank(2 + gb)).then_inc(cv.h, 1)
                em("vector", _hs); cv.n += 1
                hs_cv = cv.n
                fx["hs_cv"][jt] = hs_cv
                wait("vector", cv, hs_cv)
                if fx["hq_ct"][gb] is not None:
                    wait("vector", ct, fx["hq_ct"][gb])
                def _hq(e, jt=jt, gb=gb):
                    e.tensor_mul(hsq2[:, gb * 512:(gb + 1) * 512], hsil[:, jt * 512:(jt + 1) * 512],
                                 hsil[:, jt * 512:(jt + 1) * 512]).then_inc(cv.h, 1)
                em("vector", _hq); cv.n += 1
                fx["hq_cv"][gb] = cv.n
                def emit_ss(jt=jt, gb=gb, hqv=cv.n):
                    wait("tensor", cv, hqv)
                    def _ss(e, jt=jt, gb=gb):
                        e.matmul(pp[0:1, 6 * 512:7 * 512], ones1s[:], hsq2[:, gb * 512:(gb + 1) * 512],
                                 start=(jt == 0), stop=(jt == 63)).then_inc(ct.h, 1)
                    em("tensor", _ss); ct.n += 1
                    fx["hq_ct"][gb] = ct.n
                if pend_ss[0] is not None:
                    pend_ss[0]()
                pend_ss[0] = emit_ss
        if pend_ss[0] is not None:
            pend_ss[0]()
            pend_ss[0] = None
        gu_ct = ct.n

        wait("vector", ct, gu_ct)
        def _ms(e):
            e.tensor_scalar(invv[0:1, :], pp[0:1, 6 * 512:7 * 512], 0.25 / 8192.0,
                            2.9802322387695312e-08, ALU.mult, ALU.add).then_inc(cv.h, 1)
        em("vector", _ms); cv.n += 1
        wait("scalar", cv, cv.n)
        def _sq2(e):
            e.activation(invv[0:1, :], invv[0:1, :], AF.Sqrt).then_inc(cs.h, 1)
        em("scalar", _sq2); cs.n += 1
        wait("vector", cs, cs.n)
        def _rc2(e):
            e.reciprocal(invv[0:1, :], invv[0:1, :]).then_inc(cv.h, 1)
        em("vector", _rc2); cv.n += 1
        inv_cv = cv.n
        wait("tensor", cv, inv_cv)
        def _ti(e):
            ii = None
            for tb in range(4):
                ii = e.transpose(pp[:, 7 * 512 + tb: 7 * 512 + tb + 1],
                                 invv[0:1, tb * 128:(tb + 1) * 128], idfs[0:1, 0:1])
            ii.then_inc(ct.h, 1)
        em("tensor", _ti); ct.n += 1
        wait("vector", ct, ct.n)
        def _ic(e):
            e.tensor_copy(invc[:], pp[:, 7 * 512: 7 * 512 + 4]).then_inc(cv.h, 1)
        em("vector", _ic); cv.n += 1
        invc_cv = cv.n

        for dchalf in range(2):
            wait("tensor", cv, invc_cv)
            if dchalf == 1:
                for u in range(8):
                    if fx["xo_user"][u] is not None:
                        wait("tensor", cv, fx["xo_user"][u])
            wp_load(0, dchalf)
            wp_load(1, dchalf)
            for jt in range(64):
                if jt + 2 < 64:
                    wp_load(jt + 2, dchalf)
                wsl = jt % 4
                wait("tensor", s_wp[wsl], s_wp[wsl].n)
                wait("tensor", cv, fx["hs_cv"][jt])
                def _mm(e, jt=jt, wsl=wsl):
                    ii = None
                    for tb in range(4):
                        for dc in range(2):
                            ii = e.matmul(bank(tb * 2 + dc),
                                          hsil[:, jt * 512 + tb * 128: jt * 512 + (tb + 1) * 128],
                                          wpb[:, wsl * 1024 + dc * 512: wsl * 1024 + (dc + 1) * 512],
                                          start=(jt == 0), stop=(jt == 63))
                    ii.then_inc(ct.h, 1)
                em("tensor", _mm); ct.n += 1
                fx["wp_cons"][wsl] = ct.n
            mm2_ct = ct.n
            wait("vector", ct, mm2_ct)
            for tb in range(4):
                xsl = tb % 2
                if s_out[xsl].n > 0:
                    wait("vector", s_out[xsl], s_out[xsl].n)
                def _xo(e, tb=tb, sl=sl, dchalf=dchalf, xsl=xsl):
                    xv = x1c.ap().rearrange("p (s b c) -> p s b c", s=2, b=4)[:, sl, tb]
                    ii = None
                    for dc in range(2):
                        d0 = dchalf * 1024 + dc * 512
                        ii = e.scalar_tensor_tensor(
                            xo[:, xsl * 1024 + dc * 512: xsl * 1024 + (dc + 1) * 512],
                            bank(tb * 2 + dc), invc[:, tb:tb + 1], xv[:, d0:d0 + 512],
                            ALU.mult, ALU.add)
                    ii.then_inc(cv.h, 1)
                em("vector", _xo); cv.n += 1
                fx["xo_user"][dchalf * 4 + tb] = cv.n
                wait("gpsimd", cv, cv.n)
                def _ow(e, tb=tb, ch=ch, dchalf=dchalf, xsl=xsl):
                    e.dma_start(out[(ch * 4 + tb) * 128:(ch * 4 + tb + 1) * 128,
                                    dchalf * 1024:(dchalf + 1) * 1024],
                                xo[:, xsl * 1024:(xsl + 1) * 1024]).then_inc(s_out[xsl].h, 16)
                em("gpsimd", _ow)
                s_out[xsl].n += 16
        fx["xc_cons"][sl] = cv.n
        tb45[0] = (cv, cv.n)

    for q in range(2):
        wait("gpsimd", s_out[q], s_out[q].n)

    blk = es2.enter_context(nc.Block())
    for name, attr in (("sync", blk.sync), ("tensor", blk.tensor), ("vector", blk.vector),
                       ("scalar", blk.scalar), ("gpsimd", blk.gpsimd)):
        def _run(e, ops=streams[name]):
            for op in ops:
                op(e)
        attr(_run)
    es2.close()
    keep.close()
    sems.close()
    return nc


def kernel(L0_attn_Z_in, L0_attn_A_l, L0_attn_r_l, L0_attn_B_l, L0_attn_C_l,
           L0_ffn_w_gate_up, L0_ffn_w_proj):
    bf = ml_dtypes.bfloat16
    Z = np.asarray(L0_attn_Z_in, dtype=np.float32)
    A = np.ascontiguousarray(np.asarray(L0_attn_A_l, dtype=np.float32).T)
    B = np.ascontiguousarray(np.asarray(L0_attn_B_l, dtype=np.float32).T)
    Cm = np.ascontiguousarray(np.asarray(L0_attn_C_l, dtype=np.float32).T)
    r = np.asarray(L0_attn_r_l, dtype=np.float32).reshape(1)
    WguT = np.ascontiguousarray(np.asarray(L0_ffn_w_gate_up, dtype=np.float32).T)
    WpT = np.ascontiguousarray(np.asarray(L0_ffn_w_proj, dtype=np.float32).T)

    idf = np.eye(128, dtype=np.float32)
    idb = np.eye(128, dtype=np.float32).astype(bf)
    ind64 = np.zeros((128, 2), dtype=np.float32)
    ind64[0:64, 0] = 1.0
    ind64[64:128, 1] = 1.0
    ind2 = np.ascontiguousarray(ind64.T).astype(bf)
    ind64 = ind64.astype(bf)
    ones1 = np.ones((128, 1), dtype=np.float32).astype(bf)
    onerow = np.ones((1, 128), dtype=np.float32)

    nc = build_program(R=Z.shape[1])
    common = dict(bT=B, cT=Cm, aT=A, rl=r, wguT=WguT, wpT=WpT, idf=idf, idb=idb,
                  ind64=ind64, ind2=ind2, ones1=ones1, onerow=onerow)
    in_maps = [dict(common, z=np.ascontiguousarray(Z[b])) for b in range(Z.shape[0])]
    res = run_bass_kernel_spmd(nc, in_maps, core_ids=list(range(8)))
    return np.stack([np.asarray(res.results[b]["out"]) for b in range(Z.shape[0])], axis=0)
